# version 02
# baseline (speedup 1.0000x reference)
import math
from contextlib import ExitStack

import numpy as np
import ml_dtypes

import concourse.bass as bass
import concourse.mybir as mybir
from concourse.bass_utils import run_bass_kernel_spmd

F32 = mybir.dt.float32
BF16 = mybir.dt.bfloat16
AF = mybir.ActivationFunctionType
ALU = mybir.AluOpType
AX = mybir.AxisListType
BF = ml_dtypes.bfloat16

B, S, H = 2, 4096, 4096
NH, HD = 32, 128
SCALE = float(HD) ** -0.5
LN_EPS = 1e-5
RMS_EPS = 1.1920929e-07
TAU_INV = float(1.0 / np.float32(0.1))
NEG_BIG = -1.0e30
SINK_ITERS = 50
NT_KV = S // 128
NT_Q = 8
NCH = H // 128


def _alibi_slopes(n):
    cp2 = 2 ** math.floor(math.log2(n))
    base = 2.0 ** (-(2.0 ** (-(math.log2(cp2) - 3))))
    slopes = base ** np.arange(1, cp2 + 1, dtype=np.float64)
    if cp2 != n:
        extra_base = 2.0 ** (-(2.0 ** (-(math.log2(2 * cp2) - 3))))
        rem = min(cp2, n - cp2)
        extra = extra_base ** np.arange(1, 1 + 2 * rem, 2, dtype=np.float64)
        slopes = np.concatenate([slopes, extra])
    return slopes.astype(np.float32)

SLOPES = _alibi_slopes(NH)


class Kb:
    ENGINES = ("pe", "act", "dve", "pool", "sync")
    ESEM = {"pe": "s_pe", "act": "s_act", "dve": "s_dve", "pool": "s_pool"}
    QSEM = {"sync": "q_sync", "pool": "q_pool", "act": "q_act", "dve": "q_dve"}

    def __init__(self):
        self.ops = {e: [] for e in self.ENGINES}
        self.cnt = {}
        for s in list(self.ESEM.values()) + list(self.QSEM.values()):
            self.cnt[s] = 0
        self.res = {}
        self.waited = {e: {} for e in self.ENGINES}

    def _need(self, eng, waits, token):
        sem, val = token
        if eng == "pe" and sem == self.ESEM.get("pe"):
            return
        if self.waited[eng].get(sem, 0) >= val:
            return
        self.waited[eng][sem] = val
        waits[sem] = max(waits.get(sem, 0), val)

    def _emit(self, eng, fn, reads, writes, inc_sem, inc_n):
        waits = {}
        for r in reads:
            ent = self.res.get(r)
            if ent and ent[0]:
                self._need(eng, waits, ent[0])
        for w in writes:
            ent = self.res.get(w)
            if ent:
                if ent[0]:
                    self._need(eng, waits, ent[0])
                for sem, val in ent[1].items():
                    self._need(eng, waits, (sem, val))
        self.cnt[inc_sem] += inc_n
        token = (inc_sem, self.cnt[inc_sem])
        for r in reads:
            ent = self.res.setdefault(r, [None, {}])
            ent[1][token[0]] = token[1]
        for w in writes:
            self.res[w] = [token, {}]
        self.ops[eng].append((sorted(waits.items()), fn, (inc_sem, inc_n)))
        return token

    def op(self, eng, fn, reads=(), writes=()):
        return self._emit(eng, fn, reads, writes, self.ESEM[eng], 1)

    def dma(self, eng, fn, reads=(), writes=()):
        return self._emit(eng, fn, reads, writes, self.QSEM[eng], 16)

    def barrier(self):
        snap = dict(self.cnt)
        for eng in self.ENGINES:
            waits = {}
            for sem, val in snap.items():
                if val == 0:
                    continue
                if eng in self.ESEM and self.ESEM[eng] == sem:
                    continue
                if self.waited[eng].get(sem, 0) >= val:
                    continue
                self.waited[eng][sem] = val
                waits[sem] = val
            if waits:
                self.ops[eng].append((sorted(waits.items()), None, None))


def build_program(upto="FULL"):
    nc = bass.Bass("TRN2", target_bir_lowering=False)
    K = Kb()
    sems = {}
    _emit_state = {}

    def _emit():
        def mk(eng_name, nce):
            def body(_):
                for waits, fn, inc in K.ops[eng_name]:
                    for s, v in waits:
                        nce.wait_ge(sems[s], v)
                    if fn is not None:
                        ins = fn()
                        if inc is not None:
                            ins.then_inc(sems[inc[0]], inc[1])
            return body
        with nc.Block() as block:
            block.tensor(mk("pe", _emit_state["NCT"]))
            block.scalar(mk("act", _emit_state["NCS"]))
            block.vector(mk("dve", _emit_state["NCV"]))
            block.gpsimd(mk("pool", _emit_state["NCP"]))
            block.sync(mk("sync", _emit_state["NCY"]))
        return nc

    def _dump(pairs):
        NCP = _emit_state["NCP"]
        K.barrier()
        for nm, handle in pairs:
            shp = list(handle[:].shape)
            dt_ = handle[:].dtype
            dout = nc.dram_tensor(nm, shp, dt_, kind="ExternalOutput").ap()
            K.dma("pool", (lambda d2=dout, h2=handle: NCP.dma_start(out=d2, in_=h2[:])),
                  reads=(), writes=(nm + "_out",))
        K.ops["pool"].append(([("q_pool", K.cnt["q_pool"])], None, None))
        return _emit()

    d = {}
    d["xkv"] = nc.dram_tensor("xkv", [S, H], F32, kind="ExternalInput").ap()
    d["xq"] = nc.dram_tensor("xq", [1024, H], F32, kind="ExternalInput").ap()
    d["xqt"] = nc.dram_tensor("xqt", [H, 1024], F32, kind="ExternalInput").ap()
    d["wqkv"] = nc.dram_tensor("wqkv", [34, 128, 32 * 128], F32, kind="ExternalInput").ap()
    d["wo"] = nc.dram_tensor("wo", [32, 128, 32 * 128], F32, kind="ExternalInput").ap()
    d["wbr"] = nc.dram_tensor("wbr", [128, 8192], F32, kind="ExternalInput").ap()
    d["w24"] = nc.dram_tensor("w24", [128, 32 * 24], F32, kind="ExternalInput").ap()
    d["bases24"] = nc.dram_tensor("bases24", [1, 24], F32, kind="ExternalInput").ap()
    d["gate"] = nc.dram_tensor("gate", [1, 1], F32, kind="ExternalInput").ap()
    d["dplain"] = nc.dram_tensor("dplain", [128, 512], F32, kind="ExternalInput").ap()
    d["dmask"] = nc.dram_tensor("dmask", [128, 512], F32, kind="ExternalInput").ap()
    d["qcol"] = nc.dram_tensor("qcol", [128, 1], F32, kind="ExternalInput").ap()
    d["i128h"] = nc.dram_tensor("i128h", [128, 128], BF16, kind="ExternalInput").ap()
    d["i128f"] = nc.dram_tensor("i128f", [128, 128], F32, kind="ExternalInput").ap()
    d["ones1h"] = nc.dram_tensor("ones1h", [1, 128], BF16, kind="ExternalInput").ap()
    d["ones1f"] = nc.dram_tensor("ones1f", [1, 128], F32, kind="ExternalInput").ap()
    d["onescol_h"] = nc.dram_tensor("onescol_h", [128, 1], BF16, kind="ExternalInput").ap()
    d["yout"] = nc.dram_tensor("yout", [32, 128, 1024], F32, kind="ExternalOutput").ap()
    hscr = nc.dram_tensor("hscr", [24, 1024], BF16, kind="Internal").ap()

    es = ExitStack()
    sb = lambda name, shape, dt: es.enter_context(nc.sbuf_tensor(name, shape, dt))

    xnt = sb("xnt", [128, NCH * 1024], BF16)
    esKV = ExitStack()
    kt = esKV.enter_context(nc.sbuf_tensor("kt", [128, S], BF16, side="right"))
    vt = esKV.enter_context(nc.sbuf_tensor("vt", [128, S], BF16, side="right"))
    dplain = sb("dplain_s", [128, 512], F32)
    dmask = sb("dmask_s", [128, 512], F32)
    qcol = sb("qcol_s", [128, 1], F32)
    i128h = sb("i128h_s", [128, 128], BF16)
    i128f = sb("i128f_s", [128, 128], F32)
    ones1h = sb("ones1h_s", [1, 128], BF16)
    ones1f = sb("ones1f_s", [1, 128], F32)
    onescol_h = sb("onescol_h_s", [128, 1], BF16)
    rstdF = sb("rstdF", [128, 8], F32)
    lgraw = sb("lgraw", [24, 1024], F32)
    negc = sb("negc", [128, 1], F32)
    bases_row = sb("bases_row", [1, 24], F32)
    gate_s = sb("gate_s", [1, 1], F32)

    for s in list(Kb.ESEM.values()) + list(Kb.QSEM.values()):
        sems[s] = es.enter_context(nc.semaphore(s))

    NCV = nc.vector
    NCS = nc.scalar
    NCT = nc.tensor
    NCP = nc.gpsimd
    NCY = nc.sync
    _emit_state.update(NCV=NCV, NCS=NCS, NCT=NCT, NCP=NCP, NCY=NCY)

    for name, dst in [("dplain", dplain), ("dmask", dmask), ("qcol", qcol),
                      ("i128h", i128h), ("i128f", i128f), ("ones1h", ones1h),
                      ("ones1f", ones1f), ("onescol_h", onescol_h),
                      ("bases24", bases_row), ("gate", gate_s)]:
        src = d[name]
        K.dma("pool", (lambda dd=dst, ss=src: NCP.dma_start(out=dd[:], in_=ss)),
              reads=(), writes=(name + "_r",))

    esA = ExitStack()
    sbA = lambda name, shape, dt: esA.enter_context(nc.sbuf_tensor(name, shape, dt))
    xbuf = [sbA(f"xbuf{p}", [128, H], F32) for p in range(2)]
    xnbuf = [sbA(f"xnbuf{p}", [128, H], BF16) for p in range(2)]
    stage = [sbA(f"stage{p}", [128, H], BF16) for p in range(2)]
    junk = sbA("junk", [128, H], BF16)
    wkvraw = sbA("wkvraw", [128, 4096], F32)
    wkvb = sbA("wkvb", [128, 2 * 4096], BF16)
    w24raw = sbA("w24raw", [128, 32 * 24], F32)
    w24b = sbA("w24b", [128, 32 * 24], BF16)
    s1 = sbA("s1", [128, 1], F32)
    s2 = sbA("s2", [128, 1], F32)
    mn = sbA("mn", [128, 1], F32)
    mn2 = sbA("mn2", [128, 1], F32)
    var = sbA("var", [128, 1], F32)
    sd = sbA("sd", [128, 1], F32)
    rstd = sbA("rstd", [128, 1], F32)
    negmr = sbA("negmr", [128, 1], F32)
    s2f = sbA("s2f", [128, 1], F32)
    sdF = sbA("sdF", [128, 1], F32)
    psA = ExitStack()
    psT = [psA.enter_context(nc.psum_tensor(f"psT{p}", [128, 512], BF16)) for p in range(2)]
    psK = [psA.enter_context(nc.psum_tensor(f"psK{p}", [128, 128], F32)) for p in range(2)]
    psV = [psA.enter_context(nc.psum_tensor(f"psV{p}", [128, 128], F32)) for p in range(2)]
    psL = psA.enter_context(nc.psum_tensor("psL", [24, 128], F32))

    for bi, blk in enumerate((32, 33)):
        K.dma("sync", (lambda b2=blk: NCY.dma_start(out=wkvraw[:], in_=d["wqkv"][b2])),
              reads=("wkvb_c",), writes=("wkvraw",))
        K.op("pool", (lambda b3=bi: NCP.tensor_copy(out=wkvb[:, b3 * 4096:(b3 + 1) * 4096], in_=wkvraw[:])),
             reads=("wkvraw",), writes=("wkvb", "wkvb_c"))
    K.dma("sync", (lambda: NCY.dma_start(out=w24raw[:], in_=d["w24"])),
          reads=(), writes=("w24raw",))
    K.op("pool", (lambda: NCP.tensor_copy(out=w24b[:], in_=w24raw[:])),
         reads=("w24raw",), writes=("w24b",))

    def ln_tile(src_ap, p, tag):
        xb, xn = xbuf[p], xnbuf[p]
        rx, rn = f"xbuf{p}", f"xnbuf{p}"
        K.dma("sync", (lambda: NCY.dma_start(out=xb[:], in_=src_ap)),
              reads=(rn,), writes=(rx,))
        K.op("dve", (lambda: NCV.reduce_sum(out=s1[:], in_=xb[:], axis=AX.X)),
             reads=(rx,), writes=("s1",))
        K.op("act", (lambda: NCS.activation(out=junk[:], in_=xb[:], func=AF.Square,
                                            bias=0.0, scale=1.0, accum_out=s2[:])),
             reads=(rx,), writes=("junk", "s2"))
        K.op("dve", (lambda: NCV.tensor_scalar_mul(out=mn[:], in0=s1[:], scalar1=1.0 / H)),
             reads=("s1",), writes=("mn",))
        K.op("dve", (lambda: NCV.tensor_mul(out=mn2[:], in0=mn[:], in1=mn[:])),
             reads=("mn",), writes=("mn2",))
        K.op("dve", (lambda: NCV.scalar_tensor_tensor(out=var[:], in0=s2[:], scalar=1.0 / H,
                                                      in1=mn2[:], op0=ALU.mult, op1=ALU.subtract)),
             reads=("s2", "mn2"), writes=("var",))
        K.op("dve", (lambda: NCV.tensor_scalar_add(out=var[:], in0=var[:], scalar1=LN_EPS)),
             reads=("var",), writes=("var",))
        K.op("act", (lambda: NCS.activation(out=sd[:], in_=var[:], func=AF.Sqrt, bias=0.0, scale=1.0)),
             reads=("var",), writes=("sd",))
        K.op("dve", (lambda: NCV.reciprocal(out=rstd[:], in_=sd[:])),
             reads=("sd",), writes=("rstd",))
        K.op("dve", (lambda: NCV.scalar_tensor_tensor(out=negmr[:], in0=mn[:], scalar=-1.0,
                                                      in1=rstd[:], op0=ALU.mult, op1=ALU.mult)),
             reads=("mn", "rstd"), writes=("negmr",))
        K.op("act", (lambda: NCS.activation(out=xn[:], in_=xb[:], func=AF.Identity,
                                            bias=negmr[:, 0:1], scale=rstd[:, 0:1])),
             reads=(rx, "rstd", "negmr"), writes=(rn,))
        return xn, rn

    for i in range(NT_KV):
        p = i % 2
        xn, rn = ln_tile(d["xkv"][128 * i:128 * (i + 1), :], p, f"a1_{i}")
        st, rs = stage[p], f"stage{p}"
        for g in range(8):
            tp = g % 2
            for u in range(4):
                j = 4 * g + u
                K.op("pe", (lambda xn2=xn, j2=j, t2=tp: NCT.transpose(
                    out=psT[t2][:, 128 * (j2 % 4):128 * (j2 % 4 + 1)],
                    in_=xn2[:, 128 * j2:128 * (j2 + 1)], identity=i128h[:])),
                    reads=(rn, "i128h_r", f"psTe{tp}"), writes=(f"psT{tp}",))
            K.op("act", (lambda s3=st, g2=g, t2=tp: NCS.copy(
                out=s3[:, 512 * g2:512 * (g2 + 1)], in_=psT[t2][:])),
                reads=(f"psT{tp}",), writes=(rs, f"psTe{tp}"))
        kp = i % 2
        for c in range(32):
            K.op("pe", (lambda c2=c, s3=st, k2=kp: NCT.matmul(
                out=psK[k2][:], lhsT=wkvb[:, 128 * c2:128 * (c2 + 1)],
                rhs=s3[:, 128 * c2:128 * (c2 + 1)],
                start=(c2 == 0), stop=(c2 == 31))),
                reads=(rs, "wkvb", f"psKe{kp}"), writes=(f"psK{kp}",))
        K.op("act", (lambda i2=i, k2=kp: NCS.copy(out=kt[:, 128 * i2:128 * (i2 + 1)], in_=psK[k2][:])),
             reads=(f"psK{kp}",), writes=("kt", f"psKe{kp}"))
        for c in range(32):
            K.op("pe", (lambda c2=c, s3=st, k2=kp: NCT.matmul(
                out=psV[k2][:], lhsT=s3[:, 128 * c2:128 * (c2 + 1)],
                rhs=wkvb[:, 4096 + 128 * c2:4096 + 128 * (c2 + 1)],
                start=(c2 == 0), stop=(c2 == 31))),
                reads=(rs, "wkvb", f"psVe{kp}"), writes=(f"psV{kp}",))
        K.op("act", (lambda i2=i, k2=kp: NCS.copy(out=vt[:, 128 * i2:128 * (i2 + 1)], in_=psV[k2][:])),
             reads=(f"psV{kp}",), writes=("vt", f"psVe{kp}"))

    for e in range(NT_Q):
        p = e % 2
        xn, rn = ln_tile(d["xq"][128 * e:128 * (e + 1), :], p, f"a2_{e}")
        for g in range(8):
            tp = g % 2
            for u in range(4):
                j = 4 * g + u
                K.op("pe", (lambda xn2=xn, j2=j, t2=tp: NCT.transpose(
                    out=psT[t2][:, 128 * (j2 % 4):128 * (j2 % 4 + 1)],
                    in_=xn2[:, 128 * j2:128 * (j2 + 1)], identity=i128h[:])),
                    reads=(rn, "i128h_r", f"psTe{tp}"), writes=(f"psT{tp}",))
            for u in range(4):
                j = 4 * g + u
                K.op("act", (lambda j2=j, e2=e, t2=tp, u2=u: NCS.copy(
                    out=xnt[:, 1024 * j2 + 128 * e2:1024 * j2 + 128 * (e2 + 1)],
                    in_=psT[t2][:, 128 * u2:128 * (u2 + 1)])),
                    reads=(f"psT{tp}",),
                    writes=("xnt", f"psTe{tp}") if u == 3 else ("xnt",))
        K.op("act", (lambda xn2=xn: NCS.activation(out=junk[:], in_=xn2[:], func=AF.Square,
                                                   bias=0.0, scale=1.0, accum_out=s2f[:])),
             reads=(rn,), writes=("junk", "s2f"))
        K.op("dve", (lambda: NCV.tensor_scalar(out=s2f[:], in0=s2f[:], scalar1=1.0 / H,
                                               scalar2=RMS_EPS, op0=ALU.mult, op1=ALU.add)),
             reads=("s2f",), writes=("s2f",))
        K.op("act", (lambda: NCS.activation(out=sdF[:], in_=s2f[:], func=AF.Sqrt, bias=0.0, scale=1.0)),
             reads=("s2f",), writes=("sdF",))
        K.op("dve", (lambda e2=e: NCV.reciprocal(out=rstdF[:, e2:e2 + 1], in_=sdF[:])),
             reads=("sdF",), writes=("rstdF",))
        for c in range(32):
            K.op("pe", (lambda c2=c, e2=e: NCT.matmul(
                out=psL[:], lhsT=w24b[:, 24 * c2:24 * (c2 + 1)],
                rhs=xnt[:, 1024 * c2 + 128 * e2:1024 * c2 + 128 * (e2 + 1)],
                start=(c2 == 0), stop=(c2 == 31))),
                reads=("xnt", "w24b", "psLe"), writes=("psL",))
        K.op("act", (lambda e2=e: NCS.copy(out=lgraw[:, 128 * e2:128 * (e2 + 1)], in_=psL[:])),
             reads=("psL",), writes=("lgraw", "psLe"))

    if upto == "A":
        return _dump([("d_xnt", xnt), ("d_kt", kt), ("d_vt", vt),
                      ("d_lgraw", lgraw), ("d_rstdF", rstdF)])
    K.barrier()
    psA.close()
    esA.close()

    qt = sb("qt", [128, NH * 1024], BF16)
    esB = ExitStack()
    sbB = lambda name, shape, dt: esB.enter_context(nc.sbuf_tensor(name, shape, dt))
    wqraw = sbB("wqraw", [128, 4096], F32)
    wqb = [sbB(f"wqb{p}", [128, 4096], BF16) for p in range(2)]
    qsq = sbB("qsq", [128, 1024], BF16)
    q2row = sbB("q2row", [1, 1024], F32)
    q2cols = sbB("q2cols", [1, 40], F32)
    tiny = sbB("tiny", [1, 4], F32)
    psB = ExitStack()
    psQ = [psB.enter_context(nc.psum_tensor(f"psQ{p}", [128, 512], F32)) for p in range(2)]
    ps12 = psB.enter_context(nc.psum_tensor("ps12", [1, 512], F32))
    psC1 = psB.enter_context(nc.psum_tensor("psC1", [128, 1], F32))

    for h in range(NH):
        K.dma("sync", (lambda h2=h: NCY.dma_start(out=wqraw[:], in_=d["wqkv"][h2])),
              reads=("wqraw_c",), writes=("wqraw",))
        wp = h % 2
        K.op("pool", (lambda w2=wp: NCP.tensor_copy(out=wqb[w2][:], in_=wqraw[:])),
             reads=("wqraw",), writes=(f"wqb{wp}", "wqraw_c"))
        for th in range(2):
            qp_ = (2 * h + th) % 2
            for c in range(32):
                K.op("pe", (lambda c2=c, t2=th, q2=qp_, w2=wp: NCT.matmul(
                    out=psQ[q2][:], lhsT=wqb[w2][:, 128 * c2:128 * (c2 + 1)],
                    rhs=xnt[:, 1024 * c2 + 512 * t2:1024 * c2 + 512 * (t2 + 1)],
                    start=(c2 == 0), stop=(c2 == 31))),
                    reads=(f"wqb{wp}", "xnt", f"psQe{qp_}"), writes=(f"psQ{qp_}",))
            K.op("act", (lambda h2=h, t2=th, q2=qp_: NCS.copy(
                out=qt[:, 1024 * h2 + 512 * t2:1024 * h2 + 512 * (t2 + 1)], in_=psQ[q2][:])),
                reads=(f"psQ{qp_}",), writes=("qt", f"psQe{qp_}"))
        K.op("act", (lambda h2=h: NCS.activation(out=qsq[:], in_=qt[:, 1024 * h2:1024 * (h2 + 1)],
                                                 func=AF.Square, bias=0.0, scale=1.0)),
             reads=("qt",), writes=("qsq",))
        for th in range(2):
            K.op("pe", (lambda t2=th: NCT.matmul(
                out=ps12[:], lhsT=onescol_h[:], rhs=qsq[:, 512 * t2:512 * (t2 + 1)],
                start=True, stop=True)),
                reads=("qsq", "onescol_h_r", "ps12e"), writes=("ps12",))
            K.op("act", (lambda t2=th: NCS.copy(out=q2row[:, 512 * t2:512 * (t2 + 1)], in_=ps12[:])),
                 reads=("ps12",), writes=("q2row", "ps12e"))
        K.op("dve", (lambda h2=h: NCV.reduce_max(out=q2cols[:, h2:h2 + 1], in_=q2row[:], axis=AX.X)),
             reads=("q2row",), writes=("q2cols",))
    for g in range(4):
        K.op("act", (lambda g2=g: NCS.activation(out=qsq[:], in_=kt[:, 1024 * g2:1024 * (g2 + 1)],
                                                 func=AF.Square, bias=0.0, scale=1.0)),
             reads=("kt", "qsq_c"), writes=("qsq",))
        for th in range(2):
            K.op("pe", (lambda t2=th: NCT.matmul(
                out=ps12[:], lhsT=onescol_h[:], rhs=qsq[:, 512 * t2:512 * (t2 + 1)],
                start=True, stop=True)),
                reads=("qsq", "onescol_h_r", "ps12e"), writes=("ps12", "qsq_c"))
            K.op("act", (lambda t2=th: NCS.copy(out=q2row[:, 512 * t2:512 * (t2 + 1)], in_=ps12[:])),
                 reads=("ps12",), writes=("q2row", "ps12e"))
        K.op("dve", (lambda g2=g: NCV.reduce_max(out=q2cols[:, 32 + g2:33 + g2], in_=q2row[:], axis=AX.X)),
             reads=("q2row",), writes=("q2cols",))
    K.op("dve", (lambda: NCV.reduce_max(out=tiny[:, 0:1], in_=q2cols[:, 0:32], axis=AX.X)),
         reads=("q2cols",), writes=("t_q2m",))
    K.op("dve", (lambda: NCV.reduce_max(out=tiny[:, 1:2], in_=q2cols[:, 32:36], axis=AX.X)),
         reads=("q2cols",), writes=("t_k2m",))
    K.op("dve", (lambda: NCV.tensor_mul(out=tiny[:, 2:3], in0=tiny[:, 0:1], in1=tiny[:, 1:2])),
         reads=("t_q2m", "t_k2m"), writes=("t_prod",))
    K.op("act", (lambda: NCS.activation(out=tiny[:, 3:4], in_=tiny[:, 2:3], func=AF.Sqrt,
                                        bias=0.0, scale=1.0)),
         reads=("t_prod",), writes=("t_sq",))
    K.op("dve", (lambda: NCV.tensor_scalar(out=tiny[:, 2:3], in0=tiny[:, 3:4], scalar1=SCALE,
                                           scalar2=3.0, op0=ALU.mult, op1=ALU.add)),
         reads=("t_sq",), writes=("t_C",))
    K.op("pe", (lambda: NCT.matmul(out=psC1[:], lhsT=ones1f[:], rhs=tiny[:, 2:3],
                                   start=True, stop=True)),
         reads=("t_C", "ones1f_r"), writes=("psC1",))
    K.op("act", (lambda: NCS.activation(out=negc[:], in_=psC1[:], func=AF.Copy, bias=0.0, scale=-1.0)),
         reads=("psC1",), writes=("negc",))

    if upto == "B":
        return _dump([("d_qt", qt), ("d_negc", negc), ("d_lgraw", lgraw)])
    K.barrier()
    psB.close()
    esB.close()

    esC = ExitStack()
    sbC = lambda name, shape, dt: esC.enter_context(nc.sbuf_tensor(name, shape, dt))
    tsc = [sbC(f"tsc{p}", [128, 512], F32) for p in range(2)]
    pbuf = [sbC(f"pbuf{p}", [128, 512], BF16) for p in range(2)]
    pts = [sbC(f"pts{p}", [128, 512], BF16) for p in range(2)]
    rows = sbC("rows", [128, 8], F32)
    rs_ = sbC("rs", [128, 1], F32)
    rrec = sbC("rrec", [128, 1], F32)
    rect = sbC("rect", [1, 128], F32)
    rtile = sbC("rtile", [128, 128], F32)
    basehe = sbC("basehe", [128, 1], F32)
    tbcol = sbC("tbcol", [128, 1], F32)
    biast = [sbC(f"biast{p}", [128, 1], F32) for p in range(2)]
    psC = ExitStack()
    psS = [psC.enter_context(nc.psum_tensor(f"psS{p}", [128, 512], F32)) for p in range(2)]
    psPT = [psC.enter_context(nc.psum_tensor(f"psPT{p}", [128, 512], BF16)) for p in range(2)]
    psO = [psC.enter_context(nc.psum_tensor(f"psO{p}", [128, 128], F32)) for p in range(2)]
    psR = psC.enter_context(nc.psum_tensor("psR", [128, 128], F32))
    psRT = psC.enter_context(nc.psum_tensor("psRT", [1, 128], F32))

    kbctr = 0
    for h in range(NH):
        slope = float(SLOPES[h])
        sprime = slope / SCALE
        for e in range(NT_Q):
            oe = (h * NT_Q + e) % 2
            qlo = 1024 * h + 128 * e
            K.op("dve", (lambda e2=e, sl=slope: NCV.tensor_scalar(
                out=tbcol[:], in0=qcol[:], scalar1=float(512 * e2), scalar2=-sl,
                op0=ALU.add, op1=ALU.mult)),
                reads=("qcol_r",), writes=("tbcol",))
            K.op("dve", (lambda: NCV.scalar_tensor_tensor(
                out=basehe[:], in0=tbcol[:], scalar=1.0, in1=negc[:],
                op0=ALU.mult, op1=ALU.add)),
                reads=("tbcol", "negc"), writes=("basehe",))
            for kb in range(e + 1):
                pp = kbctr % 2
                kbctr += 1
                K.op("pe", (lambda q2=qlo, k2=kb, p2=pp: NCT.matmul(
                    out=psS[p2][:], lhsT=qt[:, q2:q2 + 128],
                    rhs=kt[:, 512 * k2:512 * (k2 + 1)], start=True, stop=True)),
                    reads=("qt", "kt", f"psSe{pp}"), writes=(f"psS{pp}",))
                K.op("dve", (lambda k2=kb, p2=pp, sl=slope: NCV.tensor_scalar_add(
                    out=biast[p2][:], in0=basehe[:], scalar1=float(sl * 512 * k2))),
                    reads=("basehe",), writes=(f"biast{pp}",))
                dt_ = dmask if kb == e else dplain
                K.op("dve", (lambda d2=dt_, p2=pp, sp=sprime: NCV.scalar_tensor_tensor(
                    out=tsc[p2][:], in0=d2[:], scalar=sp, in1=psS[p2][:],
                    op0=ALU.mult, op1=ALU.add)),
                    reads=("dplain_r", "dmask_r", f"psS{pp}", f"tsce{pp}"),
                    writes=(f"tsc{pp}", f"psSe{pp}"))
                K.op("act", (lambda p2=pp, k2=kb: NCS.activation(
                    out=pbuf[p2][:], in_=tsc[p2][:], func=AF.Exp,
                    bias=biast[p2][:, 0:1], scale=SCALE,
                    accum_out=rows[:, k2:k2 + 1])),
                    reads=(f"tsc{pp}", f"biast{pp}", f"pbe{pp}", "rows_c"),
                    writes=(f"pbuf{pp}", "rows", f"tsce{pp}"))
                for u in range(4):
                    K.op("pe", (lambda p2=pp, u2=u: NCT.transpose(
                        out=psPT[p2][:, 128 * u2:128 * (u2 + 1)],
                        in_=pbuf[p2][:, 128 * u2:128 * (u2 + 1)], identity=i128h[:])),
                        reads=(f"pbuf{pp}", "i128h_r", f"psPTe{pp}"), writes=(f"psPT{pp}",))
                K.op("act", (lambda p2=pp: NCS.copy(out=pts[p2][:], in_=psPT[p2][:])),
                     reads=(f"psPT{pp}",), writes=(f"pts{pp}", f"psPTe{pp}", f"pbe{pp}"))
                for u in range(4):
                    kc = 4 * kb + u
                    K.op("pe", (lambda p2=pp, u2=u, k3=kc, o2=oe, st=(kb == 0 and u == 0),
                                sp2=(kb == e and u == 3): NCT.matmul(
                        out=psO[o2][:], lhsT=vt[:, 128 * k3:128 * (k3 + 1)],
                        rhs=pts[p2][:, 128 * u2:128 * (u2 + 1)], start=st, stop=sp2)),
                        reads=(f"pts{pp}", "vt", f"psOe{oe}"), writes=(f"psO{oe}",))
            K.op("dve", (lambda e2=e: NCV.reduce_sum(out=rs_[:], in_=rows[:, 0:e2 + 1], axis=AX.X)),
                 reads=("rows",), writes=("rs", "rows_c"))
            K.op("dve", (lambda: NCV.reciprocal(out=rrec[:], in_=rs_[:])),
                 reads=("rs",), writes=("rrec",))
            K.op("pe", (lambda: NCT.transpose(out=psRT[:], in_=rrec[:], identity=i128f[:])),
                 reads=("rrec", "i128f_r", "psRTe"), writes=("psRT",))
            K.op("act", (lambda: NCS.copy(out=rect[:], in_=psRT[:])),
                 reads=("psRT",), writes=("rect", "psRTe"))
            K.op("pe", (lambda: NCT.matmul(out=psR[:], lhsT=ones1f[:], rhs=rect[:],
                                           start=True, stop=True)),
                 reads=("rect", "ones1f_r", "psRe"), writes=("psR",))
            K.op("act", (lambda: NCS.copy(out=rtile[:], in_=psR[:])),
                 reads=("psR",), writes=("rtile", "psRe"))
            K.op("dve", (lambda q2=qlo, o2=oe: NCV.tensor_tensor(
                out=qt[:, q2:q2 + 128], in0=psO[o2][:], in1=rtile[:], op=ALU.mult)),
                reads=(f"psO{oe}", "rtile"), writes=("qt", f"psOe{oe}"))

    if upto == "C":
        return _dump([("d_qt", qt)])
    K.barrier()
    psC.close()
    esC.close()
    esKV.close()

    esS = ExitStack()
    sbS = lambda name, shape, dt: esS.enter_context(nc.sbuf_tensor(name, shape, dt))
    lg = sbS("lg", [128, 192], F32)
    bb24 = sbS("bb24", [128, 24], F32)
    zres = sbS("zres", [128, 128], F32)
    asb = sbS("asb", [128, 128], F32)
    bsb = sbS("bsb", [128, 128], F32)
    esb = sbS("esb", [128, 128], F32)
    hsb = sbS("hsb", [128, 128], F32)
    msb = sbS("msb", [128, 32], F32)
    lsb = sbS("lsb", [128, 32], F32)
    ssb = sbS("ssb", [128, 32], F32)
    ubuf = sbS("ubuf", [128, 32], F32)
    vbuf = sbS("vbuf", [128, 32], F32)
    mp = sbS("mp", [128, 8], F32)
    ep = sbS("ep", [128, 32], F32)
    sp_ = sbS("sp", [128, 8], F32)
    hpre = sbS("hpre", [128, 32], F32)
    hpost = sbS("hpost", [128, 32], F32)
    gcol = sbS("gcol", [128, 1], F32)
    omg = sbS("omg", [128, 1], F32)
    hresT = sbS("hresT", [16, 1024], BF16)
    hpreT = sbS("hpreT", [4, 1024], BF16)
    hpostT = sbS("hpostT", [4, 1024], BF16)
    psSt = ExitStack()
    psLT = psSt.enter_context(nc.psum_tensor("psLT", [128, 24], F32))
    psHT = psSt.enter_context(nc.psum_tensor("psHT", [16, 128], F32))
    psG = psSt.enter_context(nc.psum_tensor("psG", [128, 1], F32))

    K.op("pe", (lambda: NCT.matmul(out=psLT[:], lhsT=ones1f[:], rhs=bases_row[:],
                                   start=True, stop=True)),
         reads=("bases24_r", "ones1f_r", "psLTe"), writes=("psLT",))
    K.op("act", (lambda: NCS.copy(out=bb24[:], in_=psLT[:])),
         reads=("psLT",), writes=("bb24", "psLTe"))
    for e in range(8):
        K.op("pe", (lambda e2=e: NCT.transpose(out=psLT[:], in_=lgraw[:, 128 * e2:128 * (e2 + 1)],
                                               identity=i128f[0:24, 0:24])),
             reads=("lgraw", "i128f_r", "psLTe"), writes=("psLT",))
        K.op("dve", (lambda e2=e: NCV.scalar_tensor_tensor(
            out=lg[:, 24 * e2:24 * (e2 + 1)], in0=psLT[:], scalar=rstdF[:, e2:e2 + 1],
            in1=bb24[:], op0=ALU.mult, op1=ALU.add)),
            reads=("psLT", "rstdF", "bb24"), writes=("lg", "psLTe"))
    lgv = lg.rearrange("p (a x) -> p a x", a=8, x=24)
    K.op("dve", (lambda: NCV.tensor_scalar_mul(
        out=zres.rearrange("p (a x) -> p a x", a=8, x=16),
        in0=lgv[:, :, 0:16], scalar1=TAU_INV)),
        reads=("lg",), writes=("zres",))
    K.op("dve", (lambda: NCV.memset(ubuf[:], 0.0)), reads=(), writes=("ubuf",))
    K.op("dve", (lambda: NCV.memset(vbuf[:], 0.0)), reads=(), writes=("vbuf",))

    z4 = zres.rearrange("p (a i j) -> p a i j", a=8, i=4, j=4)
    a4 = asb.rearrange("p (a i j) -> p a i j", a=8, i=4, j=4)
    b4 = bsb.rearrange("p (a i j) -> p a i j", a=8, i=4, j=4)
    e4 = esb.rearrange("p (a i j) -> p a i j", a=8, i=4, j=4)
    uv = ubuf.rearrange("p (a i) -> p a i", a=8, i=4)
    vv = vbuf.rearrange("p (a j) -> p a j", a=8, j=4)
    mv = msb.rearrange("p (a i) -> p a i", a=8, i=4)
    sv = ssb.rearrange("p (a i) -> p a i", a=8, i=4)
    vb_bc = vv.unsqueeze(2).broadcast_to([128, 8, 4, 4])
    ub_bc = uv.unsqueeze(3).broadcast_to([128, 8, 4, 4])
    mb_bc = mv.unsqueeze(3).broadcast_to([128, 8, 4, 4])
    m2_bc = mv.unsqueeze(2).broadcast_to([128, 8, 4, 4])
    a4_ji = asb.rearrange("p (a i j) -> p a j i", a=8, i=4, j=4)
    e4_ji = esb.rearrange("p (a i j) -> p a j i", a=8, i=4, j=4)

    for it in range(SINK_ITERS):
        K.op("dve", (lambda: NCV.tensor_tensor(out=a4, in0=z4, in1=vb_bc, op=ALU.subtract)),
             reads=("zres", "vbuf"), writes=("asb",))
        K.op("dve", (lambda: NCV.tensor_reduce(out=mv, in_=a4, axis=AX.X, op=ALU.max)),
             reads=("asb",), writes=("msb",))
        K.op("dve", (lambda: NCV.tensor_tensor(out=b4, in0=a4, in1=mb_bc, op=ALU.subtract)),
             reads=("asb", "msb"), writes=("bsb",))
        K.op("act", (lambda: NCS.activation(out=esb[:], in_=bsb[:], func=AF.Exp, bias=0.0, scale=1.0)),
             reads=("bsb",), writes=("esb",))
        K.op("dve", (lambda: NCV.tensor_reduce(out=sv, in_=e4, axis=AX.X, op=ALU.add)),
             reads=("esb",), writes=("ssb",))
        K.op("act", (lambda: NCS.activation(out=lsb[:], in_=ssb[:], func=AF.Ln, bias=0.0, scale=1.0)),
             reads=("ssb",), writes=("lsb",))
        K.op("dve", (lambda: NCV.tensor_add(out=ubuf[:], in0=msb[:], in1=lsb[:])),
             reads=("msb", "lsb"), writes=("ubuf",))
        K.op("dve", (lambda: NCV.tensor_tensor(out=a4, in0=z4, in1=ub_bc, op=ALU.subtract)),
             reads=("zres", "ubuf"), writes=("asb",))
        K.op("dve", (lambda: NCV.tensor_reduce(out=mv, in_=a4_ji, axis=AX.X, op=ALU.max)),
             reads=("asb",), writes=("msb",))
        K.op("dve", (lambda: NCV.tensor_tensor(out=b4, in0=a4, in1=m2_bc, op=ALU.subtract)),
             reads=("asb", "msb"), writes=("bsb",))
        K.op("act", (lambda: NCS.activation(out=esb[:], in_=bsb[:], func=AF.Exp, bias=0.0, scale=1.0)),
             reads=("bsb",), writes=("esb",))
        K.op("dve", (lambda: NCV.tensor_reduce(out=sv, in_=e4_ji, axis=AX.X, op=ALU.add)),
             reads=("esb",), writes=("ssb",))
        K.op("act", (lambda: NCS.activation(out=lsb[:], in_=ssb[:], func=AF.Ln, bias=0.0, scale=1.0)),
             reads=("ssb",), writes=("lsb",))
        K.op("dve", (lambda: NCV.tensor_add(out=vbuf[:], in0=msb[:], in1=lsb[:])),
             reads=("msb", "lsb"), writes=("vbuf",))
    K.op("dve", (lambda: NCV.tensor_tensor(out=a4, in0=z4, in1=ub_bc, op=ALU.subtract)),
         reads=("zres", "ubuf"), writes=("asb",))
    K.op("dve", (lambda: NCV.tensor_tensor(out=b4, in0=a4, in1=vb_bc, op=ALU.subtract)),
         reads=("asb", "vbuf"), writes=("bsb",))
    K.op("act", (lambda: NCS.activation(out=hsb[:], in_=bsb[:], func=AF.Exp, bias=0.0, scale=1.0)),
         reads=("bsb",), writes=("hsb",))
    K.op("act", (lambda: NCS.activation(out=gate_s[:], in_=gate_s[:], func=AF.Sigmoid, bias=0.0, scale=1.0)),
         reads=("gate_r",), writes=("gate_v",))
    K.op("pe", (lambda: NCT.matmul(out=psG[:], lhsT=ones1f[:], rhs=gate_s[:], start=True, stop=True)),
         reads=("gate_v", "ones1f_r"), writes=("psG",))
    K.op("act", (lambda: NCS.copy(out=gcol[:], in_=psG[:])),
         reads=("psG",), writes=("gcol",))
    K.op("dve", (lambda: NCV.tensor_scalar(out=omg[:], in0=gcol[:], scalar1=-1.0, scalar2=1.0,
                                           op0=ALU.mult, op1=ALU.add)),
         reads=("gcol",), writes=("omg",))
    K.op("dve", (lambda: NCV.tensor_scalar_mul(out=hsb[:], in0=hsb[:], scalar1=gcol[:, 0:1])),
         reads=("hsb", "gcol"), writes=("hsb",))
    hdg = hsb.rearrange("p (a x) -> p a x", a=8, x=16)
    for dgi in range(4):
        K.op("dve", (lambda d2=dgi: NCV.tensor_scalar_add(
            out=hdg[:, :, 5 * d2:5 * d2 + 1], in0=hdg[:, :, 5 * d2:5 * d2 + 1],
            scalar1=omg[:, 0:1])),
            reads=("hsb", "omg"), writes=("hsb",))
    for (name, off, out_sb) in (("hpre", 16, hpre), ("hpost", 20, hpost)):
        lgs = lgv[:, :, off:off + 4]
        mbc = mp[:].unsqueeze(2).broadcast_to([128, 8, 4])
        spc = sp_[:].unsqueeze(2).broadcast_to([128, 8, 4])
        epv = ep.rearrange("p (a n) -> p a n", a=8, n=4)
        K.op("dve", (lambda l2=lgs: NCV.tensor_reduce(out=mp[:], in_=l2, axis=AX.X, op=ALU.max)),
             reads=("lg",), writes=("mp",))
        K.op("dve", (lambda l2=lgs, m2=mbc: NCV.tensor_tensor(
            out=epv, in0=l2, in1=m2, op=ALU.subtract)),
            reads=("lg", "mp"), writes=("ep",))
        K.op("act", (lambda: NCS.activation(out=ep[:], in_=ep[:], func=AF.Exp, bias=0.0, scale=1.0)),
             reads=("ep",), writes=("ep",))
        K.op("dve", (lambda: NCV.tensor_reduce(out=sp_[:], in_=epv, axis=AX.X, op=ALU.add)),
             reads=("ep",), writes=("sp",))
        K.op("dve", (lambda: NCV.reciprocal(out=sp_[:], in_=sp_[:])),
             reads=("sp",), writes=("sp",))
        K.op("dve", (lambda o2=out_sb, s3=spc: NCV.tensor_tensor(
            out=o2.rearrange("p (a n) -> p a n", a=8, n=4),
            in0=epv, in1=s3, op=ALU.mult)),
            reads=("ep", "sp"), writes=(name,))
    for e in range(8):
        K.op("pe", (lambda e2=e: NCT.transpose(out=psHT[:], in_=hsb[:, 16 * e2:16 * (e2 + 1)],
                                               identity=i128f[:])),
             reads=("hsb", "i128f_r", "psHTe"), writes=("psHT",))
        K.op("dve", (lambda e2=e: NCV.tensor_copy(out=hresT[:, 128 * e2:128 * (e2 + 1)], in_=psHT[:])),
             reads=("psHT",), writes=("hresT", "psHTe"))
        for (src, dst, rn) in ((hpre, hpreT, "hpreT"), (hpost, hpostT, "hpostT")):
            K.op("pe", (lambda e2=e, s3=src: NCT.transpose(
                out=psHT[0:4, :], in_=s3[:, 4 * e2:4 * (e2 + 1)], identity=i128f[:])),
                reads=("hpre", "hpost", "i128f_r", "psHTe"), writes=("psHT",))
            K.op("dve", (lambda e2=e, d2=dst: NCV.tensor_copy(
                out=d2[:, 128 * e2:128 * (e2 + 1)], in_=psHT[0:4, :])),
                reads=("psHT",), writes=(rn, "psHTe"))
    K.dma("pool", (lambda: NCP.dma_start(out=hscr[0:16, :], in_=hresT[:])),
          reads=("hresT",), writes=("hscr_res",))
    K.dma("pool", (lambda: NCP.dma_start(out=hscr[16:20, :], in_=hpreT[:])),
          reads=("hpreT",), writes=("hscr_pre",))
    K.dma("pool", (lambda: NCP.dma_start(out=hscr[20:24, :], in_=hpostT[:])),
          reads=("hpostT",), writes=("hscr_post",))
    if upto == "S":
        return _dump([("d_hresT", hresT), ("d_hpreT", hpreT), ("d_hpostT", hpostT),
                      ("d_lg", lg)])
    K.barrier()
    psSt.close()
    esS.close()

    yt = sb("yt", [128, 8 * 1024], BF16)
    esD1 = ExitStack()
    sb1 = lambda name, shape, dt: esD1.enter_context(nc.sbuf_tensor(name, shape, dt))
    wbraw = sb1("wbraw", [128, 1024], F32)
    wbrb = sb1("wbrb", [128, 8192], BF16)
    rowsl = sb1("rowsl", [1, 1024], BF16)
    hpre_bc = sb1("hpre_bc", [128, 4096], BF16)
    fftmp = sb1("fftmp", [128, 512], F32)
    xpre = sb1("xpre", [128, 8192], BF16)
    psD1 = ExitStack()
    psY = [psD1.enter_context(nc.psum_tensor(f"psY{p}", [128, 512], F32)) for p in range(2)]
    psBC = [psD1.enter_context(nc.psum_tensor(f"psBC{p}", [128, 512], F32)) for p in range(2)]
    for hf in range(8):
        K.dma("sync", (lambda h2=hf: NCY.dma_start(out=wbraw[:], in_=d["wbr"][:, 1024 * h2:1024 * (h2 + 1)])),
              reads=("wbraw_c",), writes=("wbraw",))
        K.op("pool", (lambda h2=hf: NCP.tensor_copy(out=wbrb[:, 1024 * h2:1024 * (h2 + 1)], in_=wbraw[:])),
             reads=("wbraw",), writes=("wbrb", "wbraw_c"))
    for n in range(4):
        K.dma("pool", (lambda n2=n: NCP.dma_start(out=rowsl[:], in_=hscr[16 + n2:17 + n2, :])),
              reads=("hscr_pre", "rowsl"), writes=("rowsl",))
        for th in range(2):
            bp = th % 2
            K.op("pe", (lambda t2=th, b2=bp: NCT.matmul(
                out=psBC[b2][:], lhsT=ones1h[:], rhs=rowsl[:, 512 * t2:512 * (t2 + 1)],
                start=True, stop=True)),
                reads=("rowsl", "ones1h_r", f"psBCe{bp}"), writes=(f"psBC{bp}",))
            K.op("act", (lambda n2=n, t2=th, b2=bp: NCS.copy(
                out=hpre_bc[:, 1024 * n2 + 512 * t2:1024 * n2 + 512 * (t2 + 1)], in_=psBC[b2][:])),
                reads=(f"psBC{bp}",), writes=("hpre_bc", f"psBCe{bp}"))
    for cc in range(8):
        for th in range(2):
            for n in range(4):
                jg = n * 8 + cc
                xsl = (1024 * jg + 512 * th, 1024 * jg + 512 * (th + 1))
                hsl = (1024 * n + 512 * th, 1024 * n + 512 * (th + 1))
                osl = (1024 * cc + 512 * th, 1024 * cc + 512 * (th + 1))
                if n == 0:
                    K.op("pool", (lambda x2=xsl, h2=hsl, o2=osl: NCP.tensor_tensor(
                        out=xpre[:, o2[0]:o2[1]], in0=xnt[:, x2[0]:x2[1]],
                        in1=hpre_bc[:, h2[0]:h2[1]], op=ALU.mult)),
                        reads=("xnt", "hpre_bc"), writes=("xpre",))
                else:
                    K.op("pool", (lambda x2=xsl, h2=hsl: NCP.tensor_tensor(
                        out=fftmp[:], in0=xnt[:, x2[0]:x2[1]],
                        in1=hpre_bc[:, h2[0]:h2[1]], op=ALU.mult)),
                        reads=("xnt", "hpre_bc"), writes=("fftmp",))
                    K.op("pool", (lambda o2=osl: NCP.tensor_tensor(
                        out=xpre[:, o2[0]:o2[1]], in0=xpre[:, o2[0]:o2[1]],
                        in1=fftmp[:], op=ALU.add)),
                        reads=("fftmp", "xpre"), writes=("xpre",))
    for co in range(8):
        for th in range(2):
            yp = (2 * co + th) % 2
            for cc in range(8):
                K.op("pe", (lambda c2=cc, o2=co, t2=th, y2=yp: NCT.matmul(
                    out=psY[y2][:], lhsT=wbrb[:, 1024 * o2 + 128 * c2:1024 * o2 + 128 * (c2 + 1)],
                    rhs=xpre[:, 1024 * c2 + 512 * t2:1024 * c2 + 512 * (t2 + 1)],
                    start=(c2 == 0), stop=(c2 == 7))),
                    reads=("wbrb", "xpre", f"psYe{yp}"), writes=(f"psY{yp}",))
            K.op("act", (lambda o2=co, t2=th, y2=yp: NCS.copy(
                out=yt[:, 1024 * o2 + 512 * t2:1024 * o2 + 512 * (t2 + 1)], in_=psY[y2][:])),
                reads=(f"psY{yp}",), writes=("yt", f"psYe{yp}"))
    if upto == "D1":
        return _dump([("d_yt", yt), ("d_xpre", xpre), ("d_hpre_bc", hpre_bc)])
    K.barrier()
    psD1.close()
    esD1.close()

    esD = ExitStack()
    sbD = lambda name, shape, dt: esD.enter_context(nc.sbuf_tensor(name, shape, dt))
    woraw = sbD("woraw", [128, 1024], F32)
    wob = [sbD(f"wob{p}", [128, 4096], BF16) for p in range(2)]
    bc5 = sbD("bc5", [128, 5 * 1024], BF16)
    rslot = [sbD(f"rslot{j}", [1, 1024], BF16) for j in range(1)]
    xtrow = [sbD(f"xtrow{p}", [128, 1024], F32) for p in range(2)]
    osb = [sbD(f"osb{p}", [128, 512], F32) for p in range(2)]
    ffac = sbD("ffac", [128, 512], F32)
    fft2 = sbD("fft2", [128, 512], F32)
    psD = ExitStack()
    psW = [psD.enter_context(nc.psum_tensor(f"psW{p}", [128, 512], F32)) for p in range(2)]
    psB2 = [psD.enter_context(nc.psum_tensor(f"psB2{p}", [128, 512], F32)) for p in range(2)]

    for n in range(4):
        for j in range(5):
            src_row = (n * 4 + j) if j < 4 else (20 + n)
            rdep = "hscr_res" if j < 4 else "hscr_post"
            K.dma("pool", (lambda s3=src_row: NCP.dma_start(
                out=rslot[0][:], in_=hscr[s3:s3 + 1, :])),
                reads=(rdep, "rslot0"), writes=("rslot0",))
            for th in range(2):
                bp = th % 2
                K.op("pe", (lambda t2=th, b2=bp: NCT.matmul(
                    out=psB2[b2][:], lhsT=ones1h[:], rhs=rslot[0][:, 512 * t2:512 * (t2 + 1)],
                    start=True, stop=True)),
                    reads=("rslot0", "ones1h_r", f"psB2e{bp}"), writes=(f"psB2{bp}",))
                K.op("act", (lambda j2=j, t2=th, b2=bp: NCS.copy(
                    out=bc5[:, 1024 * j2 + 512 * t2:1024 * j2 + 512 * (t2 + 1)], in_=psB2[b2][:])),
                    reads=(f"psB2{bp}",), writes=("bc5", f"psB2e{bp}"))
        for cc in range(8):
            of = n * 8 + cc
            wp = of % 2
            for hf in range(4):
                K.dma("sync", (lambda o2=of, h2=hf: NCY.dma_start(
                    out=woraw[:], in_=d["wo"][o2][:, 1024 * h2:1024 * (h2 + 1)])),
                    reads=("woraw_c",), writes=("woraw",))
                K.op("pool", (lambda w2=wp, h2=hf: NCP.tensor_copy(
                    out=wob[w2][:, 1024 * h2:1024 * (h2 + 1)], in_=woraw[:])),
                    reads=("woraw",), writes=(f"wob{wp}", "woraw_c"))
            K.dma("pool", (lambda o2=of, w2=wp: NCP.dma_start(
                out=xtrow[w2][:], in_=d["xqt"][128 * o2:128 * (o2 + 1), :])),
                reads=(f"xtrowc{wp}",), writes=(f"xtrow{wp}",))
            for th in range(2):
                pw = th % 2
                for c in range(32):
                    K.op("pe", (lambda c2=c, t2=th, p2=pw, w2=wp: NCT.matmul(
                        out=psW[p2][:], lhsT=wob[w2][:, 128 * c2:128 * (c2 + 1)],
                        rhs=qt[:, 1024 * c2 + 512 * t2:1024 * c2 + 512 * (t2 + 1)],
                        start=(c2 == 0), stop=(c2 == 31))),
                        reads=(f"wob{wp}", "qt", f"psWe{pw}"), writes=(f"psW{pw}",))
                for j in range(4):
                    jg = j * 8 + cc
                    if j == 0:
                        K.op("pool", (lambda j2=jg, t2=th: NCP.tensor_tensor(
                            out=ffac[:], in0=xnt[:, 1024 * j2 + 512 * t2:1024 * j2 + 512 * (t2 + 1)],
                            in1=bc5[:, 512 * t2:512 * (t2 + 1)], op=ALU.mult)),
                            reads=("xnt", "bc5", "ffac_c"), writes=("ffac",))
                    else:
                        K.op("pool", (lambda j2=jg, j3=j, t2=th: NCP.tensor_tensor(
                            out=fft2[:], in0=xnt[:, 1024 * j2 + 512 * t2:1024 * j2 + 512 * (t2 + 1)],
                            in1=bc5[:, 1024 * j3 + 512 * t2:1024 * j3 + 512 * (t2 + 1)], op=ALU.mult)),
                            reads=("xnt", "bc5"), writes=("fft2",))
                        K.op("pool", (lambda: NCP.tensor_tensor(
                            out=ffac[:], in0=ffac[:], in1=fft2[:], op=ALU.add)),
                            reads=("fft2", "ffac"), writes=("ffac",))
                K.op("pool", (lambda c2=cc, t2=th: NCP.tensor_tensor(
                    out=fft2[:], in0=yt[:, 1024 * c2 + 512 * t2:1024 * c2 + 512 * (t2 + 1)],
                    in1=bc5[:, 4096 + 512 * t2:4096 + 512 * (t2 + 1)], op=ALU.mult)),
                    reads=("yt", "bc5"), writes=("fft2",))
                K.op("pool", (lambda: NCP.tensor_tensor(
                    out=ffac[:], in0=ffac[:], in1=fft2[:], op=ALU.add)),
                    reads=("fft2", "ffac"), writes=("ffac",))
                K.op("dve", (lambda t2=th, p2=pw, w2=wp, o2=(th % 2): NCV.tensor_tensor(
                    out=osb[o2][:], in0=psW[p2][:], in1=xtrow[w2][:, 512 * t2:512 * (t2 + 1)],
                    op=ALU.add)),
                    reads=(f"psW{pw}", f"xtrow{wp}", f"osbc{th % 2}"), writes=(f"osb{th % 2}", f"psWe{pw}"))
                K.op("dve", (lambda o2=(th % 2): NCV.tensor_tensor(
                    out=osb[o2][:], in0=osb[o2][:], in1=ffac[:], op=ALU.add)),
                    reads=("ffac", f"osb{th % 2}"), writes=(f"osb{th % 2}", "ffac_c"))
                K.dma("act", (lambda o3=of, t2=th, o2=(th % 2): NCS.dma_start(
                    out=d["yout"][o3, :, 512 * t2:512 * (t2 + 1)], in_=osb[o2][:])),
                    reads=(f"osb{th % 2}",), writes=(f"osbc{th % 2}",))

    K.barrier()
    final_qact = K.cnt["q_act"]
    K.ops["pool"].append(([("q_act", final_qact)], None, None))
    psD.close()
    esD.close()

    return _emit()


def _host_inputs(L0_attn_hidden_states, L0_attn_Wqkv, L0_attn_Wo, L0_ffn_W_res,
                 L0_ffn_W_pre, L0_ffn_W_post, L0_ffn_H_res_base, L0_ffn_H_pre_base,
                 L0_ffn_H_post_base, L0_ffn_gate, L0_ffn_W_branch):
    x = np.asarray(L0_attn_hidden_states, dtype=np.float32)
    wqkv = np.asarray(L0_attn_Wqkv, dtype=np.float32)
    wo = np.asarray(L0_attn_Wo, dtype=np.float32)

    def lay_blocks(w, nblocks):
        return np.ascontiguousarray(
            w.reshape(32, 128, nblocks, 128).transpose(2, 1, 0, 3).reshape(nblocks, 128, 32 * 128))

    wqkv_l = lay_blocks(wqkv, 34)
    wo_l = lay_blocks(wo, 32)
    wbr = np.asarray(L0_ffn_W_branch, dtype=np.float32)
    wbr_l = np.ascontiguousarray(
        wbr.reshape(8, 128, 8, 128).transpose(1, 2, 0, 3).reshape(128, 8192))
    w24 = np.concatenate([np.asarray(L0_ffn_W_res, dtype=np.float32),
                          np.asarray(L0_ffn_W_pre, dtype=np.float32),
                          np.asarray(L0_ffn_W_post, dtype=np.float32)], axis=0)
    w24_l = np.ascontiguousarray(w24.T.reshape(32, 128, 24).transpose(1, 0, 2).reshape(128, 32 * 24))
    bases = np.concatenate([np.asarray(L0_ffn_H_res_base, dtype=np.float32).reshape(-1),
                            np.asarray(L0_ffn_H_pre_base, dtype=np.float32),
                            np.asarray(L0_ffn_H_post_base, dtype=np.float32)]).reshape(1, 24)
    gate = np.asarray(L0_ffn_gate, dtype=np.float32).reshape(1, 1)

    kf = np.arange(512, dtype=np.float32)[None, :]
    qp = np.arange(128, dtype=np.float32)[:, None]
    dplain = np.ascontiguousarray(np.broadcast_to(kf, (128, 512)).astype(np.float32))
    i128h = np.eye(128).astype(BF)
    i128f = np.eye(128).astype(np.float32)
    ones1h = np.ones((1, 128), dtype=BF)
    ones1f = np.ones((1, 128), dtype=np.float32)
    onescol_h = np.ones((128, 1), dtype=BF)

    in_maps = []
    for c in range(8):
        b, r = c // 4, c % 4
        rows = np.concatenate([np.arange(512 * e + 128 * r, 512 * e + 128 * r + 128)
                               for e in range(8)])
        xq = np.ascontiguousarray(x[b][rows])
        xqt = np.ascontiguousarray(xq.T)
        dmask = np.where(kf <= 128 * r + qp, kf, np.float32(NEG_BIG)).astype(np.float32)
        qcol = (qp + 128 * r).astype(np.float32)
        in_maps.append(dict(
            xkv=np.ascontiguousarray(x[b]), xq=xq, xqt=xqt, wqkv=wqkv_l, wo=wo_l, wbr=wbr_l,
            w24=w24_l, bases24=bases, gate=gate, dplain=dplain,
            dmask=np.ascontiguousarray(dmask), qcol=np.ascontiguousarray(qcol),
            i128h=i128h, i128f=i128f, ones1h=ones1h, ones1f=ones1f,
            onescol_h=onescol_h))
    return in_maps


def kernel(**inputs):
    in_maps = _host_inputs(**inputs)
    nc = build_program()
    res = run_bass_kernel_spmd(nc, in_maps, core_ids=list(range(8)))
    out = np.empty((B, S, H), dtype=np.float32)
    for c in range(8):
        b, r = c // 4, c % 4
        y = np.asarray(res.results[c]["yout"])
        rows = y.transpose(2, 0, 1).reshape(1024, 4096)
        for e in range(8):
            out[b, 512 * e + 128 * r:512 * e + 128 * r + 128, :] = rows[128 * e:128 * (e + 1)]
    return out


# version 03
# speedup vs baseline: 1.0605x; 1.0605x over previous
import math
from contextlib import ExitStack

import numpy as np
import ml_dtypes

import concourse.bass as bass
import concourse.mybir as mybir
from concourse.bass_utils import run_bass_kernel_spmd

F32 = mybir.dt.float32
BF16 = mybir.dt.bfloat16
AF = mybir.ActivationFunctionType
ALU = mybir.AluOpType
AX = mybir.AxisListType
BF = ml_dtypes.bfloat16

B, S, H = 2, 4096, 4096
NH, HD = 32, 128
SCALE = float(HD) ** -0.5
LN_EPS = 1e-5
RMS_EPS = 1.1920929e-07
TAU_INV = float(1.0 / np.float32(0.1))
NEG_BIG = -1.0e30
SINK_ITERS = 50
NT_KV = S // 128
NT_Q = 8
NCH = H // 128


def _alibi_slopes(n):
    cp2 = 2 ** math.floor(math.log2(n))
    base = 2.0 ** (-(2.0 ** (-(math.log2(cp2) - 3))))
    slopes = base ** np.arange(1, cp2 + 1, dtype=np.float64)
    if cp2 != n:
        extra_base = 2.0 ** (-(2.0 ** (-(math.log2(2 * cp2) - 3))))
        rem = min(cp2, n - cp2)
        extra = extra_base ** np.arange(1, 1 + 2 * rem, 2, dtype=np.float64)
        slopes = np.concatenate([slopes, extra])
    return slopes.astype(np.float32)

SLOPES = _alibi_slopes(NH)


class Kb:
    ENGINES = ("pe", "act", "dve", "pool", "sync")
    ESEM = {"pe": "s_pe", "act": "s_act", "dve": "s_dve", "pool": "s_pool"}
    QSEM = {"sync": "q_sync", "pool": "q_pool", "act": "q_act", "dve": "q_dve"}

    def __init__(self):
        self.ops = {e: [] for e in self.ENGINES}
        self.cnt = {}
        for s in list(self.ESEM.values()) + list(self.QSEM.values()):
            self.cnt[s] = 0
        self.res = {}
        self.waited = {e: {} for e in self.ENGINES}

    def _need(self, eng, waits, token):
        sem, val = token
        if eng == "pe" and sem == self.ESEM.get("pe"):
            return
        if self.waited[eng].get(sem, 0) >= val:
            return
        self.waited[eng][sem] = val
        waits[sem] = max(waits.get(sem, 0), val)

    def _emit(self, eng, fn, reads, writes, inc_sem, inc_n):
        waits = {}
        for r in reads:
            ent = self.res.get(r)
            if ent and ent[0]:
                self._need(eng, waits, ent[0])
        for w in writes:
            ent = self.res.get(w)
            if ent:
                if ent[0]:
                    self._need(eng, waits, ent[0])
                for sem, val in ent[1].items():
                    self._need(eng, waits, (sem, val))
        self.cnt[inc_sem] += inc_n
        token = (inc_sem, self.cnt[inc_sem])
        for r in reads:
            ent = self.res.setdefault(r, [None, {}])
            ent[1][token[0]] = token[1]
        for w in writes:
            self.res[w] = [token, {}]
        self.ops[eng].append((sorted(waits.items()), fn, (inc_sem, inc_n)))
        return token

    def op(self, eng, fn, reads=(), writes=()):
        return self._emit(eng, fn, reads, writes, self.ESEM[eng], 1)

    def dma(self, eng, fn, reads=(), writes=()):
        return self._emit(eng, fn, reads, writes, self.QSEM[eng], 16)

    def barrier(self):
        snap = dict(self.cnt)
        for eng in self.ENGINES:
            waits = {}
            for sem, val in snap.items():
                if val == 0:
                    continue
                if eng in self.ESEM and self.ESEM[eng] == sem:
                    continue
                if self.waited[eng].get(sem, 0) >= val:
                    continue
                self.waited[eng][sem] = val
                waits[sem] = val
            if waits:
                self.ops[eng].append((sorted(waits.items()), None, None))


def build_program(upto="FULL"):
    nc = bass.Bass("TRN2", target_bir_lowering=False)
    K = Kb()
    sems = {}
    _emit_state = {}

    def _emit():
        def mk(eng_name, nce):
            def body(_):
                for waits, fn, inc in K.ops[eng_name]:
                    for s, v in waits:
                        nce.wait_ge(sems[s], v)
                    if fn is not None:
                        ins = fn()
                        if inc is not None:
                            ins.then_inc(sems[inc[0]], inc[1])
            return body
        with nc.Block() as block:
            block.tensor(mk("pe", _emit_state["NCT"]))
            block.scalar(mk("act", _emit_state["NCS"]))
            block.vector(mk("dve", _emit_state["NCV"]))
            block.gpsimd(mk("pool", _emit_state["NCP"]))
            block.sync(mk("sync", _emit_state["NCY"]))
        return nc

    def _dump(pairs):
        NCP = _emit_state["NCP"]
        K.barrier()
        for nm, handle in pairs:
            shp = list(handle[:].shape)
            dt_ = handle[:].dtype
            dout = nc.dram_tensor(nm, shp, dt_, kind="ExternalOutput").ap()
            K.dma("pool", (lambda d2=dout, h2=handle: NCP.dma_start(out=d2, in_=h2[:])),
                  reads=(), writes=(nm + "_out",))
        K.ops["pool"].append(([("q_pool", K.cnt["q_pool"])], None, None))
        return _emit()

    d = {}
    d["xkv"] = nc.dram_tensor("xkv", [S, H], F32, kind="ExternalInput").ap()
    d["xq"] = nc.dram_tensor("xq", [1024, H], F32, kind="ExternalInput").ap()
    d["xqt"] = nc.dram_tensor("xqt", [H, 1024], F32, kind="ExternalInput").ap()
    d["wqkv"] = nc.dram_tensor("wqkv", [34, 128, 32 * 128], F32, kind="ExternalInput").ap()
    d["wo"] = nc.dram_tensor("wo", [32, 128, 32 * 128], F32, kind="ExternalInput").ap()
    d["wbr"] = nc.dram_tensor("wbr", [128, 8192], F32, kind="ExternalInput").ap()
    d["w24"] = nc.dram_tensor("w24", [128, 32 * 24], F32, kind="ExternalInput").ap()
    d["bases24"] = nc.dram_tensor("bases24", [1, 24], F32, kind="ExternalInput").ap()
    d["gate"] = nc.dram_tensor("gate", [1, 1], F32, kind="ExternalInput").ap()
    d["dplain"] = nc.dram_tensor("dplain", [128, 512], F32, kind="ExternalInput").ap()
    d["dmask"] = nc.dram_tensor("dmask", [128, 512], F32, kind="ExternalInput").ap()
    d["qcol"] = nc.dram_tensor("qcol", [128, 1], F32, kind="ExternalInput").ap()
    d["i128h"] = nc.dram_tensor("i128h", [128, 128], BF16, kind="ExternalInput").ap()
    d["i128f"] = nc.dram_tensor("i128f", [128, 128], F32, kind="ExternalInput").ap()
    d["ones1h"] = nc.dram_tensor("ones1h", [1, 128], BF16, kind="ExternalInput").ap()
    d["ones1f"] = nc.dram_tensor("ones1f", [1, 128], F32, kind="ExternalInput").ap()
    d["onescol_h"] = nc.dram_tensor("onescol_h", [128, 1], BF16, kind="ExternalInput").ap()
    d["yout"] = nc.dram_tensor("yout", [32, 128, 1024], F32, kind="ExternalOutput").ap()
    hscr = nc.dram_tensor("hscr", [24, 1024], BF16, kind="Internal").ap()

    es = ExitStack()
    sb = lambda name, shape, dt: es.enter_context(nc.sbuf_tensor(name, shape, dt))

    xnt = sb("xnt", [128, NCH * 1024], BF16)
    esKV = ExitStack()
    kt = esKV.enter_context(nc.sbuf_tensor("kt", [128, S], BF16, side="right"))
    vt = esKV.enter_context(nc.sbuf_tensor("vt", [128, S], BF16, side="right"))
    dplain = sb("dplain_s", [128, 512], F32)
    dmask = sb("dmask_s", [128, 512], F32)
    qcol = sb("qcol_s", [128, 1], F32)
    i128h = sb("i128h_s", [128, 128], BF16)
    i128f = sb("i128f_s", [128, 128], F32)
    ones1h = sb("ones1h_s", [1, 128], BF16)
    ones1f = sb("ones1f_s", [1, 128], F32)
    onescol_h = sb("onescol_h_s", [128, 1], BF16)
    rstdF = sb("rstdF", [128, 8], F32)
    lgraw = sb("lgraw", [24, 1024], F32)
    negc = sb("negc", [128, 1], F32)
    bases_row = sb("bases_row", [1, 24], F32)
    gate_s = sb("gate_s", [1, 1], F32)

    for s in list(Kb.ESEM.values()) + list(Kb.QSEM.values()):
        sems[s] = es.enter_context(nc.semaphore(s))

    NCV = nc.vector
    NCS = nc.scalar
    NCT = nc.tensor
    NCP = nc.gpsimd
    NCY = nc.sync
    _emit_state.update(NCV=NCV, NCS=NCS, NCT=NCT, NCP=NCP, NCY=NCY)

    for name, dst in [("dplain", dplain), ("dmask", dmask), ("qcol", qcol),
                      ("i128h", i128h), ("i128f", i128f), ("ones1h", ones1h),
                      ("ones1f", ones1f), ("onescol_h", onescol_h),
                      ("bases24", bases_row), ("gate", gate_s)]:
        src = d[name]
        K.dma("pool", (lambda dd=dst, ss=src: NCP.dma_start(out=dd[:], in_=ss)),
              reads=(), writes=(name + "_r",))

    esA = ExitStack()
    sbA = lambda name, shape, dt: esA.enter_context(nc.sbuf_tensor(name, shape, dt))
    xbuf = [sbA(f"xbuf{p}", [128, H], F32) for p in range(2)]
    xnbuf = [sbA(f"xnbuf{p}", [128, H], BF16) for p in range(2)]
    stage = [sbA(f"stage{p}", [128, H], BF16) for p in range(2)]
    junk = sbA("junk", [128, H], BF16)
    wkvraw = sbA("wkvraw", [128, 4096], F32)
    wkvb = sbA("wkvb", [128, 2 * 4096], BF16)
    w24raw = sbA("w24raw", [128, 32 * 24], F32)
    w24b = sbA("w24b", [128, 32 * 24], BF16)
    s1 = sbA("s1", [128, 1], F32)
    s2 = sbA("s2", [128, 1], F32)
    mn = sbA("mn", [128, 1], F32)
    mn2 = sbA("mn2", [128, 1], F32)
    var = sbA("var", [128, 1], F32)
    sd = sbA("sd", [128, 1], F32)
    rstd = sbA("rstd", [128, 1], F32)
    negmr = sbA("negmr", [128, 1], F32)
    s2f = sbA("s2f", [128, 1], F32)
    sdF = sbA("sdF", [128, 1], F32)
    psA = ExitStack()
    psT = [psA.enter_context(nc.psum_tensor(f"psT{p}", [128, 512], BF16)) for p in range(2)]
    psK = [psA.enter_context(nc.psum_tensor(f"psK{p}", [128, 128], F32)) for p in range(2)]
    psV = [psA.enter_context(nc.psum_tensor(f"psV{p}", [128, 128], F32)) for p in range(2)]
    psL = psA.enter_context(nc.psum_tensor("psL", [24, 128], F32))

    for bi, blk in enumerate((32, 33)):
        K.dma("sync", (lambda b2=blk: NCY.dma_start(out=wkvraw[:], in_=d["wqkv"][b2])),
              reads=("wkvb_c",), writes=("wkvraw",))
        K.op("pool", (lambda b3=bi: NCP.tensor_copy(out=wkvb[:, b3 * 4096:(b3 + 1) * 4096], in_=wkvraw[:])),
             reads=("wkvraw",), writes=("wkvb", "wkvb_c"))
    K.dma("sync", (lambda: NCY.dma_start(out=w24raw[:], in_=d["w24"])),
          reads=(), writes=("w24raw",))
    K.op("pool", (lambda: NCP.tensor_copy(out=w24b[:], in_=w24raw[:])),
         reads=("w24raw",), writes=("w24b",))

    def ln_tile(src_ap, p, tag):
        xb, xn = xbuf[p], xnbuf[p]
        rx, rn = f"xbuf{p}", f"xnbuf{p}"
        K.dma("sync", (lambda: NCY.dma_start(out=xb[:], in_=src_ap)),
              reads=(rn,), writes=(rx,))
        K.op("dve", (lambda: NCV.reduce_sum(out=s1[:], in_=xb[:], axis=AX.X)),
             reads=(rx,), writes=("s1",))
        K.op("act", (lambda: NCS.activation(out=junk[:], in_=xb[:], func=AF.Square,
                                            bias=0.0, scale=1.0, accum_out=s2[:])),
             reads=(rx,), writes=("junk", "s2"))
        K.op("dve", (lambda: NCV.tensor_scalar_mul(out=mn[:], in0=s1[:], scalar1=1.0 / H)),
             reads=("s1",), writes=("mn",))
        K.op("dve", (lambda: NCV.tensor_mul(out=mn2[:], in0=mn[:], in1=mn[:])),
             reads=("mn",), writes=("mn2",))
        K.op("dve", (lambda: NCV.scalar_tensor_tensor(out=var[:], in0=s2[:], scalar=1.0 / H,
                                                      in1=mn2[:], op0=ALU.mult, op1=ALU.subtract)),
             reads=("s2", "mn2"), writes=("var",))
        K.op("dve", (lambda: NCV.tensor_scalar_add(out=var[:], in0=var[:], scalar1=LN_EPS)),
             reads=("var",), writes=("var",))
        K.op("act", (lambda: NCS.activation(out=sd[:], in_=var[:], func=AF.Sqrt, bias=0.0, scale=1.0)),
             reads=("var",), writes=("sd",))
        K.op("dve", (lambda: NCV.reciprocal(out=rstd[:], in_=sd[:])),
             reads=("sd",), writes=("rstd",))
        K.op("dve", (lambda: NCV.scalar_tensor_tensor(out=negmr[:], in0=mn[:], scalar=-1.0,
                                                      in1=rstd[:], op0=ALU.mult, op1=ALU.mult)),
             reads=("mn", "rstd"), writes=("negmr",))
        K.op("act", (lambda: NCS.activation(out=xn[:], in_=xb[:], func=AF.Identity,
                                            bias=negmr[:, 0:1], scale=rstd[:, 0:1])),
             reads=(rx, "rstd", "negmr"), writes=(rn,))
        return xn, rn

    for i in range(NT_KV):
        p = i % 2
        xn, rn = ln_tile(d["xkv"][128 * i:128 * (i + 1), :], p, f"a1_{i}")
        st, rs = stage[p], f"stage{p}"
        for g in range(8):
            tp = g % 2
            for u in range(4):
                j = 4 * g + u
                K.op("pe", (lambda xn2=xn, j2=j, t2=tp: NCT.transpose(
                    out=psT[t2][:, 128 * (j2 % 4):128 * (j2 % 4 + 1)],
                    in_=xn2[:, 128 * j2:128 * (j2 + 1)], identity=i128h[:])),
                    reads=(rn, "i128h_r", f"psTe{tp}"), writes=(f"psT{tp}",))
            K.op("act", (lambda s3=st, g2=g, t2=tp: NCS.copy(
                out=s3[:, 512 * g2:512 * (g2 + 1)], in_=psT[t2][:])),
                reads=(f"psT{tp}",), writes=(rs, f"psTe{tp}"))
        kp = i % 2
        for c in range(32):
            K.op("pe", (lambda c2=c, s3=st, k2=kp: NCT.matmul(
                out=psK[k2][:], lhsT=wkvb[:, 128 * c2:128 * (c2 + 1)],
                rhs=s3[:, 128 * c2:128 * (c2 + 1)],
                start=(c2 == 0), stop=(c2 == 31))),
                reads=(rs, "wkvb", f"psKe{kp}"), writes=(f"psK{kp}",))
        K.op("act", (lambda i2=i, k2=kp: NCS.copy(out=kt[:, 128 * i2:128 * (i2 + 1)], in_=psK[k2][:])),
             reads=(f"psK{kp}",), writes=("kt", f"psKe{kp}"))
        for c in range(32):
            K.op("pe", (lambda c2=c, s3=st, k2=kp: NCT.matmul(
                out=psV[k2][:], lhsT=s3[:, 128 * c2:128 * (c2 + 1)],
                rhs=wkvb[:, 4096 + 128 * c2:4096 + 128 * (c2 + 1)],
                start=(c2 == 0), stop=(c2 == 31))),
                reads=(rs, "wkvb", f"psVe{kp}"), writes=(f"psV{kp}",))
        K.op("act", (lambda i2=i, k2=kp: NCS.copy(out=vt[:, 128 * i2:128 * (i2 + 1)], in_=psV[k2][:])),
             reads=(f"psV{kp}",), writes=("vt", f"psVe{kp}"))

    for e in range(NT_Q):
        p = e % 2
        xn, rn = ln_tile(d["xq"][128 * e:128 * (e + 1), :], p, f"a2_{e}")
        for g in range(8):
            tp = g % 2
            for u in range(4):
                j = 4 * g + u
                K.op("pe", (lambda xn2=xn, j2=j, t2=tp: NCT.transpose(
                    out=psT[t2][:, 128 * (j2 % 4):128 * (j2 % 4 + 1)],
                    in_=xn2[:, 128 * j2:128 * (j2 + 1)], identity=i128h[:])),
                    reads=(rn, "i128h_r", f"psTe{tp}"), writes=(f"psT{tp}",))
            for u in range(4):
                j = 4 * g + u
                K.op("act", (lambda j2=j, e2=e, t2=tp, u2=u: NCS.copy(
                    out=xnt[:, 1024 * j2 + 128 * e2:1024 * j2 + 128 * (e2 + 1)],
                    in_=psT[t2][:, 128 * u2:128 * (u2 + 1)])),
                    reads=(f"psT{tp}",),
                    writes=("xnt", f"psTe{tp}") if u == 3 else ("xnt",))
        K.op("act", (lambda xn2=xn: NCS.activation(out=junk[:], in_=xn2[:], func=AF.Square,
                                                   bias=0.0, scale=1.0, accum_out=s2f[:])),
             reads=(rn,), writes=("junk", "s2f"))
        K.op("dve", (lambda: NCV.tensor_scalar(out=s2f[:], in0=s2f[:], scalar1=1.0 / H,
                                               scalar2=RMS_EPS, op0=ALU.mult, op1=ALU.add)),
             reads=("s2f",), writes=("s2f",))
        K.op("act", (lambda: NCS.activation(out=sdF[:], in_=s2f[:], func=AF.Sqrt, bias=0.0, scale=1.0)),
             reads=("s2f",), writes=("sdF",))
        K.op("dve", (lambda e2=e: NCV.reciprocal(out=rstdF[:, e2:e2 + 1], in_=sdF[:])),
             reads=("sdF",), writes=("rstdF",))
        for c in range(32):
            K.op("pe", (lambda c2=c, e2=e: NCT.matmul(
                out=psL[:], lhsT=w24b[:, 24 * c2:24 * (c2 + 1)],
                rhs=xnt[:, 1024 * c2 + 128 * e2:1024 * c2 + 128 * (e2 + 1)],
                start=(c2 == 0), stop=(c2 == 31))),
                reads=("xnt", "w24b", "psLe"), writes=("psL",))
        K.op("act", (lambda e2=e: NCS.copy(out=lgraw[:, 128 * e2:128 * (e2 + 1)], in_=psL[:])),
             reads=("psL",), writes=("lgraw", "psLe"))

    if upto == "A":
        return _dump([("d_xnt", xnt), ("d_kt", kt), ("d_vt", vt),
                      ("d_lgraw", lgraw), ("d_rstdF", rstdF)])
    K.barrier()
    psA.close()
    esA.close()

    qt = sb("qt", [128, NH * 1024], BF16)
    esB = ExitStack()
    sbB = lambda name, shape, dt: esB.enter_context(nc.sbuf_tensor(name, shape, dt))
    wqraw = sbB("wqraw", [128, 4096], F32)
    wqb = [sbB(f"wqb{p}", [128, 4096], BF16) for p in range(2)]
    qsq = sbB("qsq", [128, 1024], BF16)
    q2row = sbB("q2row", [1, 1024], F32)
    q2cols = sbB("q2cols", [1, 40], F32)
    tiny = sbB("tiny", [1, 4], F32)
    psB = ExitStack()
    psQ = [psB.enter_context(nc.psum_tensor(f"psQ{p}", [128, 512], F32)) for p in range(2)]
    ps12 = psB.enter_context(nc.psum_tensor("ps12", [1, 512], F32))
    psC1 = psB.enter_context(nc.psum_tensor("psC1", [128, 1], F32))

    for h in range(NH):
        K.dma("sync", (lambda h2=h: NCY.dma_start(out=wqraw[:], in_=d["wqkv"][h2])),
              reads=("wqraw_c",), writes=("wqraw",))
        wp = h % 2
        K.op("pool", (lambda w2=wp: NCP.tensor_copy(out=wqb[w2][:], in_=wqraw[:])),
             reads=("wqraw",), writes=(f"wqb{wp}", "wqraw_c"))
        for th in range(2):
            qp_ = (2 * h + th) % 2
            for c in range(32):
                K.op("pe", (lambda c2=c, t2=th, q2=qp_, w2=wp: NCT.matmul(
                    out=psQ[q2][:], lhsT=wqb[w2][:, 128 * c2:128 * (c2 + 1)],
                    rhs=xnt[:, 1024 * c2 + 512 * t2:1024 * c2 + 512 * (t2 + 1)],
                    start=(c2 == 0), stop=(c2 == 31))),
                    reads=(f"wqb{wp}", "xnt", f"psQe{qp_}"), writes=(f"psQ{qp_}",))
            K.op("act", (lambda h2=h, t2=th, q2=qp_: NCS.copy(
                out=qt[:, 1024 * h2 + 512 * t2:1024 * h2 + 512 * (t2 + 1)], in_=psQ[q2][:])),
                reads=(f"psQ{qp_}",), writes=("qt", f"psQe{qp_}"))
        K.op("act", (lambda h2=h: NCS.activation(out=qsq[:], in_=qt[:, 1024 * h2:1024 * (h2 + 1)],
                                                 func=AF.Square, bias=0.0, scale=1.0)),
             reads=("qt",), writes=("qsq",))
        for th in range(2):
            K.op("pe", (lambda t2=th: NCT.matmul(
                out=ps12[:], lhsT=onescol_h[:], rhs=qsq[:, 512 * t2:512 * (t2 + 1)],
                start=True, stop=True)),
                reads=("qsq", "onescol_h_r", "ps12e"), writes=("ps12",))
            K.op("act", (lambda t2=th: NCS.copy(out=q2row[:, 512 * t2:512 * (t2 + 1)], in_=ps12[:])),
                 reads=("ps12",), writes=("q2row", "ps12e"))
        K.op("dve", (lambda h2=h: NCV.reduce_max(out=q2cols[:, h2:h2 + 1], in_=q2row[:], axis=AX.X)),
             reads=("q2row",), writes=("q2cols",))
    for g in range(4):
        K.op("act", (lambda g2=g: NCS.activation(out=qsq[:], in_=kt[:, 1024 * g2:1024 * (g2 + 1)],
                                                 func=AF.Square, bias=0.0, scale=1.0)),
             reads=("kt", "qsq_c"), writes=("qsq",))
        for th in range(2):
            K.op("pe", (lambda t2=th: NCT.matmul(
                out=ps12[:], lhsT=onescol_h[:], rhs=qsq[:, 512 * t2:512 * (t2 + 1)],
                start=True, stop=True)),
                reads=("qsq", "onescol_h_r", "ps12e"), writes=("ps12", "qsq_c"))
            K.op("act", (lambda t2=th: NCS.copy(out=q2row[:, 512 * t2:512 * (t2 + 1)], in_=ps12[:])),
                 reads=("ps12",), writes=("q2row", "ps12e"))
        K.op("dve", (lambda g2=g: NCV.reduce_max(out=q2cols[:, 32 + g2:33 + g2], in_=q2row[:], axis=AX.X)),
             reads=("q2row",), writes=("q2cols",))
    K.op("dve", (lambda: NCV.reduce_max(out=tiny[:, 0:1], in_=q2cols[:, 0:32], axis=AX.X)),
         reads=("q2cols",), writes=("t_q2m",))
    K.op("dve", (lambda: NCV.reduce_max(out=tiny[:, 1:2], in_=q2cols[:, 32:36], axis=AX.X)),
         reads=("q2cols",), writes=("t_k2m",))
    K.op("dve", (lambda: NCV.tensor_mul(out=tiny[:, 2:3], in0=tiny[:, 0:1], in1=tiny[:, 1:2])),
         reads=("t_q2m", "t_k2m"), writes=("t_prod",))
    K.op("act", (lambda: NCS.activation(out=tiny[:, 3:4], in_=tiny[:, 2:3], func=AF.Sqrt,
                                        bias=0.0, scale=1.0)),
         reads=("t_prod",), writes=("t_sq",))
    K.op("dve", (lambda: NCV.tensor_scalar(out=tiny[:, 2:3], in0=tiny[:, 3:4], scalar1=SCALE,
                                           scalar2=3.0, op0=ALU.mult, op1=ALU.add)),
         reads=("t_sq",), writes=("t_C",))
    K.op("pe", (lambda: NCT.matmul(out=psC1[:], lhsT=ones1f[:], rhs=tiny[:, 2:3],
                                   start=True, stop=True)),
         reads=("t_C", "ones1f_r"), writes=("psC1",))
    K.op("act", (lambda: NCS.activation(out=negc[:], in_=psC1[:], func=AF.Copy, bias=0.0, scale=-1.0)),
         reads=("psC1",), writes=("negc",))

    if upto == "B":
        return _dump([("d_qt", qt), ("d_negc", negc), ("d_lgraw", lgraw)])
    K.barrier()
    psB.close()
    esB.close()

    esC = ExitStack()
    sbC = lambda name, shape, dt: esC.enter_context(nc.sbuf_tensor(name, shape, dt))
    tsc = [sbC(f"tsc{p}", [128, 512], F32) for p in range(2)]
    pbuf = [sbC(f"pbuf{p}", [128, 512], BF16) for p in range(2)]
    pts = [sbC(f"pts{p}", [128, 512], BF16) for p in range(2)]
    rows = sbC("rows", [128, 8], F32)
    rs_ = sbC("rs", [128, 1], F32)
    rrec = sbC("rrec", [128, 1], F32)
    rect = sbC("rect", [1, 128], F32)
    rtile = sbC("rtile", [128, 128], F32)
    basehe = sbC("basehe", [128, 1], F32)
    tbcol = sbC("tbcol", [128, 1], F32)
    biast = [sbC(f"biast{p}", [128, 1], F32) for p in range(2)]
    psC = ExitStack()
    psS = [psC.enter_context(nc.psum_tensor(f"psS{p}", [128, 512], F32)) for p in range(2)]
    psPT = [psC.enter_context(nc.psum_tensor(f"psPT{p}", [128, 512], BF16)) for p in range(2)]
    psO = [psC.enter_context(nc.psum_tensor(f"psO{p}", [128, 128], F32)) for p in range(2)]
    psR = psC.enter_context(nc.psum_tensor("psR", [128, 128], F32))
    psRT = psC.enter_context(nc.psum_tensor("psRT", [1, 128], F32))

    kbctr = 0
    for h in range(NH):
        slope = float(SLOPES[h])
        sprime = slope / SCALE
        for e in range(NT_Q):
            oe = (h * NT_Q + e) % 2
            qlo = 1024 * h + 128 * e
            K.op("dve", (lambda e2=e, sl=slope: NCV.tensor_scalar(
                out=tbcol[:], in0=qcol[:], scalar1=float(512 * e2), scalar2=-sl,
                op0=ALU.add, op1=ALU.mult)),
                reads=("qcol_r",), writes=("tbcol",))
            K.op("dve", (lambda: NCV.scalar_tensor_tensor(
                out=basehe[:], in0=tbcol[:], scalar=1.0, in1=negc[:],
                op0=ALU.mult, op1=ALU.add)),
                reads=("tbcol", "negc"), writes=("basehe",))
            for kb in range(e + 1):
                pp = kbctr % 2
                kbctr += 1
                K.op("pe", (lambda q2=qlo, k2=kb, p2=pp: NCT.matmul(
                    out=psS[p2][:], lhsT=qt[:, q2:q2 + 128],
                    rhs=kt[:, 512 * k2:512 * (k2 + 1)], start=True, stop=True)),
                    reads=("qt", "kt", f"psSe{pp}"), writes=(f"psS{pp}",))
                K.op("dve", (lambda k2=kb, p2=pp, sl=slope: NCV.tensor_scalar_add(
                    out=biast[p2][:], in0=basehe[:], scalar1=float(sl * 512 * k2))),
                    reads=("basehe",), writes=(f"biast{pp}",))
                dt_ = dmask if kb == e else dplain
                K.op("dve", (lambda d2=dt_, p2=pp, sp=sprime: NCV.scalar_tensor_tensor(
                    out=tsc[p2][:], in0=d2[:], scalar=sp, in1=psS[p2][:],
                    op0=ALU.mult, op1=ALU.add)),
                    reads=("dplain_r", "dmask_r", f"psS{pp}", f"tsce{pp}"),
                    writes=(f"tsc{pp}", f"psSe{pp}"))
                K.op("act", (lambda p2=pp, k2=kb: NCS.activation(
                    out=pbuf[p2][:], in_=tsc[p2][:], func=AF.Exp,
                    bias=biast[p2][:, 0:1], scale=SCALE,
                    accum_out=rows[:, k2:k2 + 1])),
                    reads=(f"tsc{pp}", f"biast{pp}", f"pbe{pp}", "rows_c"),
                    writes=(f"pbuf{pp}", "rows", f"tsce{pp}"))
                for u in range(4):
                    K.op("pe", (lambda p2=pp, u2=u: NCT.transpose(
                        out=psPT[p2][:, 128 * u2:128 * (u2 + 1)],
                        in_=pbuf[p2][:, 128 * u2:128 * (u2 + 1)], identity=i128h[:])),
                        reads=(f"pbuf{pp}", "i128h_r", f"psPTe{pp}"), writes=(f"psPT{pp}",))
                K.op("act", (lambda p2=pp: NCS.copy(out=pts[p2][:], in_=psPT[p2][:])),
                     reads=(f"psPT{pp}",), writes=(f"pts{pp}", f"psPTe{pp}", f"pbe{pp}"))
                for u in range(4):
                    kc = 4 * kb + u
                    K.op("pe", (lambda p2=pp, u2=u, k3=kc, o2=oe, st=(kb == 0 and u == 0),
                                sp2=(kb == e and u == 3): NCT.matmul(
                        out=psO[o2][:], lhsT=vt[:, 128 * k3:128 * (k3 + 1)],
                        rhs=pts[p2][:, 128 * u2:128 * (u2 + 1)], start=st, stop=sp2)),
                        reads=(f"pts{pp}", "vt", f"psOe{oe}"), writes=(f"psO{oe}",))
            K.op("dve", (lambda e2=e: NCV.reduce_sum(out=rs_[:], in_=rows[:, 0:e2 + 1], axis=AX.X)),
                 reads=("rows",), writes=("rs", "rows_c"))
            K.op("dve", (lambda: NCV.reciprocal(out=rrec[:], in_=rs_[:])),
                 reads=("rs",), writes=("rrec",))
            K.op("pe", (lambda: NCT.transpose(out=psRT[:], in_=rrec[:], identity=i128f[:])),
                 reads=("rrec", "i128f_r", "psRTe"), writes=("psRT",))
            K.op("act", (lambda: NCS.copy(out=rect[:], in_=psRT[:])),
                 reads=("psRT",), writes=("rect", "psRTe"))
            K.op("pe", (lambda: NCT.matmul(out=psR[:], lhsT=ones1f[:], rhs=rect[:],
                                           start=True, stop=True)),
                 reads=("rect", "ones1f_r", "psRe"), writes=("psR",))
            K.op("act", (lambda: NCS.copy(out=rtile[:], in_=psR[:])),
                 reads=("psR",), writes=("rtile", "psRe"))
            K.op("dve", (lambda q2=qlo, o2=oe: NCV.tensor_tensor(
                out=qt[:, q2:q2 + 128], in0=psO[o2][:], in1=rtile[:], op=ALU.mult)),
                reads=(f"psO{oe}", "rtile"), writes=("qt", f"psOe{oe}"))

    if upto == "C":
        return _dump([("d_qt", qt)])
    K.barrier()
    psC.close()
    esC.close()
    esKV.close()

    esS = ExitStack()
    sbS = lambda name, shape, dt: esS.enter_context(nc.sbuf_tensor(name, shape, dt))
    lg = sbS("lg", [128, 192], F32)
    bb24 = sbS("bb24", [128, 24], F32)
    zres = sbS("zres", [128, 128], F32)
    asb = sbS("asb", [128, 128], F32)
    bsb = sbS("bsb", [128, 128], F32)
    esb = sbS("esb", [128, 128], F32)
    hsb = sbS("hsb", [128, 128], F32)
    msb = sbS("msb", [128, 32], F32)
    lsb = sbS("lsb", [128, 32], F32)
    ssb = sbS("ssb", [128, 32], F32)
    ubuf = sbS("ubuf", [128, 32], F32)
    vbuf = sbS("vbuf", [128, 32], F32)
    mp = sbS("mp", [128, 8], F32)
    ep = sbS("ep", [128, 32], F32)
    sp_ = sbS("sp", [128, 8], F32)
    hpre = sbS("hpre", [128, 32], F32)
    hpost = sbS("hpost", [128, 32], F32)
    gcol = sbS("gcol", [128, 1], F32)
    omg = sbS("omg", [128, 1], F32)
    hresT = sbS("hresT", [16, 1024], BF16)
    hpreT = sbS("hpreT", [4, 1024], BF16)
    hpostT = sbS("hpostT", [4, 1024], BF16)
    psSt = ExitStack()
    psLT = psSt.enter_context(nc.psum_tensor("psLT", [128, 24], F32))
    psHT = psSt.enter_context(nc.psum_tensor("psHT", [16, 128], F32))
    psG = psSt.enter_context(nc.psum_tensor("psG", [128, 1], F32))

    K.op("pe", (lambda: NCT.matmul(out=psLT[:], lhsT=ones1f[:], rhs=bases_row[:],
                                   start=True, stop=True)),
         reads=("bases24_r", "ones1f_r", "psLTe"), writes=("psLT",))
    K.op("act", (lambda: NCS.copy(out=bb24[:], in_=psLT[:])),
         reads=("psLT",), writes=("bb24", "psLTe"))
    for e in range(8):
        K.op("pe", (lambda e2=e: NCT.transpose(out=psLT[:], in_=lgraw[:, 128 * e2:128 * (e2 + 1)],
                                               identity=i128f[0:24, 0:24])),
             reads=("lgraw", "i128f_r", "psLTe"), writes=("psLT",))
        K.op("dve", (lambda e2=e: NCV.scalar_tensor_tensor(
            out=lg[:, 24 * e2:24 * (e2 + 1)], in0=psLT[:], scalar=rstdF[:, e2:e2 + 1],
            in1=bb24[:], op0=ALU.mult, op1=ALU.add)),
            reads=("psLT", "rstdF", "bb24"), writes=("lg", "psLTe"))
    lgv = lg.rearrange("p (a x) -> p a x", a=8, x=24)
    K.op("dve", (lambda: NCV.tensor_scalar_mul(
        out=zres.rearrange("p (a x) -> p a x", a=8, x=16),
        in0=lgv[:, :, 0:16], scalar1=TAU_INV)),
        reads=("lg",), writes=("zres",))
    K.op("dve", (lambda: NCV.memset(ubuf[:], 0.0)), reads=(), writes=("ubuf",))
    K.op("dve", (lambda: NCV.memset(vbuf[:], 0.0)), reads=(), writes=("vbuf",))

    z4 = zres.rearrange("p (a i j) -> p a i j", a=8, i=4, j=4)
    a4 = asb.rearrange("p (a i j) -> p a i j", a=8, i=4, j=4)
    b4 = bsb.rearrange("p (a i j) -> p a i j", a=8, i=4, j=4)
    e4 = esb.rearrange("p (a i j) -> p a i j", a=8, i=4, j=4)
    uv = ubuf.rearrange("p (a i) -> p a i", a=8, i=4)
    vv = vbuf.rearrange("p (a j) -> p a j", a=8, j=4)
    mv = msb.rearrange("p (a i) -> p a i", a=8, i=4)
    sv = ssb.rearrange("p (a i) -> p a i", a=8, i=4)
    vb_bc = vv.unsqueeze(2).broadcast_to([128, 8, 4, 4])
    ub_bc = uv.unsqueeze(3).broadcast_to([128, 8, 4, 4])
    mb_bc = mv.unsqueeze(3).broadcast_to([128, 8, 4, 4])
    m2_bc = mv.unsqueeze(2).broadcast_to([128, 8, 4, 4])
    a4_ji = asb.rearrange("p (a i j) -> p a j i", a=8, i=4, j=4)
    e4_ji = esb.rearrange("p (a i j) -> p a j i", a=8, i=4, j=4)

    for it in range(SINK_ITERS):
        K.op("dve", (lambda: NCV.tensor_tensor(out=a4, in0=z4, in1=vb_bc, op=ALU.subtract)),
             reads=("zres", "vbuf"), writes=("asb",))
        K.op("dve", (lambda: NCV.tensor_reduce(out=mv, in_=a4, axis=AX.X, op=ALU.max)),
             reads=("asb",), writes=("msb",))
        K.op("dve", (lambda: NCV.tensor_tensor(out=b4, in0=a4, in1=mb_bc, op=ALU.subtract)),
             reads=("asb", "msb"), writes=("bsb",))
        K.op("act", (lambda: NCS.activation(out=esb[:], in_=bsb[:], func=AF.Exp, bias=0.0, scale=1.0)),
             reads=("bsb",), writes=("esb",))
        K.op("dve", (lambda: NCV.tensor_reduce(out=sv, in_=e4, axis=AX.X, op=ALU.add)),
             reads=("esb",), writes=("ssb",))
        K.op("act", (lambda: NCS.activation(out=lsb[:], in_=ssb[:], func=AF.Ln, bias=0.0, scale=1.0)),
             reads=("ssb",), writes=("lsb",))
        K.op("dve", (lambda: NCV.tensor_add(out=ubuf[:], in0=msb[:], in1=lsb[:])),
             reads=("msb", "lsb"), writes=("ubuf",))
        K.op("dve", (lambda: NCV.tensor_tensor(out=a4, in0=z4, in1=ub_bc, op=ALU.subtract)),
             reads=("zres", "ubuf"), writes=("asb",))
        K.op("dve", (lambda: NCV.tensor_reduce(out=mv, in_=a4_ji, axis=AX.X, op=ALU.max)),
             reads=("asb",), writes=("msb",))
        K.op("dve", (lambda: NCV.tensor_tensor(out=b4, in0=a4, in1=m2_bc, op=ALU.subtract)),
             reads=("asb", "msb"), writes=("bsb",))
        K.op("act", (lambda: NCS.activation(out=esb[:], in_=bsb[:], func=AF.Exp, bias=0.0, scale=1.0)),
             reads=("bsb",), writes=("esb",))
        K.op("dve", (lambda: NCV.tensor_reduce(out=sv, in_=e4_ji, axis=AX.X, op=ALU.add)),
             reads=("esb",), writes=("ssb",))
        K.op("act", (lambda: NCS.activation(out=lsb[:], in_=ssb[:], func=AF.Ln, bias=0.0, scale=1.0)),
             reads=("ssb",), writes=("lsb",))
        K.op("dve", (lambda: NCV.tensor_add(out=vbuf[:], in0=msb[:], in1=lsb[:])),
             reads=("msb", "lsb"), writes=("vbuf",))
    K.op("dve", (lambda: NCV.tensor_tensor(out=a4, in0=z4, in1=ub_bc, op=ALU.subtract)),
         reads=("zres", "ubuf"), writes=("asb",))
    K.op("dve", (lambda: NCV.tensor_tensor(out=b4, in0=a4, in1=vb_bc, op=ALU.subtract)),
         reads=("asb", "vbuf"), writes=("bsb",))
    K.op("act", (lambda: NCS.activation(out=hsb[:], in_=bsb[:], func=AF.Exp, bias=0.0, scale=1.0)),
         reads=("bsb",), writes=("hsb",))
    K.op("act", (lambda: NCS.activation(out=gate_s[:], in_=gate_s[:], func=AF.Sigmoid, bias=0.0, scale=1.0)),
         reads=("gate_r",), writes=("gate_v",))
    K.op("pe", (lambda: NCT.matmul(out=psG[:], lhsT=ones1f[:], rhs=gate_s[:], start=True, stop=True)),
         reads=("gate_v", "ones1f_r"), writes=("psG",))
    K.op("act", (lambda: NCS.copy(out=gcol[:], in_=psG[:])),
         reads=("psG",), writes=("gcol",))
    K.op("dve", (lambda: NCV.tensor_scalar(out=omg[:], in0=gcol[:], scalar1=-1.0, scalar2=1.0,
                                           op0=ALU.mult, op1=ALU.add)),
         reads=("gcol",), writes=("omg",))
    K.op("dve", (lambda: NCV.tensor_scalar_mul(out=hsb[:], in0=hsb[:], scalar1=gcol[:, 0:1])),
         reads=("hsb", "gcol"), writes=("hsb",))
    hdg = hsb.rearrange("p (a x) -> p a x", a=8, x=16)
    for dgi in range(4):
        K.op("dve", (lambda d2=dgi: NCV.tensor_scalar_add(
            out=hdg[:, :, 5 * d2:5 * d2 + 1], in0=hdg[:, :, 5 * d2:5 * d2 + 1],
            scalar1=omg[:, 0:1])),
            reads=("hsb", "omg"), writes=("hsb",))
    for (name, off, out_sb) in (("hpre", 16, hpre), ("hpost", 20, hpost)):
        lgs = lgv[:, :, off:off + 4]
        mbc = mp[:].unsqueeze(2).broadcast_to([128, 8, 4])
        spc = sp_[:].unsqueeze(2).broadcast_to([128, 8, 4])
        epv = ep.rearrange("p (a n) -> p a n", a=8, n=4)
        K.op("dve", (lambda l2=lgs: NCV.tensor_reduce(out=mp[:], in_=l2, axis=AX.X, op=ALU.max)),
             reads=("lg",), writes=("mp",))
        K.op("dve", (lambda l2=lgs, m2=mbc: NCV.tensor_tensor(
            out=epv, in0=l2, in1=m2, op=ALU.subtract)),
            reads=("lg", "mp"), writes=("ep",))
        K.op("act", (lambda: NCS.activation(out=ep[:], in_=ep[:], func=AF.Exp, bias=0.0, scale=1.0)),
             reads=("ep",), writes=("ep",))
        K.op("dve", (lambda: NCV.tensor_reduce(out=sp_[:], in_=epv, axis=AX.X, op=ALU.add)),
             reads=("ep",), writes=("sp",))
        K.op("dve", (lambda: NCV.reciprocal(out=sp_[:], in_=sp_[:])),
             reads=("sp",), writes=("sp",))
        K.op("dve", (lambda o2=out_sb, s3=spc: NCV.tensor_tensor(
            out=o2.rearrange("p (a n) -> p a n", a=8, n=4),
            in0=epv, in1=s3, op=ALU.mult)),
            reads=("ep", "sp"), writes=(name,))
    for e in range(8):
        K.op("pe", (lambda e2=e: NCT.transpose(out=psHT[:], in_=hsb[:, 16 * e2:16 * (e2 + 1)],
                                               identity=i128f[:])),
             reads=("hsb", "i128f_r", "psHTe"), writes=("psHT",))
        K.op("dve", (lambda e2=e: NCV.tensor_copy(out=hresT[:, 128 * e2:128 * (e2 + 1)], in_=psHT[:])),
             reads=("psHT",), writes=("hresT", "psHTe"))
        for (src, dst, rn) in ((hpre, hpreT, "hpreT"), (hpost, hpostT, "hpostT")):
            K.op("pe", (lambda e2=e, s3=src: NCT.transpose(
                out=psHT[0:4, :], in_=s3[:, 4 * e2:4 * (e2 + 1)], identity=i128f[:])),
                reads=("hpre", "hpost", "i128f_r", "psHTe"), writes=("psHT",))
            K.op("dve", (lambda e2=e, d2=dst: NCV.tensor_copy(
                out=d2[:, 128 * e2:128 * (e2 + 1)], in_=psHT[0:4, :])),
                reads=("psHT",), writes=(rn, "psHTe"))
    K.dma("pool", (lambda: NCP.dma_start(out=hscr[0:16, :], in_=hresT[:])),
          reads=("hresT",), writes=("hscr_res",))
    K.dma("pool", (lambda: NCP.dma_start(out=hscr[16:20, :], in_=hpreT[:])),
          reads=("hpreT",), writes=("hscr_pre",))
    K.dma("pool", (lambda: NCP.dma_start(out=hscr[20:24, :], in_=hpostT[:])),
          reads=("hpostT",), writes=("hscr_post",))
    if upto == "S":
        return _dump([("d_hresT", hresT), ("d_hpreT", hpreT), ("d_hpostT", hpostT),
                      ("d_lg", lg)])
    K.barrier()
    psSt.close()
    esS.close()

    yt = sb("yt", [128, 8 * 1024], BF16)
    esD1 = ExitStack()
    sb1 = lambda name, shape, dt: esD1.enter_context(nc.sbuf_tensor(name, shape, dt))
    wbraw = sb1("wbraw", [128, 1024], F32)
    wbrb = sb1("wbrb", [128, 8192], BF16)
    rowsl = sb1("rowsl", [1, 1024], BF16)
    hpre_bc = sb1("hpre_bc", [128, 4096], BF16)
    fftmp = sb1("fftmp", [128, 512], F32)
    xpre = sb1("xpre", [128, 8192], BF16)
    psD1 = ExitStack()
    psY = [psD1.enter_context(nc.psum_tensor(f"psY{p}", [128, 512], F32)) for p in range(2)]
    psBC = [psD1.enter_context(nc.psum_tensor(f"psBC{p}", [128, 512], F32)) for p in range(2)]
    for hf in range(8):
        K.dma("sync", (lambda h2=hf: NCY.dma_start(out=wbraw[:], in_=d["wbr"][:, 1024 * h2:1024 * (h2 + 1)])),
              reads=("wbraw_c",), writes=("wbraw",))
        K.op("pool", (lambda h2=hf: NCP.tensor_copy(out=wbrb[:, 1024 * h2:1024 * (h2 + 1)], in_=wbraw[:])),
             reads=("wbraw",), writes=("wbrb", "wbraw_c"))
    for n in range(4):
        K.dma("pool", (lambda n2=n: NCP.dma_start(out=rowsl[:], in_=hscr[16 + n2:17 + n2, :])),
              reads=("hscr_pre", "rowsl"), writes=("rowsl",))
        for th in range(2):
            bp = th % 2
            K.op("pe", (lambda t2=th, b2=bp: NCT.matmul(
                out=psBC[b2][:], lhsT=ones1h[:], rhs=rowsl[:, 512 * t2:512 * (t2 + 1)],
                start=True, stop=True)),
                reads=("rowsl", "ones1h_r", f"psBCe{bp}"), writes=(f"psBC{bp}",))
            K.op("act", (lambda n2=n, t2=th, b2=bp: NCS.copy(
                out=hpre_bc[:, 1024 * n2 + 512 * t2:1024 * n2 + 512 * (t2 + 1)], in_=psBC[b2][:])),
                reads=(f"psBC{bp}",), writes=("hpre_bc", f"psBCe{bp}"))
    for cc in range(8):
        for th in range(2):
            for n in range(4):
                jg = n * 8 + cc
                xsl = (1024 * jg + 512 * th, 1024 * jg + 512 * (th + 1))
                hsl = (1024 * n + 512 * th, 1024 * n + 512 * (th + 1))
                osl = (1024 * cc + 512 * th, 1024 * cc + 512 * (th + 1))
                if n == 0:
                    K.op("pool", (lambda x2=xsl, h2=hsl, o2=osl: NCP.tensor_tensor(
                        out=xpre[:, o2[0]:o2[1]], in0=xnt[:, x2[0]:x2[1]],
                        in1=hpre_bc[:, h2[0]:h2[1]], op=ALU.mult)),
                        reads=("xnt", "hpre_bc"), writes=("xpre",))
                else:
                    K.op("pool", (lambda x2=xsl, h2=hsl: NCP.tensor_tensor(
                        out=fftmp[:], in0=xnt[:, x2[0]:x2[1]],
                        in1=hpre_bc[:, h2[0]:h2[1]], op=ALU.mult)),
                        reads=("xnt", "hpre_bc"), writes=("fftmp",))
                    K.op("pool", (lambda o2=osl: NCP.tensor_tensor(
                        out=xpre[:, o2[0]:o2[1]], in0=xpre[:, o2[0]:o2[1]],
                        in1=fftmp[:], op=ALU.add)),
                        reads=("fftmp", "xpre"), writes=("xpre",))
    for co in range(8):
        for th in range(2):
            yp = (2 * co + th) % 2
            for cc in range(8):
                K.op("pe", (lambda c2=cc, o2=co, t2=th, y2=yp: NCT.matmul(
                    out=psY[y2][:], lhsT=wbrb[:, 1024 * o2 + 128 * c2:1024 * o2 + 128 * (c2 + 1)],
                    rhs=xpre[:, 1024 * c2 + 512 * t2:1024 * c2 + 512 * (t2 + 1)],
                    start=(c2 == 0), stop=(c2 == 7))),
                    reads=("wbrb", "xpre", f"psYe{yp}"), writes=(f"psY{yp}",))
            K.op("act", (lambda o2=co, t2=th, y2=yp: NCS.copy(
                out=yt[:, 1024 * o2 + 512 * t2:1024 * o2 + 512 * (t2 + 1)], in_=psY[y2][:])),
                reads=(f"psY{yp}",), writes=("yt", f"psYe{yp}"))
    if upto == "D1":
        return _dump([("d_yt", yt), ("d_xpre", xpre), ("d_hpre_bc", hpre_bc)])
    K.barrier()
    psD1.close()
    esD1.close()

    esD = ExitStack()
    sbD = lambda name, shape, dt: esD.enter_context(nc.sbuf_tensor(name, shape, dt))
    woraw = sbD("woraw", [128, 1024], F32)
    wob = [sbD(f"wob{p}", [128, 4096], BF16) for p in range(2)]
    bc5 = sbD("bc5", [128, 5 * 1024], BF16)
    rslot = [sbD(f"rslot{j}", [1, 1024], BF16) for j in range(1)]
    xtrow = [sbD("xtrow0", [128, 1024], F32)] * 2
    osb = [sbD(f"osb{p}", [128, 1024], F32) for p in range(2)]
    fft2 = [sbD(f"fft2{p}", [128, 1024], F32) for p in range(2)]
    psD = ExitStack()
    psW = [psD.enter_context(nc.psum_tensor(f"psW{p}", [128, 1024], F32)) for p in range(2)]
    psB2 = [psD.enter_context(nc.psum_tensor(f"psB2{p}", [128, 512], F32)) for p in range(2)]

    for n in range(4):
        for j in range(5):
            src_row = (n * 4 + j) if j < 4 else (20 + n)
            rdep = "hscr_res" if j < 4 else "hscr_post"
            K.dma("pool", (lambda s3=src_row: NCP.dma_start(
                out=rslot[0][:], in_=hscr[s3:s3 + 1, :])),
                reads=(rdep, "rslot0"), writes=("rslot0",))
            for th in range(2):
                bp = th % 2
                K.op("pe", (lambda t2=th, b2=bp: NCT.matmul(
                    out=psB2[b2][:], lhsT=ones1h[:], rhs=rslot[0][:, 512 * t2:512 * (t2 + 1)],
                    start=True, stop=True)),
                    reads=("rslot0", "ones1h_r", f"psB2e{bp}"), writes=(f"psB2{bp}",))
                K.op("act", (lambda j2=j, t2=th, b2=bp: NCS.copy(
                    out=bc5[:, 1024 * j2 + 512 * t2:1024 * j2 + 512 * (t2 + 1)], in_=psB2[b2][:])),
                    reads=(f"psB2{bp}",), writes=("bc5", f"psB2e{bp}"))
        for cc in range(8):
            of = n * 8 + cc
            wp = of % 2
            for hf in range(4):
                K.dma("sync", (lambda o2=of, h2=hf: NCY.dma_start(
                    out=woraw[:], in_=d["wo"][o2][:, 1024 * h2:1024 * (h2 + 1)])),
                    reads=("woraw_c",), writes=("woraw",))
                K.op("pool", (lambda w2=wp, h2=hf: NCP.tensor_copy(
                    out=wob[w2][:, 1024 * h2:1024 * (h2 + 1)], in_=woraw[:])),
                    reads=("woraw",), writes=(f"wob{wp}", "woraw_c"))
            K.dma("pool", (lambda o2=of: NCP.dma_start(
                out=xtrow[0][:], in_=d["xqt"][128 * o2:128 * (o2 + 1), :])),
                reads=(), writes=("xtrow0",))
            op_ = of % 2
            for th in range(2):
                for c in range(32):
                    K.op("pe", (lambda c2=c, t2=th, p2=op_, w2=wp: NCT.matmul(
                        out=psW[p2][:, 512 * t2:512 * (t2 + 1)],
                        lhsT=wob[w2][:, 128 * c2:128 * (c2 + 1)],
                        rhs=qt[:, 1024 * c2 + 512 * t2:1024 * c2 + 512 * (t2 + 1)],
                        start=(c2 == 0), stop=(c2 == 31))),
                        reads=(f"wob{wp}", "qt", f"psWe{op_}"), writes=(f"psW{op_}",))
            K.op("dve", (lambda p2=op_, w2=wp: NCV.tensor_tensor(
                out=osb[p2][:], in0=psW[p2][:], in1=xtrow[w2][:], op=ALU.add)),
                reads=(f"psW{op_}", "xtrow0", f"osbc{op_}"), writes=(f"osb{op_}", f"psWe{op_}"))
            for j in range(4):
                jg = j * 8 + cc
                K.op("dve", (lambda j2=jg, j3=j, p2=op_: NCV.tensor_tensor(
                    out=fft2[p2][:], in0=xnt[:, 1024 * j2:1024 * (j2 + 1)],
                    in1=bc5[:, 1024 * j3:1024 * (j3 + 1)], op=ALU.mult)),
                    reads=("xnt", "bc5"), writes=(f"fft2{op_}",))
                K.op("dve", (lambda p2=op_: NCV.tensor_tensor(
                    out=osb[p2][:], in0=osb[p2][:], in1=fft2[p2][:], op=ALU.add)),
                    reads=(f"fft2{op_}",), writes=(f"osb{op_}",))
            K.op("dve", (lambda c2=cc, p2=op_: NCV.tensor_tensor(
                out=fft2[p2][:], in0=yt[:, 1024 * c2:1024 * (c2 + 1)],
                in1=bc5[:, 4096:5120], op=ALU.mult)),
                reads=("yt", "bc5"), writes=(f"fft2{op_}",))
            K.op("dve", (lambda p2=op_: NCV.tensor_tensor(
                out=osb[p2][:], in0=osb[p2][:], in1=fft2[p2][:], op=ALU.add)),
                reads=(f"fft2{op_}",), writes=(f"osb{op_}",))
            K.dma("act", (lambda o3=of, p2=op_: NCS.dma_start(
                out=d["yout"][o3], in_=osb[p2][:])),
                reads=(f"osb{op_}",), writes=(f"osbc{op_}",))

    K.barrier()
    final_qact = K.cnt["q_act"]
    K.ops["pool"].append(([("q_act", final_qact)], None, None))
    psD.close()
    esD.close()

    return _emit()


def _host_inputs(L0_attn_hidden_states, L0_attn_Wqkv, L0_attn_Wo, L0_ffn_W_res,
                 L0_ffn_W_pre, L0_ffn_W_post, L0_ffn_H_res_base, L0_ffn_H_pre_base,
                 L0_ffn_H_post_base, L0_ffn_gate, L0_ffn_W_branch):
    x = np.asarray(L0_attn_hidden_states, dtype=np.float32)
    wqkv = np.asarray(L0_attn_Wqkv, dtype=np.float32)
    wo = np.asarray(L0_attn_Wo, dtype=np.float32)

    def lay_blocks(w, nblocks):
        return np.ascontiguousarray(
            w.reshape(32, 128, nblocks, 128).transpose(2, 1, 0, 3).reshape(nblocks, 128, 32 * 128))

    wqkv_l = lay_blocks(wqkv, 34)
    wo_l = lay_blocks(wo, 32)
    wbr = np.asarray(L0_ffn_W_branch, dtype=np.float32)
    wbr_l = np.ascontiguousarray(
        wbr.reshape(8, 128, 8, 128).transpose(1, 2, 0, 3).reshape(128, 8192))
    w24 = np.concatenate([np.asarray(L0_ffn_W_res, dtype=np.float32),
                          np.asarray(L0_ffn_W_pre, dtype=np.float32),
                          np.asarray(L0_ffn_W_post, dtype=np.float32)], axis=0)
    w24_l = np.ascontiguousarray(w24.T.reshape(32, 128, 24).transpose(1, 0, 2).reshape(128, 32 * 24))
    bases = np.concatenate([np.asarray(L0_ffn_H_res_base, dtype=np.float32).reshape(-1),
                            np.asarray(L0_ffn_H_pre_base, dtype=np.float32),
                            np.asarray(L0_ffn_H_post_base, dtype=np.float32)]).reshape(1, 24)
    gate = np.asarray(L0_ffn_gate, dtype=np.float32).reshape(1, 1)

    kf = np.arange(512, dtype=np.float32)[None, :]
    qp = np.arange(128, dtype=np.float32)[:, None]
    dplain = np.ascontiguousarray(np.broadcast_to(kf, (128, 512)).astype(np.float32))
    i128h = np.eye(128).astype(BF)
    i128f = np.eye(128).astype(np.float32)
    ones1h = np.ones((1, 128), dtype=BF)
    ones1f = np.ones((1, 128), dtype=np.float32)
    onescol_h = np.ones((128, 1), dtype=BF)

    in_maps = []
    for c in range(8):
        b, r = c // 4, c % 4
        rows = np.concatenate([np.arange(512 * e + 128 * r, 512 * e + 128 * r + 128)
                               for e in range(8)])
        xq = np.ascontiguousarray(x[b][rows])
        xqt = np.ascontiguousarray(xq.T)
        dmask = np.where(kf <= 128 * r + qp, kf, np.float32(NEG_BIG)).astype(np.float32)
        qcol = (qp + 128 * r).astype(np.float32)
        in_maps.append(dict(
            xkv=np.ascontiguousarray(x[b]), xq=xq, xqt=xqt, wqkv=wqkv_l, wo=wo_l, wbr=wbr_l,
            w24=w24_l, bases24=bases, gate=gate, dplain=dplain,
            dmask=np.ascontiguousarray(dmask), qcol=np.ascontiguousarray(qcol),
            i128h=i128h, i128f=i128f, ones1h=ones1h, ones1f=ones1f,
            onescol_h=onescol_h))
    return in_maps


def kernel(**inputs):
    in_maps = _host_inputs(**inputs)
    nc = build_program()
    res = run_bass_kernel_spmd(nc, in_maps, core_ids=list(range(8)))
    out = np.empty((B, S, H), dtype=np.float32)
    for c in range(8):
        b, r = c // 4, c % 4
        y = np.asarray(res.results[c]["yout"])
        rows = y.transpose(2, 0, 1).reshape(1024, 4096)
        for e in range(8):
            out[b, 512 * e + 128 * r:512 * e + 128 * r + 128, :] = rows[128 * e:128 * (e + 1)]
    return out


# version 04
# speedup vs baseline: 1.2771x; 1.2042x over previous
import math
from contextlib import ExitStack

import numpy as np
import ml_dtypes

import concourse.bass as bass
import concourse.mybir as mybir
from concourse.bass_utils import run_bass_kernel_spmd

F32 = mybir.dt.float32
BF16 = mybir.dt.bfloat16
AF = mybir.ActivationFunctionType
ALU = mybir.AluOpType
AX = mybir.AxisListType
BF = ml_dtypes.bfloat16

B, S, H = 2, 4096, 4096
NH, HD = 32, 128
SCALE = float(HD) ** -0.5
LN_EPS = 1e-5
RMS_EPS = 1.1920929e-07
TAU_INV = float(1.0 / np.float32(0.1))
NEG_BIG = -1.0e30
SINK_ITERS = 50
NT_KV = S // 128
NT_Q = 8
NCH = H // 128


def _alibi_slopes(n):
    cp2 = 2 ** math.floor(math.log2(n))
    base = 2.0 ** (-(2.0 ** (-(math.log2(cp2) - 3))))
    slopes = base ** np.arange(1, cp2 + 1, dtype=np.float64)
    if cp2 != n:
        extra_base = 2.0 ** (-(2.0 ** (-(math.log2(2 * cp2) - 3))))
        rem = min(cp2, n - cp2)
        extra = extra_base ** np.arange(1, 1 + 2 * rem, 2, dtype=np.float64)
        slopes = np.concatenate([slopes, extra])
    return slopes.astype(np.float32)

SLOPES = _alibi_slopes(NH)


class Kb:
    ENGINES = ("pe", "act", "dve", "pool", "sync")
    ESEM = {"pe": "s_pe", "act": "s_act", "dve": "s_dve", "pool": "s_pool"}
    QSEM = {"sync": "q_sync", "pool": "q_pool", "act": "q_act", "dve": "q_dve"}

    def __init__(self):
        self.ops = {e: [] for e in self.ENGINES}
        self.cnt = {}
        for s in list(self.ESEM.values()) + list(self.QSEM.values()):
            self.cnt[s] = 0
        self.res = {}
        self.waited = {e: {} for e in self.ENGINES}

    def _need(self, eng, waits, token):
        sem, val = token
        if eng == "pe" and sem == self.ESEM.get("pe"):
            return
        if self.waited[eng].get(sem, 0) >= val:
            return
        self.waited[eng][sem] = val
        waits[sem] = max(waits.get(sem, 0), val)

    def _emit(self, eng, fn, reads, writes, inc_sem, inc_n):
        waits = {}
        for r in reads:
            ent = self.res.get(r)
            if ent and ent[0]:
                self._need(eng, waits, ent[0])
        for w in writes:
            ent = self.res.get(w)
            if ent:
                if ent[0]:
                    self._need(eng, waits, ent[0])
                for sem, val in ent[1].items():
                    self._need(eng, waits, (sem, val))
        self.cnt[inc_sem] += inc_n
        token = (inc_sem, self.cnt[inc_sem])
        for r in reads:
            ent = self.res.setdefault(r, [None, {}])
            ent[1][token[0]] = token[1]
        for w in writes:
            self.res[w] = [token, {}]
        self.ops[eng].append((sorted(waits.items()), fn, (inc_sem, inc_n)))
        return token

    def op(self, eng, fn, reads=(), writes=()):
        return self._emit(eng, fn, reads, writes, self.ESEM[eng], 1)

    def dma(self, eng, fn, reads=(), writes=()):
        return self._emit(eng, fn, reads, writes, self.QSEM[eng], 16)

    def barrier(self):
        snap = dict(self.cnt)
        for eng in self.ENGINES:
            waits = {}
            for sem, val in snap.items():
                if val == 0:
                    continue
                if eng in self.ESEM and self.ESEM[eng] == sem:
                    continue
                if self.waited[eng].get(sem, 0) >= val:
                    continue
                self.waited[eng][sem] = val
                waits[sem] = val
            if waits:
                self.ops[eng].append((sorted(waits.items()), None, None))


def build_program(upto="FULL"):
    nc = bass.Bass("TRN2", target_bir_lowering=False)
    K = Kb()
    sems = {}
    _emit_state = {}

    def _emit():
        def mk(eng_name, nce):
            def body(_):
                for waits, fn, inc in K.ops[eng_name]:
                    for s, v in waits:
                        nce.wait_ge(sems[s], v)
                    if fn is not None:
                        ins = fn()
                        if inc is not None:
                            ins.then_inc(sems[inc[0]], inc[1])
            return body
        with nc.Block() as block:
            block.tensor(mk("pe", _emit_state["NCT"]))
            block.scalar(mk("act", _emit_state["NCS"]))
            block.vector(mk("dve", _emit_state["NCV"]))
            block.gpsimd(mk("pool", _emit_state["NCP"]))
            block.sync(mk("sync", _emit_state["NCY"]))
        return nc

    def _dump(pairs):
        NCP = _emit_state["NCP"]
        K.barrier()
        for nm, handle in pairs:
            shp = list(handle[:].shape)
            dt_ = handle[:].dtype
            dout = nc.dram_tensor(nm, shp, dt_, kind="ExternalOutput").ap()
            K.dma("pool", (lambda d2=dout, h2=handle: NCP.dma_start(out=d2, in_=h2[:])),
                  reads=(), writes=(nm + "_out",))
        K.ops["pool"].append(([("q_pool", K.cnt["q_pool"])], None, None))
        return _emit()

    d = {}
    d["xkv"] = nc.dram_tensor("xkv", [S, H], F32, kind="ExternalInput").ap()
    d["xq"] = nc.dram_tensor("xq", [1024, H], F32, kind="ExternalInput").ap()
    d["xqt"] = nc.dram_tensor("xqt", [H, 1024], F32, kind="ExternalInput").ap()
    d["wqkv"] = nc.dram_tensor("wqkv", [34, 128, 32 * 128], F32, kind="ExternalInput").ap()
    d["wo"] = nc.dram_tensor("wo", [32, 128, 32 * 128], F32, kind="ExternalInput").ap()
    d["wbr"] = nc.dram_tensor("wbr", [128, 8192], F32, kind="ExternalInput").ap()
    d["w24"] = nc.dram_tensor("w24", [128, 32 * 24], F32, kind="ExternalInput").ap()
    d["bases24"] = nc.dram_tensor("bases24", [1, 24], F32, kind="ExternalInput").ap()
    d["gate"] = nc.dram_tensor("gate", [1, 1], F32, kind="ExternalInput").ap()
    d["dplain"] = nc.dram_tensor("dplain", [128, 512], F32, kind="ExternalInput").ap()
    d["dmask"] = nc.dram_tensor("dmask", [128, 512], F32, kind="ExternalInput").ap()
    d["dplain2"] = nc.dram_tensor("dplain2", [128, 1024], F32, kind="ExternalInput").ap()
    d["dmask2"] = nc.dram_tensor("dmask2", [128, 1024], F32, kind="ExternalInput").ap()
    d["qcol"] = nc.dram_tensor("qcol", [128, 1], F32, kind="ExternalInput").ap()
    d["i128h"] = nc.dram_tensor("i128h", [128, 128], BF16, kind="ExternalInput").ap()
    d["i128f"] = nc.dram_tensor("i128f", [128, 128], F32, kind="ExternalInput").ap()
    d["ones1h"] = nc.dram_tensor("ones1h", [1, 128], BF16, kind="ExternalInput").ap()
    d["ones1f"] = nc.dram_tensor("ones1f", [1, 128], F32, kind="ExternalInput").ap()
    d["onescol_h"] = nc.dram_tensor("onescol_h", [128, 1], BF16, kind="ExternalInput").ap()
    d["yout"] = nc.dram_tensor("yout", [32, 128, 1024], F32, kind="ExternalOutput").ap()
    hscr = nc.dram_tensor("hscr", [24, 1024], BF16, kind="Internal").ap()

    es = ExitStack()
    sb = lambda name, shape, dt: es.enter_context(nc.sbuf_tensor(name, shape, dt))

    xnt = sb("xnt", [128, NCH * 1024], BF16)
    esKV = ExitStack()
    kt = esKV.enter_context(nc.sbuf_tensor("kt", [128, S], BF16, side="right"))
    vt = esKV.enter_context(nc.sbuf_tensor("vt", [128, S], BF16, side="right"))
    dplain = esKV.enter_context(nc.sbuf_tensor("dplain_s", [128, 512], F32, side="right"))
    dmask = esKV.enter_context(nc.sbuf_tensor("dmask_s", [128, 512], F32, side="right"))
    dplain2 = esKV.enter_context(nc.sbuf_tensor("dplain2_s", [128, 1024], F32, side="right"))
    dmask2 = esKV.enter_context(nc.sbuf_tensor("dmask2_s", [128, 1024], F32, side="right"))

    qcol = sb("qcol_s", [128, 1], F32)
    i128h = sb("i128h_s", [128, 128], BF16)
    i128f = sb("i128f_s", [128, 128], F32)
    ones1h = sb("ones1h_s", [1, 128], BF16)
    ones1f = sb("ones1f_s", [1, 128], F32)
    onescol_h = sb("onescol_h_s", [128, 1], BF16)
    rstdF = sb("rstdF", [128, 8], F32)
    lgraw = sb("lgraw", [24, 1024], F32)
    negc = sb("negc", [128, 1], F32)
    bases_row = sb("bases_row", [1, 24], F32)
    gate_s = sb("gate_s", [1, 1], F32)

    for s in list(Kb.ESEM.values()) + list(Kb.QSEM.values()):
        sems[s] = es.enter_context(nc.semaphore(s))

    NCV = nc.vector
    NCS = nc.scalar
    NCT = nc.tensor
    NCP = nc.gpsimd
    NCY = nc.sync
    _emit_state.update(NCV=NCV, NCS=NCS, NCT=NCT, NCP=NCP, NCY=NCY)

    for name, dst in [("dplain", dplain), ("dmask", dmask), ("dplain2", dplain2),
                      ("dmask2", dmask2), ("qcol", qcol),
                      ("i128h", i128h), ("i128f", i128f), ("ones1h", ones1h),
                      ("ones1f", ones1f), ("onescol_h", onescol_h),
                      ("bases24", bases_row), ("gate", gate_s)]:
        src = d[name]
        K.dma("pool", (lambda dd=dst, ss=src: NCP.dma_start(out=dd[:], in_=ss)),
              reads=(), writes=(name + "_r",))

    esA = ExitStack()
    sbA = lambda name, shape, dt: esA.enter_context(nc.sbuf_tensor(name, shape, dt))
    xbuf = [sbA(f"xbuf{p}", [128, H], F32) for p in range(2)]
    xnbuf = [sbA(f"xnbuf{p}", [128, H], BF16) for p in range(2)]
    stage = [sbA(f"stage{p}", [128, H], BF16) for p in range(2)]
    junk = sbA("junk", [128, H], BF16)
    wkvraw = sbA("wkvraw", [128, 4096], F32)
    wkvb = sbA("wkvb", [128, 2 * 4096], BF16)
    w24raw = sbA("w24raw", [128, 32 * 24], F32)
    w24b = sbA("w24b", [128, 32 * 24], BF16)
    s1 = sbA("s1", [128, 1], F32)
    s2 = sbA("s2", [128, 1], F32)
    mn = sbA("mn", [128, 1], F32)
    mn2 = sbA("mn2", [128, 1], F32)
    var = sbA("var", [128, 1], F32)
    sd = sbA("sd", [128, 1], F32)
    rstd = sbA("rstd", [128, 1], F32)
    negmr = sbA("negmr", [128, 1], F32)
    s2f = sbA("s2f", [128, 1], F32)
    sdF = sbA("sdF", [128, 1], F32)
    psA = ExitStack()
    psT = [psA.enter_context(nc.psum_tensor(f"psT{p}", [128, 512], BF16)) for p in range(2)]
    psK = [psA.enter_context(nc.psum_tensor(f"psK{p}", [128, 128], F32)) for p in range(2)]
    psV = [psA.enter_context(nc.psum_tensor(f"psV{p}", [128, 128], F32)) for p in range(2)]
    psL = psA.enter_context(nc.psum_tensor("psL", [24, 128], F32))

    for bi, blk in enumerate((32, 33)):
        K.dma("sync", (lambda b2=blk: NCY.dma_start(out=wkvraw[:], in_=d["wqkv"][b2])),
              reads=("wkvb_c",), writes=("wkvraw",))
        K.op("pool", (lambda b3=bi: NCP.tensor_copy(out=wkvb[:, b3 * 4096:(b3 + 1) * 4096], in_=wkvraw[:])),
             reads=("wkvraw",), writes=("wkvb", "wkvb_c"))
    K.dma("sync", (lambda: NCY.dma_start(out=w24raw[:], in_=d["w24"])),
          reads=(), writes=("w24raw",))
    K.op("pool", (lambda: NCP.tensor_copy(out=w24b[:], in_=w24raw[:])),
         reads=("w24raw",), writes=("w24b",))

    def ln_tile(src_ap, p, tag):
        xb, xn = xbuf[p], xnbuf[p]
        rx, rn = f"xbuf{p}", f"xnbuf{p}"
        K.dma("sync", (lambda: NCY.dma_start(out=xb[:], in_=src_ap)),
              reads=(rn,), writes=(rx,))
        K.op("dve", (lambda: NCV.reduce_sum(out=s1[:], in_=xb[:], axis=AX.X)),
             reads=(rx,), writes=("s1",))
        K.op("act", (lambda: NCS.activation(out=junk[:], in_=xb[:], func=AF.Square,
                                            bias=0.0, scale=1.0, accum_out=s2[:])),
             reads=(rx,), writes=("junk", "s2"))
        K.op("dve", (lambda: NCV.tensor_scalar_mul(out=mn[:], in0=s1[:], scalar1=1.0 / H)),
             reads=("s1",), writes=("mn",))
        K.op("dve", (lambda: NCV.tensor_mul(out=mn2[:], in0=mn[:], in1=mn[:])),
             reads=("mn",), writes=("mn2",))
        K.op("dve", (lambda: NCV.scalar_tensor_tensor(out=var[:], in0=s2[:], scalar=1.0 / H,
                                                      in1=mn2[:], op0=ALU.mult, op1=ALU.subtract)),
             reads=("s2", "mn2"), writes=("var",))
        K.op("dve", (lambda: NCV.tensor_scalar_add(out=var[:], in0=var[:], scalar1=LN_EPS)),
             reads=("var",), writes=("var",))
        K.op("act", (lambda: NCS.activation(out=sd[:], in_=var[:], func=AF.Sqrt, bias=0.0, scale=1.0)),
             reads=("var",), writes=("sd",))
        K.op("dve", (lambda: NCV.reciprocal(out=rstd[:], in_=sd[:])),
             reads=("sd",), writes=("rstd",))
        K.op("dve", (lambda: NCV.scalar_tensor_tensor(out=negmr[:], in0=mn[:], scalar=-1.0,
                                                      in1=rstd[:], op0=ALU.mult, op1=ALU.mult)),
             reads=("mn", "rstd"), writes=("negmr",))
        K.op("act", (lambda: NCS.activation(out=xn[:], in_=xb[:], func=AF.Identity,
                                            bias=negmr[:, 0:1], scale=rstd[:, 0:1])),
             reads=(rx, "rstd", "negmr"), writes=(rn,))
        return xn, rn

    for i in range(NT_KV):
        p = i % 2
        xn, rn = ln_tile(d["xkv"][128 * i:128 * (i + 1), :], p, f"a1_{i}")
        st, rs = stage[p], f"stage{p}"
        for g in range(8):
            tp = g % 2
            for u in range(4):
                j = 4 * g + u
                K.op("pe", (lambda xn2=xn, j2=j, t2=tp: NCT.transpose(
                    out=psT[t2][:, 128 * (j2 % 4):128 * (j2 % 4 + 1)],
                    in_=xn2[:, 128 * j2:128 * (j2 + 1)], identity=i128h[:])),
                    reads=(rn, "i128h_r", f"psTe{tp}"), writes=(f"psT{tp}",))
            K.op("act", (lambda s3=st, g2=g, t2=tp: NCS.copy(
                out=s3[:, 512 * g2:512 * (g2 + 1)], in_=psT[t2][:])),
                reads=(f"psT{tp}",), writes=(rs, f"psTe{tp}"))
        kp = i % 2
        for c in range(32):
            K.op("pe", (lambda c2=c, s3=st, k2=kp: NCT.matmul(
                out=psK[k2][:], lhsT=wkvb[:, 128 * c2:128 * (c2 + 1)],
                rhs=s3[:, 128 * c2:128 * (c2 + 1)],
                start=(c2 == 0), stop=(c2 == 31))),
                reads=(rs, "wkvb", f"psKe{kp}"), writes=(f"psK{kp}",))
        K.op("act", (lambda i2=i, k2=kp: NCS.copy(out=kt[:, 128 * i2:128 * (i2 + 1)], in_=psK[k2][:])),
             reads=(f"psK{kp}",), writes=("kt", f"psKe{kp}"))
        for c in range(32):
            K.op("pe", (lambda c2=c, s3=st, k2=kp: NCT.matmul(
                out=psV[k2][:], lhsT=s3[:, 128 * c2:128 * (c2 + 1)],
                rhs=wkvb[:, 4096 + 128 * c2:4096 + 128 * (c2 + 1)],
                start=(c2 == 0), stop=(c2 == 31))),
                reads=(rs, "wkvb", f"psVe{kp}"), writes=(f"psV{kp}",))
        K.op("act", (lambda i2=i, k2=kp: NCS.copy(out=vt[:, 128 * i2:128 * (i2 + 1)], in_=psV[k2][:])),
             reads=(f"psV{kp}",), writes=("vt", f"psVe{kp}"))

    for e in range(NT_Q):
        p = e % 2
        xn, rn = ln_tile(d["xq"][128 * e:128 * (e + 1), :], p, f"a2_{e}")
        for g in range(8):
            tp = g % 2
            for u in range(4):
                j = 4 * g + u
                K.op("pe", (lambda xn2=xn, j2=j, t2=tp: NCT.transpose(
                    out=psT[t2][:, 128 * (j2 % 4):128 * (j2 % 4 + 1)],
                    in_=xn2[:, 128 * j2:128 * (j2 + 1)], identity=i128h[:])),
                    reads=(rn, "i128h_r", f"psTe{tp}"), writes=(f"psT{tp}",))
            for u in range(4):
                j = 4 * g + u
                K.op("act", (lambda j2=j, e2=e, t2=tp, u2=u: NCS.copy(
                    out=xnt[:, 1024 * j2 + 128 * e2:1024 * j2 + 128 * (e2 + 1)],
                    in_=psT[t2][:, 128 * u2:128 * (u2 + 1)])),
                    reads=(f"psT{tp}",),
                    writes=("xnt", f"psTe{tp}") if u == 3 else ("xnt",))
        K.op("act", (lambda xn2=xn: NCS.activation(out=junk[:], in_=xn2[:], func=AF.Square,
                                                   bias=0.0, scale=1.0, accum_out=s2f[:])),
             reads=(rn,), writes=("junk", "s2f"))
        K.op("dve", (lambda: NCV.tensor_scalar(out=s2f[:], in0=s2f[:], scalar1=1.0 / H,
                                               scalar2=RMS_EPS, op0=ALU.mult, op1=ALU.add)),
             reads=("s2f",), writes=("s2f",))
        K.op("act", (lambda: NCS.activation(out=sdF[:], in_=s2f[:], func=AF.Sqrt, bias=0.0, scale=1.0)),
             reads=("s2f",), writes=("sdF",))
        K.op("dve", (lambda e2=e: NCV.reciprocal(out=rstdF[:, e2:e2 + 1], in_=sdF[:])),
             reads=("sdF",), writes=("rstdF",))
        for c in range(32):
            K.op("pe", (lambda c2=c, e2=e: NCT.matmul(
                out=psL[:], lhsT=w24b[:, 24 * c2:24 * (c2 + 1)],
                rhs=xnt[:, 1024 * c2 + 128 * e2:1024 * c2 + 128 * (e2 + 1)],
                start=(c2 == 0), stop=(c2 == 31))),
                reads=("xnt", "w24b", "psLe"), writes=("psL",))
        K.op("act", (lambda e2=e: NCS.copy(out=lgraw[:, 128 * e2:128 * (e2 + 1)], in_=psL[:])),
             reads=("psL",), writes=("lgraw", "psLe"))

    if upto == "A":
        return _dump([("d_xnt", xnt), ("d_kt", kt), ("d_vt", vt),
                      ("d_lgraw", lgraw), ("d_rstdF", rstdF)])
    K.barrier()
    psA.close()
    esA.close()

    qt = sb("qt", [128, NH * 1024], BF16)
    esB = ExitStack()
    sbB = lambda name, shape, dt: esB.enter_context(nc.sbuf_tensor(name, shape, dt))
    wqraw = sbB("wqraw", [128, 4096], F32)
    wqb = [sbB(f"wqb{p}", [128, 4096], BF16) for p in range(2)]
    qsq = sbB("qsq", [128, 1024], BF16)
    q2row = sbB("q2row", [1, 1024], F32)
    q2cols = sbB("q2cols", [1, 40], F32)
    tiny = sbB("tiny", [1, 4], F32)
    psB = ExitStack()
    psQ = [psB.enter_context(nc.psum_tensor(f"psQ{p}", [128, 512], F32)) for p in range(2)]
    ps12 = psB.enter_context(nc.psum_tensor("ps12", [1, 512], F32))
    psC1 = psB.enter_context(nc.psum_tensor("psC1", [128, 1], F32))

    for h in range(NH):
        K.dma("sync", (lambda h2=h: NCY.dma_start(out=wqraw[:], in_=d["wqkv"][h2])),
              reads=("wqraw_c",), writes=("wqraw",))
        wp = h % 2
        K.op("pool", (lambda w2=wp: NCP.tensor_copy(out=wqb[w2][:], in_=wqraw[:])),
             reads=("wqraw",), writes=(f"wqb{wp}", "wqraw_c"))
        for th in range(2):
            qp_ = (2 * h + th) % 2
            for c in range(32):
                K.op("pe", (lambda c2=c, t2=th, q2=qp_, w2=wp: NCT.matmul(
                    out=psQ[q2][:], lhsT=wqb[w2][:, 128 * c2:128 * (c2 + 1)],
                    rhs=xnt[:, 1024 * c2 + 512 * t2:1024 * c2 + 512 * (t2 + 1)],
                    start=(c2 == 0), stop=(c2 == 31))),
                    reads=(f"wqb{wp}", "xnt", f"psQe{qp_}"), writes=(f"psQ{qp_}",))
            K.op("act", (lambda h2=h, t2=th, q2=qp_: NCS.copy(
                out=qt[:, 1024 * h2 + 512 * t2:1024 * h2 + 512 * (t2 + 1)], in_=psQ[q2][:])),
                reads=(f"psQ{qp_}",), writes=("qt", f"psQe{qp_}"))
        K.op("act", (lambda h2=h: NCS.activation(out=qsq[:], in_=qt[:, 1024 * h2:1024 * (h2 + 1)],
                                                 func=AF.Square, bias=0.0, scale=1.0)),
             reads=("qt",), writes=("qsq",))
        for th in range(2):
            K.op("pe", (lambda t2=th: NCT.matmul(
                out=ps12[:], lhsT=onescol_h[:], rhs=qsq[:, 512 * t2:512 * (t2 + 1)],
                start=True, stop=True)),
                reads=("qsq", "onescol_h_r", "ps12e"), writes=("ps12",))
            K.op("act", (lambda t2=th: NCS.copy(out=q2row[:, 512 * t2:512 * (t2 + 1)], in_=ps12[:])),
                 reads=("ps12",), writes=("q2row", "ps12e"))
        K.op("dve", (lambda h2=h: NCV.reduce_max(out=q2cols[:, h2:h2 + 1], in_=q2row[:], axis=AX.X)),
             reads=("q2row",), writes=("q2cols",))
    for g in range(4):
        K.op("act", (lambda g2=g: NCS.activation(out=qsq[:], in_=kt[:, 1024 * g2:1024 * (g2 + 1)],
                                                 func=AF.Square, bias=0.0, scale=1.0)),
             reads=("kt", "qsq_c"), writes=("qsq",))
        for th in range(2):
            K.op("pe", (lambda t2=th: NCT.matmul(
                out=ps12[:], lhsT=onescol_h[:], rhs=qsq[:, 512 * t2:512 * (t2 + 1)],
                start=True, stop=True)),
                reads=("qsq", "onescol_h_r", "ps12e"), writes=("ps12", "qsq_c"))
            K.op("act", (lambda t2=th: NCS.copy(out=q2row[:, 512 * t2:512 * (t2 + 1)], in_=ps12[:])),
                 reads=("ps12",), writes=("q2row", "ps12e"))
        K.op("dve", (lambda g2=g: NCV.reduce_max(out=q2cols[:, 32 + g2:33 + g2], in_=q2row[:], axis=AX.X)),
             reads=("q2row",), writes=("q2cols",))
    K.op("dve", (lambda: NCV.reduce_max(out=tiny[:, 0:1], in_=q2cols[:, 0:32], axis=AX.X)),
         reads=("q2cols",), writes=("t_q2m",))
    K.op("dve", (lambda: NCV.reduce_max(out=tiny[:, 1:2], in_=q2cols[:, 32:36], axis=AX.X)),
         reads=("q2cols",), writes=("t_k2m",))
    K.op("dve", (lambda: NCV.tensor_mul(out=tiny[:, 2:3], in0=tiny[:, 0:1], in1=tiny[:, 1:2])),
         reads=("t_q2m", "t_k2m"), writes=("t_prod",))
    K.op("act", (lambda: NCS.activation(out=tiny[:, 3:4], in_=tiny[:, 2:3], func=AF.Sqrt,
                                        bias=0.0, scale=1.0)),
         reads=("t_prod",), writes=("t_sq",))
    K.op("dve", (lambda: NCV.tensor_scalar(out=tiny[:, 2:3], in0=tiny[:, 3:4], scalar1=SCALE,
                                           scalar2=3.0, op0=ALU.mult, op1=ALU.add)),
         reads=("t_sq",), writes=("t_C",))
    K.op("pe", (lambda: NCT.matmul(out=psC1[:], lhsT=ones1f[:], rhs=tiny[:, 2:3],
                                   start=True, stop=True)),
         reads=("t_C", "ones1f_r"), writes=("psC1",))
    K.op("act", (lambda: NCS.activation(out=negc[:], in_=psC1[:], func=AF.Copy, bias=0.0, scale=-1.0)),
         reads=("psC1",), writes=("negc",))

    if upto == "B":
        return _dump([("d_qt", qt), ("d_negc", negc), ("d_lgraw", lgraw)])
    K.barrier()
    psB.close()
    esB.close()

    esC = ExitStack()
    sbC = lambda name, shape, dt: esC.enter_context(nc.sbuf_tensor(name, shape, dt))
    tsc = [sbC(f"tsc{p}", [128, 1024], F32) for p in range(2)]
    pstore = [sbC(f"pstore{p}", [128, 4096], BF16) for p in range(2)]
    pts = [sbC(f"pts{p}", [128, 1024], BF16) for p in range(2)]
    rows = [sbC(f"rows{p}", [128, 4], F32) for p in range(2)]
    rs_ = sbC("rs", [128, 1], F32)
    rrec = [sbC(f"rrec{p}", [128, 1], F32) for p in range(2)]
    basehe = sbC("basehe", [128, 1], F32)
    tbcol = sbC("tbcol", [128, 1], F32)
    biast = [sbC(f"biast{p}", [128, 1], F32) for p in range(2)]
    psC = ExitStack()
    psS2 = [psC.enter_context(nc.psum_tensor(f"psS2{p}", [128, 1024], F32)) for p in range(2)]
    psPT = [psC.enter_context(nc.psum_tensor(f"psPT{p}", [128, 1024], BF16)) for p in range(2)]
    psO = [psC.enter_context(nc.psum_tensor(f"psO{p}", [128, 128], F32)) for p in range(2)]

    idxs = [(h, e) for h in range(NH) for e in range(NT_Q)]
    ctr = {"p1": 0, "p2": 0}

    def pair_width(nk, p):
        return 512 if 2 * p + 1 >= nk else 1024

    def c_phase1(i):
        h, e = idxs[i]
        ip = i % 2
        slope = float(SLOPES[h])
        sprime = slope / SCALE
        qlo = 1024 * h + 128 * e
        nk = e + 1
        npairs = (nk + 1) // 2
        K.op("dve", (lambda e2=e, sl=slope: NCV.tensor_scalar(
            out=tbcol[:], in0=qcol[:], scalar1=float(512 * e2), scalar2=-sl,
            op0=ALU.add, op1=ALU.mult)),
            reads=("qcol_r",), writes=("tbcol",))
        K.op("dve", (lambda: NCV.scalar_tensor_tensor(
            out=basehe[:], in0=tbcol[:], scalar=1.0, in1=negc[:],
            op0=ALU.mult, op1=ALU.add)),
            reads=("tbcol", "negc"), writes=("basehe",))
        for p in range(npairs):
            kb0 = 2 * p
            w = pair_width(nk, p)
            pp = ctr["p1"] % 2
            ctr["p1"] += 1
            for s_ in range(w // 512):
                kb = kb0 + s_
                K.op("pe", (lambda q2=qlo, k2=kb, p2=pp, s2=s_: NCT.matmul(
                    out=psS2[p2][:, 512 * s2:512 * (s2 + 1)], lhsT=qt[:, q2:q2 + 128],
                    rhs=kt[:, 512 * k2:512 * (k2 + 1)], start=True, stop=True)),
                    reads=("qt", "kt", f"psS2e{pp}"), writes=(f"psS2{pp}",))
            if w == 512:
                dt_, dr = dmask, "dmask_r"
            elif kb0 + 2 == nk:
                dt_, dr = dmask2, "dmask2_r"
            else:
                dt_, dr = dplain2, "dplain2_r"
            K.op("dve", (lambda k2=kb0, p2=pp, sl=slope: NCV.tensor_scalar_add(
                out=biast[p2][:], in0=basehe[:], scalar1=float(sl * 512 * k2))),
                reads=("basehe",), writes=(f"biast{pp}",))
            K.op("dve", (lambda d2=dt_, p2=pp, sp=sprime, w2=w: NCV.scalar_tensor_tensor(
                out=tsc[p2][:, 0:w2], in0=d2[:, 0:w2], scalar=sp, in1=psS2[p2][:, 0:w2],
                op0=ALU.mult, op1=ALU.add)),
                reads=(dr, f"psS2{pp}", f"tsce{pp}"),
                writes=(f"tsc{pp}", f"psS2e{pp}"))
            K.op("act", (lambda p2=pp, p3=p, i2=ip, w2=w: NCS.activation(
                out=pstore[i2][:, 1024 * p3:1024 * p3 + w2], in_=tsc[p2][:, 0:w2], func=AF.Exp,
                bias=biast[p2][:, 0:1], scale=SCALE,
                accum_out=rows[i2][:, p3:p3 + 1])),
                reads=(f"tsc{pp}", f"biast{pp}", f"rows{ip}_c"),
                writes=(f"pstore{ip}", f"rows{ip}", f"tsce{pp}"))
        K.op("dve", (lambda i2=ip, n2=npairs: NCV.reduce_sum(
            out=rs_[:], in_=rows[i2][:, 0:n2], axis=AX.X)),
            reads=(f"rows{ip}",), writes=("rs", f"rows{ip}_c"))
        K.op("dve", (lambda i2=ip: NCV.reciprocal(out=rrec[i2][:], in_=rs_[:])),
             reads=("rs",), writes=(f"rrec{ip}",))

    def c_phase2(i):
        h, e = idxs[i]
        ip = i % 2
        oe = i % 2
        qlo = 1024 * h + 128 * e
        nk = e + 1
        npairs = (nk + 1) // 2
        nchunks = 4 * nk
        for p in range(npairs):
            w = pair_width(nk, p)
            K.op("dve", (lambda i2=ip, p3=p, w2=w: NCV.tensor_scalar_mul(
                out=pstore[i2][:, 1024 * p3:1024 * p3 + w2],
                in0=pstore[i2][:, 1024 * p3:1024 * p3 + w2], scalar1=rrec[i2][:, 0:1])),
                reads=(f"pstore{ip}", f"rrec{ip}"), writes=(f"pstore{ip}",))
            pt = ctr["p2"] % 2
            ctr["p2"] += 1
            for u in range(w // 128):
                K.op("pe", (lambda i2=ip, p3=p, u2=u, t2=pt: NCT.transpose(
                    out=psPT[t2][:, 128 * u2:128 * (u2 + 1)],
                    in_=pstore[i2][:, 1024 * p3 + 128 * u2:1024 * p3 + 128 * (u2 + 1)],
                    identity=i128h[:])),
                    reads=(f"pstore{ip}", "i128h_r", f"psPTe{pt}"), writes=(f"psPT{pt}",))
            K.op("act", (lambda t2=pt, w2=w: NCS.copy(out=pts[t2][:, 0:w2], in_=psPT[t2][:, 0:w2])),
                 reads=(f"psPT{pt}",), writes=(f"pts{pt}", f"psPTe{pt}"))
            for u in range(w // 128):
                kc = 8 * p + u
                st = (p == 0 and u == 0)
                sp2 = (kc == nchunks - 1)
                K.op("pe", (lambda t2=pt, u2=u, k3=kc, o2=oe, s3=st, s4=sp2: NCT.matmul(
                    out=psO[o2][:], lhsT=vt[:, 128 * k3:128 * (k3 + 1)],
                    rhs=pts[t2][:, 128 * u2:128 * (u2 + 1)], start=s3, stop=s4)),
                    reads=(f"pts{pt}", "vt", f"psOe{oe}"), writes=(f"psO{oe}",))
        K.op("act", (lambda q2=qlo, o2=oe: NCS.copy(out=qt[:, q2:q2 + 128], in_=psO[o2][:])),
             reads=(f"psO{oe}",), writes=("qt", f"psOe{oe}"))

    for i in range(len(idxs)):
        c_phase1(i)
        if i > 0:
            c_phase2(i - 1)
    c_phase2(len(idxs) - 1)

    if upto == "C":
        return _dump([("d_qt", qt)])
    K.barrier()
    psC.close()
    esC.close()
    esKV.close()

    esS = ExitStack()
    sbS = lambda name, shape, dt: esS.enter_context(nc.sbuf_tensor(name, shape, dt))
    lg = sbS("lg", [128, 192], F32)
    bb24 = sbS("bb24", [128, 24], F32)
    zres = sbS("zres", [128, 128], F32)
    asb = sbS("asb", [128, 128], F32)
    bsb = sbS("bsb", [128, 128], F32)
    esb = sbS("esb", [128, 128], F32)
    hsb = sbS("hsb", [128, 128], F32)
    msb = sbS("msb", [128, 32], F32)
    lsb = sbS("lsb", [128, 32], F32)
    ssb = sbS("ssb", [128, 32], F32)
    ubuf = sbS("ubuf", [128, 32], F32)
    vbuf = sbS("vbuf", [128, 32], F32)
    mp = sbS("mp", [128, 8], F32)
    ep = sbS("ep", [128, 32], F32)
    sp_ = sbS("sp", [128, 8], F32)
    hpre = sbS("hpre", [128, 32], F32)
    hpost = sbS("hpost", [128, 32], F32)
    gcol = sbS("gcol", [128, 1], F32)
    omg = sbS("omg", [128, 1], F32)
    hresT = sbS("hresT", [16, 1024], BF16)
    hpreT = sbS("hpreT", [4, 1024], BF16)
    hpostT = sbS("hpostT", [4, 1024], BF16)
    psSt = ExitStack()
    psLT = psSt.enter_context(nc.psum_tensor("psLT", [128, 24], F32))
    psHT = psSt.enter_context(nc.psum_tensor("psHT", [16, 128], F32))
    psG = psSt.enter_context(nc.psum_tensor("psG", [128, 1], F32))

    K.op("pe", (lambda: NCT.matmul(out=psLT[:], lhsT=ones1f[:], rhs=bases_row[:],
                                   start=True, stop=True)),
         reads=("bases24_r", "ones1f_r", "psLTe"), writes=("psLT",))
    K.op("act", (lambda: NCS.copy(out=bb24[:], in_=psLT[:])),
         reads=("psLT",), writes=("bb24", "psLTe"))
    for e in range(8):
        K.op("pe", (lambda e2=e: NCT.transpose(out=psLT[:], in_=lgraw[:, 128 * e2:128 * (e2 + 1)],
                                               identity=i128f[0:24, 0:24])),
             reads=("lgraw", "i128f_r", "psLTe"), writes=("psLT",))
        K.op("dve", (lambda e2=e: NCV.scalar_tensor_tensor(
            out=lg[:, 24 * e2:24 * (e2 + 1)], in0=psLT[:], scalar=rstdF[:, e2:e2 + 1],
            in1=bb24[:], op0=ALU.mult, op1=ALU.add)),
            reads=("psLT", "rstdF", "bb24"), writes=("lg", "psLTe"))
    lgv = lg.rearrange("p (a x) -> p a x", a=8, x=24)
    K.op("dve", (lambda: NCV.tensor_scalar_mul(
        out=zres.rearrange("p (a x) -> p a x", a=8, x=16),
        in0=lgv[:, :, 0:16], scalar1=TAU_INV)),
        reads=("lg",), writes=("zres",))
    K.op("dve", (lambda: NCV.memset(ubuf[:], 0.0)), reads=(), writes=("ubuf",))
    K.op("dve", (lambda: NCV.memset(vbuf[:], 0.0)), reads=(), writes=("vbuf",))

    z4 = zres.rearrange("p (a i j) -> p a i j", a=8, i=4, j=4)
    a4 = asb.rearrange("p (a i j) -> p a i j", a=8, i=4, j=4)
    b4 = bsb.rearrange("p (a i j) -> p a i j", a=8, i=4, j=4)
    e4 = esb.rearrange("p (a i j) -> p a i j", a=8, i=4, j=4)
    uv = ubuf.rearrange("p (a i) -> p a i", a=8, i=4)
    vv = vbuf.rearrange("p (a j) -> p a j", a=8, j=4)
    mv = msb.rearrange("p (a i) -> p a i", a=8, i=4)
    sv = ssb.rearrange("p (a i) -> p a i", a=8, i=4)
    vb_bc = vv.unsqueeze(2).broadcast_to([128, 8, 4, 4])
    ub_bc = uv.unsqueeze(3).broadcast_to([128, 8, 4, 4])
    mb_bc = mv.unsqueeze(3).broadcast_to([128, 8, 4, 4])
    m2_bc = mv.unsqueeze(2).broadcast_to([128, 8, 4, 4])
    a4_ji = asb.rearrange("p (a i j) -> p a j i", a=8, i=4, j=4)
    e4_ji = esb.rearrange("p (a i j) -> p a j i", a=8, i=4, j=4)

    for it in range(SINK_ITERS):
        K.op("dve", (lambda: NCV.tensor_tensor(out=a4, in0=z4, in1=vb_bc, op=ALU.subtract)),
             reads=("zres", "vbuf"), writes=("asb",))
        K.op("dve", (lambda: NCV.tensor_reduce(out=mv, in_=a4, axis=AX.X, op=ALU.max)),
             reads=("asb",), writes=("msb",))
        K.op("dve", (lambda: NCV.tensor_tensor(out=b4, in0=a4, in1=mb_bc, op=ALU.subtract)),
             reads=("asb", "msb"), writes=("bsb",))
        K.op("act", (lambda: NCS.activation(out=esb[:], in_=bsb[:], func=AF.Exp, bias=0.0, scale=1.0)),
             reads=("bsb",), writes=("esb",))
        K.op("dve", (lambda: NCV.tensor_reduce(out=sv, in_=e4, axis=AX.X, op=ALU.add)),
             reads=("esb",), writes=("ssb",))
        K.op("act", (lambda: NCS.activation(out=lsb[:], in_=ssb[:], func=AF.Ln, bias=0.0, scale=1.0)),
             reads=("ssb",), writes=("lsb",))
        K.op("dve", (lambda: NCV.tensor_add(out=ubuf[:], in0=msb[:], in1=lsb[:])),
             reads=("msb", "lsb"), writes=("ubuf",))
        K.op("dve", (lambda: NCV.tensor_tensor(out=a4, in0=z4, in1=ub_bc, op=ALU.subtract)),
             reads=("zres", "ubuf"), writes=("asb",))
        K.op("dve", (lambda: NCV.tensor_reduce(out=mv, in_=a4_ji, axis=AX.X, op=ALU.max)),
             reads=("asb",), writes=("msb",))
        K.op("dve", (lambda: NCV.tensor_tensor(out=b4, in0=a4, in1=m2_bc, op=ALU.subtract)),
             reads=("asb", "msb"), writes=("bsb",))
        K.op("act", (lambda: NCS.activation(out=esb[:], in_=bsb[:], func=AF.Exp, bias=0.0, scale=1.0)),
             reads=("bsb",), writes=("esb",))
        K.op("dve", (lambda: NCV.tensor_reduce(out=sv, in_=e4_ji, axis=AX.X, op=ALU.add)),
             reads=("esb",), writes=("ssb",))
        K.op("act", (lambda: NCS.activation(out=lsb[:], in_=ssb[:], func=AF.Ln, bias=0.0, scale=1.0)),
             reads=("ssb",), writes=("lsb",))
        K.op("dve", (lambda: NCV.tensor_add(out=vbuf[:], in0=msb[:], in1=lsb[:])),
             reads=("msb", "lsb"), writes=("vbuf",))
    K.op("dve", (lambda: NCV.tensor_tensor(out=a4, in0=z4, in1=ub_bc, op=ALU.subtract)),
         reads=("zres", "ubuf"), writes=("asb",))
    K.op("dve", (lambda: NCV.tensor_tensor(out=b4, in0=a4, in1=vb_bc, op=ALU.subtract)),
         reads=("asb", "vbuf"), writes=("bsb",))
    K.op("act", (lambda: NCS.activation(out=hsb[:], in_=bsb[:], func=AF.Exp, bias=0.0, scale=1.0)),
         reads=("bsb",), writes=("hsb",))
    K.op("act", (lambda: NCS.activation(out=gate_s[:], in_=gate_s[:], func=AF.Sigmoid, bias=0.0, scale=1.0)),
         reads=("gate_r",), writes=("gate_v",))
    K.op("pe", (lambda: NCT.matmul(out=psG[:], lhsT=ones1f[:], rhs=gate_s[:], start=True, stop=True)),
         reads=("gate_v", "ones1f_r"), writes=("psG",))
    K.op("act", (lambda: NCS.copy(out=gcol[:], in_=psG[:])),
         reads=("psG",), writes=("gcol",))
    K.op("dve", (lambda: NCV.tensor_scalar(out=omg[:], in0=gcol[:], scalar1=-1.0, scalar2=1.0,
                                           op0=ALU.mult, op1=ALU.add)),
         reads=("gcol",), writes=("omg",))
    K.op("dve", (lambda: NCV.tensor_scalar_mul(out=hsb[:], in0=hsb[:], scalar1=gcol[:, 0:1])),
         reads=("hsb", "gcol"), writes=("hsb",))
    hdg = hsb.rearrange("p (a x) -> p a x", a=8, x=16)
    for dgi in range(4):
        K.op("dve", (lambda d2=dgi: NCV.tensor_scalar_add(
            out=hdg[:, :, 5 * d2:5 * d2 + 1], in0=hdg[:, :, 5 * d2:5 * d2 + 1],
            scalar1=omg[:, 0:1])),
            reads=("hsb", "omg"), writes=("hsb",))
    for (name, off, out_sb) in (("hpre", 16, hpre), ("hpost", 20, hpost)):
        lgs = lgv[:, :, off:off + 4]
        mbc = mp[:].unsqueeze(2).broadcast_to([128, 8, 4])
        spc = sp_[:].unsqueeze(2).broadcast_to([128, 8, 4])
        epv = ep.rearrange("p (a n) -> p a n", a=8, n=4)
        K.op("dve", (lambda l2=lgs: NCV.tensor_reduce(out=mp[:], in_=l2, axis=AX.X, op=ALU.max)),
             reads=("lg",), writes=("mp",))
        K.op("dve", (lambda l2=lgs, m2=mbc: NCV.tensor_tensor(
            out=epv, in0=l2, in1=m2, op=ALU.subtract)),
            reads=("lg", "mp"), writes=("ep",))
        K.op("act", (lambda: NCS.activation(out=ep[:], in_=ep[:], func=AF.Exp, bias=0.0, scale=1.0)),
             reads=("ep",), writes=("ep",))
        K.op("dve", (lambda: NCV.tensor_reduce(out=sp_[:], in_=epv, axis=AX.X, op=ALU.add)),
             reads=("ep",), writes=("sp",))
        K.op("dve", (lambda: NCV.reciprocal(out=sp_[:], in_=sp_[:])),
             reads=("sp",), writes=("sp",))
        K.op("dve", (lambda o2=out_sb, s3=spc: NCV.tensor_tensor(
            out=o2.rearrange("p (a n) -> p a n", a=8, n=4),
            in0=epv, in1=s3, op=ALU.mult)),
            reads=("ep", "sp"), writes=(name,))
    for e in range(8):
        K.op("pe", (lambda e2=e: NCT.transpose(out=psHT[:], in_=hsb[:, 16 * e2:16 * (e2 + 1)],
                                               identity=i128f[:])),
             reads=("hsb", "i128f_r", "psHTe"), writes=("psHT",))
        K.op("dve", (lambda e2=e: NCV.tensor_copy(out=hresT[:, 128 * e2:128 * (e2 + 1)], in_=psHT[:])),
             reads=("psHT",), writes=("hresT", "psHTe"))
        for (src, dst, rn) in ((hpre, hpreT, "hpreT"), (hpost, hpostT, "hpostT")):
            K.op("pe", (lambda e2=e, s3=src: NCT.transpose(
                out=psHT[0:4, :], in_=s3[:, 4 * e2:4 * (e2 + 1)], identity=i128f[:])),
                reads=("hpre", "hpost", "i128f_r", "psHTe"), writes=("psHT",))
            K.op("dve", (lambda e2=e, d2=dst: NCV.tensor_copy(
                out=d2[:, 128 * e2:128 * (e2 + 1)], in_=psHT[0:4, :])),
                reads=("psHT",), writes=(rn, "psHTe"))
    K.dma("pool", (lambda: NCP.dma_start(out=hscr[0:16, :], in_=hresT[:])),
          reads=("hresT",), writes=("hscr_res",))
    K.dma("pool", (lambda: NCP.dma_start(out=hscr[16:20, :], in_=hpreT[:])),
          reads=("hpreT",), writes=("hscr_pre",))
    K.dma("pool", (lambda: NCP.dma_start(out=hscr[20:24, :], in_=hpostT[:])),
          reads=("hpostT",), writes=("hscr_post",))
    if upto == "S":
        return _dump([("d_hresT", hresT), ("d_hpreT", hpreT), ("d_hpostT", hpostT),
                      ("d_lg", lg)])
    K.barrier()
    psSt.close()
    esS.close()

    yt = sb("yt", [128, 8 * 1024], BF16)
    esD1 = ExitStack()
    sb1 = lambda name, shape, dt: esD1.enter_context(nc.sbuf_tensor(name, shape, dt))
    wbraw = sb1("wbraw", [128, 1024], F32)
    wbrb = sb1("wbrb", [128, 8192], BF16)
    rowsl = sb1("rowsl", [1, 1024], BF16)
    hpre_bc = sb1("hpre_bc", [128, 4096], BF16)
    fftmp = sb1("fftmp", [128, 512], F32)
    xpre = sb1("xpre", [128, 8192], BF16)
    psD1 = ExitStack()
    psY = [psD1.enter_context(nc.psum_tensor(f"psY{p}", [128, 512], F32)) for p in range(2)]
    psBC = [psD1.enter_context(nc.psum_tensor(f"psBC{p}", [128, 512], F32)) for p in range(2)]
    for hf in range(8):
        K.dma("sync", (lambda h2=hf: NCY.dma_start(out=wbraw[:], in_=d["wbr"][:, 1024 * h2:1024 * (h2 + 1)])),
              reads=("wbraw_c",), writes=("wbraw",))
        K.op("pool", (lambda h2=hf: NCP.tensor_copy(out=wbrb[:, 1024 * h2:1024 * (h2 + 1)], in_=wbraw[:])),
             reads=("wbraw",), writes=("wbrb", "wbraw_c"))
    for n in range(4):
        K.dma("pool", (lambda n2=n: NCP.dma_start(out=rowsl[:], in_=hscr[16 + n2:17 + n2, :])),
              reads=("hscr_pre", "rowsl"), writes=("rowsl",))
        for th in range(2):
            bp = th % 2
            K.op("pe", (lambda t2=th, b2=bp: NCT.matmul(
                out=psBC[b2][:], lhsT=ones1h[:], rhs=rowsl[:, 512 * t2:512 * (t2 + 1)],
                start=True, stop=True)),
                reads=("rowsl", "ones1h_r", f"psBCe{bp}"), writes=(f"psBC{bp}",))
            K.op("act", (lambda n2=n, t2=th, b2=bp: NCS.copy(
                out=hpre_bc[:, 1024 * n2 + 512 * t2:1024 * n2 + 512 * (t2 + 1)], in_=psBC[b2][:])),
                reads=(f"psBC{bp}",), writes=("hpre_bc", f"psBCe{bp}"))
    for cc in range(8):
        for th in range(2):
            for n in range(4):
                jg = n * 8 + cc
                xsl = (1024 * jg + 512 * th, 1024 * jg + 512 * (th + 1))
                hsl = (1024 * n + 512 * th, 1024 * n + 512 * (th + 1))
                osl = (1024 * cc + 512 * th, 1024 * cc + 512 * (th + 1))
                if n == 0:
                    K.op("pool", (lambda x2=xsl, h2=hsl, o2=osl: NCP.tensor_tensor(
                        out=xpre[:, o2[0]:o2[1]], in0=xnt[:, x2[0]:x2[1]],
                        in1=hpre_bc[:, h2[0]:h2[1]], op=ALU.mult)),
                        reads=("xnt", "hpre_bc"), writes=("xpre",))
                else:
                    K.op("pool", (lambda x2=xsl, h2=hsl: NCP.tensor_tensor(
                        out=fftmp[:], in0=xnt[:, x2[0]:x2[1]],
                        in1=hpre_bc[:, h2[0]:h2[1]], op=ALU.mult)),
                        reads=("xnt", "hpre_bc"), writes=("fftmp",))
                    K.op("pool", (lambda o2=osl: NCP.tensor_tensor(
                        out=xpre[:, o2[0]:o2[1]], in0=xpre[:, o2[0]:o2[1]],
                        in1=fftmp[:], op=ALU.add)),
                        reads=("fftmp", "xpre"), writes=("xpre",))
    for co in range(8):
        for th in range(2):
            yp = (2 * co + th) % 2
            for cc in range(8):
                K.op("pe", (lambda c2=cc, o2=co, t2=th, y2=yp: NCT.matmul(
                    out=psY[y2][:], lhsT=wbrb[:, 1024 * o2 + 128 * c2:1024 * o2 + 128 * (c2 + 1)],
                    rhs=xpre[:, 1024 * c2 + 512 * t2:1024 * c2 + 512 * (t2 + 1)],
                    start=(c2 == 0), stop=(c2 == 7))),
                    reads=("wbrb", "xpre", f"psYe{yp}"), writes=(f"psY{yp}",))
            K.op("act", (lambda o2=co, t2=th, y2=yp: NCS.copy(
                out=yt[:, 1024 * o2 + 512 * t2:1024 * o2 + 512 * (t2 + 1)], in_=psY[y2][:])),
                reads=(f"psY{yp}",), writes=("yt", f"psYe{yp}"))
    if upto == "D1":
        return _dump([("d_yt", yt), ("d_xpre", xpre), ("d_hpre_bc", hpre_bc)])
    K.barrier()
    psD1.close()
    esD1.close()

    esD = ExitStack()
    sbD = lambda name, shape, dt: esD.enter_context(nc.sbuf_tensor(name, shape, dt))
    woraw = sbD("woraw", [128, 1024], F32)
    wob = [sbD(f"wob{p}", [128, 4096], BF16) for p in range(2)]
    bc5 = sbD("bc5", [128, 5 * 1024], BF16)
    rslot = [sbD(f"rslot{j}", [1, 1024], BF16) for j in range(1)]
    xtrow = [sbD("xtrow0", [128, 1024], F32)] * 2
    osb = [sbD(f"osb{p}", [128, 1024], F32) for p in range(2)]
    fft2 = [sbD(f"fft2{p}", [128, 1024], F32) for p in range(2)]
    psD = ExitStack()
    psW = [psD.enter_context(nc.psum_tensor(f"psW{p}", [128, 1024], F32)) for p in range(2)]
    psB2 = [psD.enter_context(nc.psum_tensor(f"psB2{p}", [128, 512], F32)) for p in range(2)]

    for n in range(4):
        for j in range(5):
            src_row = (n * 4 + j) if j < 4 else (20 + n)
            rdep = "hscr_res" if j < 4 else "hscr_post"
            K.dma("pool", (lambda s3=src_row: NCP.dma_start(
                out=rslot[0][:], in_=hscr[s3:s3 + 1, :])),
                reads=(rdep, "rslot0"), writes=("rslot0",))
            for th in range(2):
                bp = th % 2
                K.op("pe", (lambda t2=th, b2=bp: NCT.matmul(
                    out=psB2[b2][:], lhsT=ones1h[:], rhs=rslot[0][:, 512 * t2:512 * (t2 + 1)],
                    start=True, stop=True)),
                    reads=("rslot0", "ones1h_r", f"psB2e{bp}"), writes=(f"psB2{bp}",))
                K.op("act", (lambda j2=j, t2=th, b2=bp: NCS.copy(
                    out=bc5[:, 1024 * j2 + 512 * t2:1024 * j2 + 512 * (t2 + 1)], in_=psB2[b2][:])),
                    reads=(f"psB2{bp}",), writes=("bc5", f"psB2e{bp}"))
        for cc in range(8):
            of = n * 8 + cc
            wp = of % 2
            for hf in range(4):
                K.dma("sync", (lambda o2=of, h2=hf: NCY.dma_start(
                    out=woraw[:], in_=d["wo"][o2][:, 1024 * h2:1024 * (h2 + 1)])),
                    reads=("woraw_c",), writes=("woraw",))
                K.op("act", (lambda w2=wp, h2=hf: NCS.copy(
                    out=wob[w2][:, 1024 * h2:1024 * (h2 + 1)], in_=woraw[:])),
                    reads=("woraw",), writes=(f"wob{wp}", "woraw_c"))
            K.dma("pool", (lambda o2=of: NCP.dma_start(
                out=xtrow[0][:], in_=d["xqt"][128 * o2:128 * (o2 + 1), :])),
                reads=(), writes=("xtrow0",))
            op_ = of % 2
            for th in range(2):
                for c in range(32):
                    K.op("pe", (lambda c2=c, t2=th, p2=op_, w2=wp: NCT.matmul(
                        out=psW[p2][:, 512 * t2:512 * (t2 + 1)],
                        lhsT=wob[w2][:, 128 * c2:128 * (c2 + 1)],
                        rhs=qt[:, 1024 * c2 + 512 * t2:1024 * c2 + 512 * (t2 + 1)],
                        start=(c2 == 0), stop=(c2 == 31))),
                        reads=(f"wob{wp}", "qt", f"psWe{op_}"), writes=(f"psW{op_}",))
            K.op("dve", (lambda p2=op_, w2=wp: NCV.tensor_tensor(
                out=osb[p2][:], in0=psW[p2][:], in1=xtrow[w2][:], op=ALU.add)),
                reads=(f"psW{op_}", "xtrow0", f"osbc{op_}"), writes=(f"osb{op_}", f"psWe{op_}"))
            for j in range(4):
                jg = j * 8 + cc
                K.op("dve", (lambda j2=jg, j3=j, p2=op_: NCV.tensor_tensor(
                    out=fft2[p2][:], in0=xnt[:, 1024 * j2:1024 * (j2 + 1)],
                    in1=bc5[:, 1024 * j3:1024 * (j3 + 1)], op=ALU.mult)),
                    reads=("xnt", "bc5"), writes=(f"fft2{op_}",))
                K.op("dve", (lambda p2=op_: NCV.tensor_tensor(
                    out=osb[p2][:], in0=osb[p2][:], in1=fft2[p2][:], op=ALU.add)),
                    reads=(f"fft2{op_}",), writes=(f"osb{op_}",))
            K.op("dve", (lambda c2=cc, p2=op_: NCV.tensor_tensor(
                out=fft2[p2][:], in0=yt[:, 1024 * c2:1024 * (c2 + 1)],
                in1=bc5[:, 4096:5120], op=ALU.mult)),
                reads=("yt", "bc5"), writes=(f"fft2{op_}",))
            K.op("dve", (lambda p2=op_: NCV.tensor_tensor(
                out=osb[p2][:], in0=osb[p2][:], in1=fft2[p2][:], op=ALU.add)),
                reads=(f"fft2{op_}",), writes=(f"osb{op_}",))
            K.dma("act", (lambda o3=of, p2=op_: NCS.dma_start(
                out=d["yout"][o3], in_=osb[p2][:])),
                reads=(f"osb{op_}",), writes=(f"osbc{op_}",))

    K.barrier()
    final_qact = K.cnt["q_act"]
    K.ops["pool"].append(([("q_act", final_qact)], None, None))
    psD.close()
    esD.close()

    return _emit()


def _host_inputs(L0_attn_hidden_states, L0_attn_Wqkv, L0_attn_Wo, L0_ffn_W_res,
                 L0_ffn_W_pre, L0_ffn_W_post, L0_ffn_H_res_base, L0_ffn_H_pre_base,
                 L0_ffn_H_post_base, L0_ffn_gate, L0_ffn_W_branch):
    x = np.asarray(L0_attn_hidden_states, dtype=np.float32)
    wqkv = np.asarray(L0_attn_Wqkv, dtype=np.float32)
    wo = np.asarray(L0_attn_Wo, dtype=np.float32)

    def lay_blocks(w, nblocks):
        return np.ascontiguousarray(
            w.reshape(32, 128, nblocks, 128).transpose(2, 1, 0, 3).reshape(nblocks, 128, 32 * 128))

    wqkv_l = lay_blocks(wqkv, 34)
    wo_l = lay_blocks(wo, 32)
    wbr = np.asarray(L0_ffn_W_branch, dtype=np.float32)
    wbr_l = np.ascontiguousarray(
        wbr.reshape(8, 128, 8, 128).transpose(1, 2, 0, 3).reshape(128, 8192))
    w24 = np.concatenate([np.asarray(L0_ffn_W_res, dtype=np.float32),
                          np.asarray(L0_ffn_W_pre, dtype=np.float32),
                          np.asarray(L0_ffn_W_post, dtype=np.float32)], axis=0)
    w24_l = np.ascontiguousarray(w24.T.reshape(32, 128, 24).transpose(1, 0, 2).reshape(128, 32 * 24))
    bases = np.concatenate([np.asarray(L0_ffn_H_res_base, dtype=np.float32).reshape(-1),
                            np.asarray(L0_ffn_H_pre_base, dtype=np.float32),
                            np.asarray(L0_ffn_H_post_base, dtype=np.float32)]).reshape(1, 24)
    gate = np.asarray(L0_ffn_gate, dtype=np.float32).reshape(1, 1)

    kf = np.arange(512, dtype=np.float32)[None, :]
    kf2 = np.arange(1024, dtype=np.float32)[None, :]
    qp = np.arange(128, dtype=np.float32)[:, None]
    dplain = np.ascontiguousarray(np.broadcast_to(kf, (128, 512)).astype(np.float32))
    dplain2 = np.ascontiguousarray(np.broadcast_to(kf2, (128, 1024)).astype(np.float32))
    i128h = np.eye(128).astype(BF)
    i128f = np.eye(128).astype(np.float32)
    ones1h = np.ones((1, 128), dtype=BF)
    ones1f = np.ones((1, 128), dtype=np.float32)
    onescol_h = np.ones((128, 1), dtype=BF)

    in_maps = []
    for c in range(8):
        b, r = c // 4, c % 4
        rows = np.concatenate([np.arange(512 * e + 128 * r, 512 * e + 128 * r + 128)
                               for e in range(8)])
        xq = np.ascontiguousarray(x[b][rows])
        xqt = np.ascontiguousarray(xq.T)
        dmask = np.where(kf <= 128 * r + qp, kf, np.float32(NEG_BIG)).astype(np.float32)
        dmask2 = np.where((kf2 < 512) | (kf2 - 512 <= 128 * r + qp), kf2,
                          np.float32(NEG_BIG)).astype(np.float32)
        qcol = (qp + 128 * r).astype(np.float32)
        in_maps.append(dict(
            xkv=np.ascontiguousarray(x[b]), xq=xq, xqt=xqt, wqkv=wqkv_l, wo=wo_l, wbr=wbr_l,
            w24=w24_l, bases24=bases, gate=gate, dplain=dplain, dplain2=dplain2,
            dmask2=np.ascontiguousarray(dmask2),
            dmask=np.ascontiguousarray(dmask), qcol=np.ascontiguousarray(qcol),
            i128h=i128h, i128f=i128f, ones1h=ones1h, ones1f=ones1f,
            onescol_h=onescol_h))
    return in_maps


def kernel(**inputs):
    in_maps = _host_inputs(**inputs)
    nc = build_program()
    res = run_bass_kernel_spmd(nc, in_maps, core_ids=list(range(8)))
    out = np.empty((B, S, H), dtype=np.float32)
    for c in range(8):
        b, r = c // 4, c % 4
        y = np.asarray(res.results[c]["yout"])
        rows = y.transpose(2, 0, 1).reshape(1024, 4096)
        for e in range(8):
            out[b, 512 * e + 128 * r:512 * e + 128 * r + 128, :] = rows[128 * e:128 * (e + 1)]
    return out


# version 05
# speedup vs baseline: 1.3509x; 1.0578x over previous
import math
from contextlib import ExitStack

import numpy as np
import ml_dtypes

import concourse.bass as bass
import concourse.mybir as mybir
from concourse.bass_utils import run_bass_kernel_spmd

F32 = mybir.dt.float32
BF16 = mybir.dt.bfloat16
AF = mybir.ActivationFunctionType
ALU = mybir.AluOpType
AX = mybir.AxisListType
BF = ml_dtypes.bfloat16

B, S, H = 2, 4096, 4096
NH, HD = 32, 128
SCALE = float(HD) ** -0.5
LN_EPS = 1e-5
RMS_EPS = 1.1920929e-07
TAU_INV = float(1.0 / np.float32(0.1))
NEG_BIG = -1.0e30
SINK_ITERS = 50
NT_KV = S // 128
NT_Q = 8
NCH = H // 128


def _alibi_slopes(n):
    cp2 = 2 ** math.floor(math.log2(n))
    base = 2.0 ** (-(2.0 ** (-(math.log2(cp2) - 3))))
    slopes = base ** np.arange(1, cp2 + 1, dtype=np.float64)
    if cp2 != n:
        extra_base = 2.0 ** (-(2.0 ** (-(math.log2(2 * cp2) - 3))))
        rem = min(cp2, n - cp2)
        extra = extra_base ** np.arange(1, 1 + 2 * rem, 2, dtype=np.float64)
        slopes = np.concatenate([slopes, extra])
    return slopes.astype(np.float32)

SLOPES = _alibi_slopes(NH)


class Kb:
    ENGINES = ("pe", "act", "dve", "pool", "sync")
    ESEM = {"pe": "s_pe", "act": "s_act", "dve": "s_dve", "pool": "s_pool"}
    QSEM = {"sync": "q_sync", "pool": "q_pool", "act": "q_act", "dve": "q_dve"}

    def __init__(self):
        self.ops = {e: [] for e in self.ENGINES}
        self.cnt = {}
        for s in list(self.ESEM.values()) + list(self.QSEM.values()):
            self.cnt[s] = 0
        self.res = {}
        self.waited = {e: {} for e in self.ENGINES}

    def _need(self, eng, waits, token):
        sem, val = token
        if eng == "pe" and sem == self.ESEM.get("pe"):
            return
        if self.waited[eng].get(sem, 0) >= val:
            return
        self.waited[eng][sem] = val
        waits[sem] = max(waits.get(sem, 0), val)

    def _emit(self, eng, fn, reads, writes, inc_sem, inc_n):
        waits = {}
        for r in reads:
            ent = self.res.get(r)
            if ent and ent[0]:
                self._need(eng, waits, ent[0])
        for w in writes:
            ent = self.res.get(w)
            if ent:
                if ent[0]:
                    self._need(eng, waits, ent[0])
                for sem, val in ent[1].items():
                    self._need(eng, waits, (sem, val))
        self.cnt[inc_sem] += inc_n
        token = (inc_sem, self.cnt[inc_sem])
        for r in reads:
            ent = self.res.setdefault(r, [None, {}])
            ent[1][token[0]] = token[1]
        for w in writes:
            self.res[w] = [token, {}]
        self.ops[eng].append((sorted(waits.items()), fn, (inc_sem, inc_n)))
        return token

    def op(self, eng, fn, reads=(), writes=()):
        return self._emit(eng, fn, reads, writes, self.ESEM[eng], 1)

    def dma(self, eng, fn, reads=(), writes=()):
        return self._emit(eng, fn, reads, writes, self.QSEM[eng], 16)

    def barrier(self):
        snap = dict(self.cnt)
        for eng in self.ENGINES:
            waits = {}
            for sem, val in snap.items():
                if val == 0:
                    continue
                if eng in self.ESEM and self.ESEM[eng] == sem:
                    continue
                if self.waited[eng].get(sem, 0) >= val:
                    continue
                self.waited[eng][sem] = val
                waits[sem] = val
            if waits:
                self.ops[eng].append((sorted(waits.items()), None, None))


def build_program(upto="FULL"):
    nc = bass.Bass("TRN2", target_bir_lowering=False)
    K = Kb()
    sems = {}
    _emit_state = {}

    def _emit():
        def mk(eng_name, nce):
            def body(_):
                for waits, fn, inc in K.ops[eng_name]:
                    for s, v in waits:
                        nce.wait_ge(sems[s], v)
                    if fn is not None:
                        ins = fn()
                        if inc is not None:
                            ins.then_inc(sems[inc[0]], inc[1])
            return body
        with nc.Block() as block:
            block.tensor(mk("pe", _emit_state["NCT"]))
            block.scalar(mk("act", _emit_state["NCS"]))
            block.vector(mk("dve", _emit_state["NCV"]))
            block.gpsimd(mk("pool", _emit_state["NCP"]))
            block.sync(mk("sync", _emit_state["NCY"]))
        return nc

    def _dump(pairs):
        NCP = _emit_state["NCP"]
        K.barrier()
        for nm, handle in pairs:
            shp = list(handle[:].shape)
            dt_ = handle[:].dtype
            dout = nc.dram_tensor(nm, shp, dt_, kind="ExternalOutput").ap()
            K.dma("pool", (lambda d2=dout, h2=handle: NCP.dma_start(out=d2, in_=h2[:])),
                  reads=(), writes=(nm + "_out",))
        K.ops["pool"].append(([("q_pool", K.cnt["q_pool"])], None, None))
        return _emit()

    d = {}
    d["xkv"] = nc.dram_tensor("xkv", [S, H], F32, kind="ExternalInput").ap()
    d["xq"] = nc.dram_tensor("xq", [1024, H], F32, kind="ExternalInput").ap()
    d["xqt"] = nc.dram_tensor("xqt", [H, 1024], F32, kind="ExternalInput").ap()
    d["wqkv"] = nc.dram_tensor("wqkv", [34, 128, 32 * 128], F32, kind="ExternalInput").ap()
    d["wo"] = nc.dram_tensor("wo", [32, 128, 32 * 128], F32, kind="ExternalInput").ap()
    d["wbr"] = nc.dram_tensor("wbr", [128, 8192], F32, kind="ExternalInput").ap()
    d["w24"] = nc.dram_tensor("w24", [128, 32 * 24], F32, kind="ExternalInput").ap()
    d["bases24"] = nc.dram_tensor("bases24", [1, 24], F32, kind="ExternalInput").ap()
    d["gate"] = nc.dram_tensor("gate", [1, 1], F32, kind="ExternalInput").ap()
    d["dplain2"] = nc.dram_tensor("dplain2", [128, 1024], F32, kind="ExternalInput").ap()
    d["dmask2"] = nc.dram_tensor("dmask2", [128, 1024], F32, kind="ExternalInput").ap()
    d["qcol"] = nc.dram_tensor("qcol", [128, 1], F32, kind="ExternalInput").ap()
    d["i128h"] = nc.dram_tensor("i128h", [128, 128], BF16, kind="ExternalInput").ap()
    d["i128f"] = nc.dram_tensor("i128f", [128, 128], F32, kind="ExternalInput").ap()
    d["ones1h"] = nc.dram_tensor("ones1h", [1, 128], BF16, kind="ExternalInput").ap()
    d["ones1f"] = nc.dram_tensor("ones1f", [1, 128], F32, kind="ExternalInput").ap()
    d["onescol_h"] = nc.dram_tensor("onescol_h", [128, 1], BF16, kind="ExternalInput").ap()
    d["yout"] = nc.dram_tensor("yout", [32, 128, 1024], F32, kind="ExternalOutput").ap()
    hscr = nc.dram_tensor("hscr", [24, 1024], BF16, kind="Internal").ap()

    es = ExitStack()
    sb = lambda name, shape, dt: es.enter_context(nc.sbuf_tensor(name, shape, dt))

    xnt = sb("xnt", [128, NCH * 1024], BF16)
    esKV = ExitStack()
    kt = esKV.enter_context(nc.sbuf_tensor("kt", [128, S], BF16, side="right"))
    vt = esKV.enter_context(nc.sbuf_tensor("vt", [128, S], BF16, side="right"))
    dplain2 = esKV.enter_context(nc.sbuf_tensor("dplain2_s", [128, 1024], F32, side="right"))
    dmask2 = esKV.enter_context(nc.sbuf_tensor("dmask2_s", [128, 1024], F32, side="right"))

    qcol = sb("qcol_s", [128, 1], F32)
    i128h = sb("i128h_s", [128, 128], BF16)
    i128f = sb("i128f_s", [128, 128], F32)
    ones1h = sb("ones1h_s", [1, 128], BF16)
    ones1f = sb("ones1f_s", [1, 128], F32)
    onescol_h = sb("onescol_h_s", [128, 1], BF16)
    rstdF = sb("rstdF", [128, 8], F32)
    lgraw = sb("lgraw", [24, 1024], F32)
    negc = sb("negc", [128, 1], F32)
    bases_row = sb("bases_row", [1, 24], F32)
    gate_s = sb("gate_s", [1, 1], F32)

    for s in list(Kb.ESEM.values()) + list(Kb.QSEM.values()):
        sems[s] = es.enter_context(nc.semaphore(s))

    NCV = nc.vector
    NCS = nc.scalar
    NCT = nc.tensor
    NCP = nc.gpsimd
    NCY = nc.sync
    _emit_state.update(NCV=NCV, NCS=NCS, NCT=NCT, NCP=NCP, NCY=NCY)

    for name, dst in [("dplain2", dplain2),
                      ("dmask2", dmask2), ("qcol", qcol),
                      ("i128h", i128h), ("i128f", i128f), ("ones1h", ones1h),
                      ("ones1f", ones1f), ("onescol_h", onescol_h),
                      ("bases24", bases_row), ("gate", gate_s)]:
        src = d[name]
        K.dma("pool", (lambda dd=dst, ss=src: NCP.dma_start(out=dd[:], in_=ss)),
              reads=(), writes=(name + "_r",))

    esA = ExitStack()
    sbA = lambda name, shape, dt: esA.enter_context(nc.sbuf_tensor(name, shape, dt))
    xbuf = [sbA(f"xbuf{p}", [128, H], F32) for p in range(2)]
    xnbuf = [sbA(f"xnbuf{p}", [128, H], BF16) for p in range(2)]
    stage = [sbA(f"stage{p}", [128, H], BF16) for p in range(2)]
    junk = sbA("junk", [128, H], BF16)
    wkvraw = sbA("wkvraw", [128, 4096], F32)
    wkvb = sbA("wkvb", [128, 2 * 4096], BF16)
    w24raw = sbA("w24raw", [128, 32 * 24], F32)
    w24b = sbA("w24b", [128, 32 * 24], BF16)
    s1 = [sbA("s1" + str(p), [128, 1], F32) for p in range(2)]
    s2 = [sbA("s2" + str(p), [128, 1], F32) for p in range(2)]
    mn = [sbA("mn" + str(p), [128, 1], F32) for p in range(2)]
    mn2 = [sbA("mn2" + str(p), [128, 1], F32) for p in range(2)]
    var = [sbA("var" + str(p), [128, 1], F32) for p in range(2)]
    sd = [sbA("sd" + str(p), [128, 1], F32) for p in range(2)]
    rstd = [sbA("rstd" + str(p), [128, 1], F32) for p in range(2)]
    negmr = [sbA("negmr" + str(p), [128, 1], F32) for p in range(2)]
    s2f = sbA("s2f", [128, 1], F32)
    sdF = sbA("sdF", [128, 1], F32)
    psA = ExitStack()
    psT = [psA.enter_context(nc.psum_tensor(f"psT{p}", [128, 512], BF16)) for p in range(2)]
    psK = [psA.enter_context(nc.psum_tensor(f"psK{p}", [128, 128], F32)) for p in range(2)]
    psV = [psA.enter_context(nc.psum_tensor(f"psV{p}", [128, 128], F32)) for p in range(2)]
    psL = psA.enter_context(nc.psum_tensor("psL", [24, 128], F32))

    for bi, blk in enumerate((32, 33)):
        K.dma("sync", (lambda b2=blk: NCY.dma_start(out=wkvraw[:], in_=d["wqkv"][b2])),
              reads=("wkvb_c",), writes=("wkvraw",))
        K.op("act", (lambda b3=bi: NCS.copy(out=wkvb[:, b3 * 4096:(b3 + 1) * 4096], in_=wkvraw[:])),
             reads=("wkvraw",), writes=("wkvb", "wkvb_c"))
    K.dma("sync", (lambda: NCY.dma_start(out=w24raw[:], in_=d["w24"])),
          reads=(), writes=("w24raw",))
    K.op("act", (lambda: NCS.copy(out=w24b[:], in_=w24raw[:])),
         reads=("w24raw",), writes=("w24b",))

    def ln_tile(src_ap, p, tag):
        xb, xn = xbuf[p], xnbuf[p]
        s1_, s2_, mn_, mn2_ = s1[p], s2[p], mn[p], mn2[p]
        var_, sd_, rstd_, negmr_ = var[p], sd[p], rstd[p], negmr[p]
        rx, rn = f"xbuf{p}", f"xnbuf{p}"
        sfx = str(p)
        K.dma("sync", (lambda: NCY.dma_start(out=xb[:], in_=src_ap)),
              reads=(rn,), writes=(rx,))
        K.op("dve", (lambda: NCV.reduce_sum(out=s1_[:], in_=xb[:], axis=AX.X)),
             reads=(rx,), writes=("s1" + sfx,))
        K.op("act", (lambda: NCS.activation(out=junk[:], in_=xb[:], func=AF.Square,
                                            bias=0.0, scale=1.0, accum_out=s2_[:])),
             reads=(rx, "junk"), writes=("junk", "s2" + sfx))
        K.op("dve", (lambda: NCV.tensor_scalar_mul(out=mn_[:], in0=s1_[:], scalar1=1.0 / H)),
             reads=("s1" + sfx,), writes=("mn" + sfx,))
        K.op("dve", (lambda: NCV.tensor_mul(out=mn2_[:], in0=mn_[:], in1=mn_[:])),
             reads=("mn" + sfx,), writes=("mn2" + sfx,))
        K.op("dve", (lambda: NCV.scalar_tensor_tensor(out=var_[:], in0=s2_[:], scalar=1.0 / H,
                                                      in1=mn2_[:], op0=ALU.mult, op1=ALU.subtract)),
             reads=("s2" + sfx, "mn2" + sfx), writes=("var" + sfx,))
        K.op("dve", (lambda: NCV.tensor_scalar_add(out=var_[:], in0=var_[:], scalar1=LN_EPS)),
             reads=("var" + sfx,), writes=("var" + sfx,))
        K.op("act", (lambda: NCS.activation(out=sd_[:], in_=var_[:], func=AF.Sqrt, bias=0.0, scale=1.0)),
             reads=("var" + sfx,), writes=("sd" + sfx,))
        K.op("dve", (lambda: NCV.reciprocal(out=rstd_[:], in_=sd_[:])),
             reads=("sd" + sfx,), writes=("rstd" + sfx,))
        K.op("dve", (lambda: NCV.scalar_tensor_tensor(out=negmr_[:], in0=mn_[:], scalar=-1.0,
                                                      in1=rstd_[:], op0=ALU.mult, op1=ALU.mult)),
             reads=("mn" + sfx, "rstd" + sfx), writes=("negmr" + sfx,))
        K.op("act", (lambda: NCS.activation(out=xn[:], in_=xb[:], func=AF.Identity,
                                            bias=negmr_[:, 0:1], scale=rstd_[:, 0:1])),
             reads=(rx, "rstd" + sfx, "negmr" + sfx), writes=(rn,))
        return xn, rn

    for i in range(NT_KV):
        p = i % 2
        xn, rn = ln_tile(d["xkv"][128 * i:128 * (i + 1), :], p, f"a1_{i}")
        st, rs = stage[p], f"stage{p}"
        for g in range(8):
            tp = g % 2
            for u in range(4):
                j = 4 * g + u
                K.op("pe", (lambda xn2=xn, j2=j, t2=tp: NCT.transpose(
                    out=psT[t2][:, 128 * (j2 % 4):128 * (j2 % 4 + 1)],
                    in_=xn2[:, 128 * j2:128 * (j2 + 1)], identity=i128h[:])),
                    reads=(rn, "i128h_r", f"psTe{tp}"), writes=(f"psT{tp}",))
            K.op("dve", (lambda s3=st, g2=g, t2=tp: NCV.tensor_copy(
                out=s3[:, 512 * g2:512 * (g2 + 1)], in_=psT[t2][:])),
                reads=(f"psT{tp}",), writes=(rs, f"psTe{tp}"))
        kp = i % 2
        for c in range(32):
            K.op("pe", (lambda c2=c, s3=st, k2=kp: NCT.matmul(
                out=psK[k2][:], lhsT=wkvb[:, 128 * c2:128 * (c2 + 1)],
                rhs=s3[:, 128 * c2:128 * (c2 + 1)],
                start=(c2 == 0), stop=(c2 == 31))),
                reads=(rs, "wkvb", f"psKe{kp}"), writes=(f"psK{kp}",))
        K.op("act", (lambda i2=i, k2=kp: NCS.copy(out=kt[:, 128 * i2:128 * (i2 + 1)], in_=psK[k2][:])),
             reads=(f"psK{kp}",), writes=("kt", f"psKe{kp}"))
        for c in range(32):
            K.op("pe", (lambda c2=c, s3=st, k2=kp: NCT.matmul(
                out=psV[k2][:], lhsT=s3[:, 128 * c2:128 * (c2 + 1)],
                rhs=wkvb[:, 4096 + 128 * c2:4096 + 128 * (c2 + 1)],
                start=(c2 == 0), stop=(c2 == 31))),
                reads=(rs, "wkvb", f"psVe{kp}"), writes=(f"psV{kp}",))
        K.op("act", (lambda i2=i, k2=kp: NCS.copy(out=vt[:, 128 * i2:128 * (i2 + 1)], in_=psV[k2][:])),
             reads=(f"psV{kp}",), writes=("vt", f"psVe{kp}"))

    for e in range(NT_Q):
        p = e % 2
        xn, rn = ln_tile(d["xq"][128 * e:128 * (e + 1), :], p, f"a2_{e}")
        for g in range(8):
            tp = g % 2
            for u in range(4):
                j = 4 * g + u
                K.op("pe", (lambda xn2=xn, j2=j, t2=tp: NCT.transpose(
                    out=psT[t2][:, 128 * (j2 % 4):128 * (j2 % 4 + 1)],
                    in_=xn2[:, 128 * j2:128 * (j2 + 1)], identity=i128h[:])),
                    reads=(rn, "i128h_r", f"psTe{tp}"), writes=(f"psT{tp}",))
            for u in range(4):
                j = 4 * g + u
                K.op("dve", (lambda j2=j, e2=e, t2=tp, u2=u: NCV.tensor_copy(
                    out=xnt[:, 1024 * j2 + 128 * e2:1024 * j2 + 128 * (e2 + 1)],
                    in_=psT[t2][:, 128 * u2:128 * (u2 + 1)])),
                    reads=(f"psT{tp}",),
                    writes=("xnt", f"psTe{tp}") if u == 3 else ("xnt",))
        K.op("act", (lambda xn2=xn: NCS.activation(out=junk[:], in_=xn2[:], func=AF.Square,
                                                   bias=0.0, scale=1.0, accum_out=s2f[:])),
             reads=(rn,), writes=("junk", "s2f"))
        K.op("dve", (lambda: NCV.tensor_scalar(out=s2f[:], in0=s2f[:], scalar1=1.0 / H,
                                               scalar2=RMS_EPS, op0=ALU.mult, op1=ALU.add)),
             reads=("s2f",), writes=("s2f",))
        K.op("act", (lambda: NCS.activation(out=sdF[:], in_=s2f[:], func=AF.Sqrt, bias=0.0, scale=1.0)),
             reads=("s2f",), writes=("sdF",))
        K.op("dve", (lambda e2=e: NCV.reciprocal(out=rstdF[:, e2:e2 + 1], in_=sdF[:])),
             reads=("sdF",), writes=("rstdF",))
        for c in range(32):
            K.op("pe", (lambda c2=c, e2=e: NCT.matmul(
                out=psL[:], lhsT=w24b[:, 24 * c2:24 * (c2 + 1)],
                rhs=xnt[:, 1024 * c2 + 128 * e2:1024 * c2 + 128 * (e2 + 1)],
                start=(c2 == 0), stop=(c2 == 31))),
                reads=("xnt", "w24b", "psLe"), writes=("psL",))
        K.op("act", (lambda e2=e: NCS.copy(out=lgraw[:, 128 * e2:128 * (e2 + 1)], in_=psL[:])),
             reads=("psL",), writes=("lgraw", "psLe"))

    if upto == "A":
        return _dump([("d_xnt", xnt), ("d_kt", kt), ("d_vt", vt),
                      ("d_lgraw", lgraw), ("d_rstdF", rstdF)])
    K.barrier()
    psA.close()
    esA.close()

    qt = sb("qt", [128, NH * 1024], BF16)
    esB = ExitStack()
    sbB = lambda name, shape, dt: esB.enter_context(nc.sbuf_tensor(name, shape, dt))
    esBR = ExitStack()
    wqraw = [esBR.enter_context(nc.sbuf_tensor(f"wqraw{p}", [128, 2048], F32, side="right"))
             for p in range(2)]
    wqb = [esBR.enter_context(nc.sbuf_tensor(f"wqb{p}", [128, 4096], BF16, side="right"))
           for p in range(2)]
    qsq = esBR.enter_context(nc.sbuf_tensor("qsq", [128, 512], BF16, side="right"))
    q2row = esBR.enter_context(nc.sbuf_tensor("q2row", [1, 512], F32, side="right"))
    q2cols = esBR.enter_context(nc.sbuf_tensor("q2cols", [1, 80], F32, side="right"))
    tiny = esBR.enter_context(nc.sbuf_tensor("tiny", [1, 4], F32, side="right"))
    psB = ExitStack()
    psQ = [psB.enter_context(nc.psum_tensor(f"psQ{p}", [128, 512], F32)) for p in range(2)]
    ps12 = psB.enter_context(nc.psum_tensor("ps12", [1, 512], F32))
    psC1 = psB.enter_context(nc.psum_tensor("psC1", [128, 1], F32))

    for h in range(NH):
        wp = h % 2
        for hf in range(2):
            qp2 = hf % 2
            K.dma("sync", (lambda h2=h, f2=hf, q2=qp2: NCY.dma_start(
                out=wqraw[q2][:], in_=d["wqkv"][h2][:, 2048 * f2:2048 * (f2 + 1)])),
                reads=(f"wqrawc{qp2}",), writes=(f"wqraw{qp2}",))
            K.op("act", (lambda w2=wp, f2=hf, q2=qp2: NCS.copy(
                out=wqb[w2][:, 2048 * f2:2048 * (f2 + 1)], in_=wqraw[q2][:])),
                reads=(f"wqraw{qp2}",), writes=(f"wqb{wp}", f"wqrawc{qp2}"))
        for th in range(2):
            qp_ = (2 * h + th) % 2
            for c in range(32):
                K.op("pe", (lambda c2=c, t2=th, q2=qp_, w2=wp: NCT.matmul(
                    out=psQ[q2][:], lhsT=wqb[w2][:, 128 * c2:128 * (c2 + 1)],
                    rhs=xnt[:, 1024 * c2 + 512 * t2:1024 * c2 + 512 * (t2 + 1)],
                    start=(c2 == 0), stop=(c2 == 31))),
                    reads=(f"wqb{wp}", "xnt", f"psQe{qp_}"), writes=(f"psQ{qp_}",))
            K.op("act", (lambda h2=h, t2=th, q2=qp_: NCS.copy(
                out=qt[:, 1024 * h2 + 512 * t2:1024 * h2 + 512 * (t2 + 1)], in_=psQ[q2][:])),
                reads=(f"psQ{qp_}",), writes=("qt", f"psQe{qp_}"))
        for th in range(2):
            K.op("act", (lambda h2=h, t2=th: NCS.activation(
                out=qsq[:], in_=qt[:, 1024 * h2 + 512 * t2:1024 * h2 + 512 * (t2 + 1)],
                func=AF.Square, bias=0.0, scale=1.0)),
                reads=("qt", "qsq_c"), writes=("qsq",))
            K.op("pe", (lambda t2=th: NCT.matmul(
                out=ps12[:], lhsT=onescol_h[:], rhs=qsq[:],
                start=True, stop=True)),
                reads=("qsq", "onescol_h_r", "ps12e"), writes=("ps12", "qsq_c"))
            K.op("act", (lambda: NCS.copy(out=q2row[:], in_=ps12[:])),
                 reads=("ps12", "q2row_c"), writes=("q2row", "ps12e"))
            K.op("dve", (lambda h2=h, t2=th: NCV.reduce_max(
                out=q2cols[:, 2 * h2 + t2:2 * h2 + t2 + 1], in_=q2row[:], axis=AX.X)),
                reads=("q2row",), writes=("q2cols", "q2row_c"))
    for g in range(8):
        K.op("act", (lambda g2=g: NCS.activation(out=qsq[:], in_=kt[:, 512 * g2:512 * (g2 + 1)],
                                                 func=AF.Square, bias=0.0, scale=1.0)),
             reads=("kt", "qsq_c"), writes=("qsq",))
        K.op("pe", (lambda: NCT.matmul(
            out=ps12[:], lhsT=onescol_h[:], rhs=qsq[:],
            start=True, stop=True)),
            reads=("qsq", "onescol_h_r", "ps12e"), writes=("ps12", "qsq_c"))
        K.op("act", (lambda: NCS.copy(out=q2row[:], in_=ps12[:])),
             reads=("ps12", "q2row_c"), writes=("q2row", "ps12e"))
        K.op("dve", (lambda g2=g: NCV.reduce_max(
            out=q2cols[:, 64 + g2:65 + g2], in_=q2row[:], axis=AX.X)),
            reads=("q2row",), writes=("q2cols", "q2row_c"))
    K.op("dve", (lambda: NCV.reduce_max(out=tiny[:, 0:1], in_=q2cols[:, 0:64], axis=AX.X)),
         reads=("q2cols",), writes=("t_q2m",))
    K.op("dve", (lambda: NCV.reduce_max(out=tiny[:, 1:2], in_=q2cols[:, 64:72], axis=AX.X)),
         reads=("q2cols",), writes=("t_k2m",))
    K.op("dve", (lambda: NCV.tensor_mul(out=tiny[:, 2:3], in0=tiny[:, 0:1], in1=tiny[:, 1:2])),
         reads=("t_q2m", "t_k2m"), writes=("t_prod",))
    K.op("act", (lambda: NCS.activation(out=tiny[:, 3:4], in_=tiny[:, 2:3], func=AF.Sqrt,
                                        bias=0.0, scale=1.0)),
         reads=("t_prod",), writes=("t_sq",))
    K.op("dve", (lambda: NCV.tensor_scalar(out=tiny[:, 2:3], in0=tiny[:, 3:4], scalar1=SCALE,
                                           scalar2=3.0, op0=ALU.mult, op1=ALU.add)),
         reads=("t_sq",), writes=("t_C",))
    K.op("pe", (lambda: NCT.matmul(out=psC1[:], lhsT=ones1f[:], rhs=tiny[:, 2:3],
                                   start=True, stop=True)),
         reads=("t_C", "ones1f_r"), writes=("psC1",))
    K.op("act", (lambda: NCS.activation(out=negc[:], in_=psC1[:], func=AF.Copy, bias=0.0, scale=-1.0)),
         reads=("psC1",), writes=("negc",))

    if upto == "B":
        return _dump([("d_qt", qt), ("d_negc", negc), ("d_lgraw", lgraw)])
    K.barrier()
    psB.close()
    esB.close()
    esBR.close()

    esC = ExitStack()
    sbC = lambda name, shape, dt: esC.enter_context(nc.sbuf_tensor(name, shape, dt))
    tsc = [sbC(f"tsc{p}", [128, 1024], F32) for p in range(2)]
    pstore = [sbC(f"pstore{p}", [128, 4096], BF16) for p in range(2)]
    pts = [sbC(f"pts{p}", [128, 1024], BF16) for p in range(2)]
    rows = [sbC(f"rows{p}", [128, 4], F32) for p in range(2)]
    rs_ = sbC("rs", [128, 1], F32)
    rrec = [sbC(f"rrec{p}", [128, 1], F32) for p in range(2)]
    basehe = sbC("basehe", [128, 1], F32)
    tbcol = sbC("tbcol", [128, 1], F32)
    biast = [sbC(f"biast{p}", [128, 1], F32) for p in range(2)]
    psC = ExitStack()
    psS2 = [psC.enter_context(nc.psum_tensor(f"psS2{p}", [128, 1024], F32)) for p in range(2)]
    psPT = [psC.enter_context(nc.psum_tensor(f"psPT{p}", [128, 1024], BF16)) for p in range(2)]
    psO = [psC.enter_context(nc.psum_tensor(f"psO{p}", [128, 128], F32)) for p in range(2)]

    idxs = [(h, e) for h in range(NH) for e in range(NT_Q)]
    ctr = {"p1": 0, "p2": 0}

    def pair_width(nk, p):
        return 512 if 2 * p + 1 >= nk else 1024

    def c_phase1(i):
        h, e = idxs[i]
        ip = i % 2
        slope = float(SLOPES[h])
        sprime = slope / SCALE
        qlo = 1024 * h + 128 * e
        nk = e + 1
        npairs = (nk + 1) // 2
        K.op("dve", (lambda e2=e, sl=slope: NCV.tensor_scalar(
            out=tbcol[:], in0=qcol[:], scalar1=float(512 * e2), scalar2=-sl,
            op0=ALU.add, op1=ALU.mult)),
            reads=("qcol_r",), writes=("tbcol",))
        K.op("dve", (lambda: NCV.scalar_tensor_tensor(
            out=basehe[:], in0=tbcol[:], scalar=1.0, in1=negc[:],
            op0=ALU.mult, op1=ALU.add)),
            reads=("tbcol", "negc"), writes=("basehe",))
        for p in range(npairs):
            kb0 = 2 * p
            w = pair_width(nk, p)
            pp = ctr["p1"] % 2
            ctr["p1"] += 1
            for s_ in range(w // 512):
                kb = kb0 + s_
                K.op("pe", (lambda q2=qlo, k2=kb, p2=pp, s2=s_: NCT.matmul(
                    out=psS2[p2][:, 512 * s2:512 * (s2 + 1)], lhsT=qt[:, q2:q2 + 128],
                    rhs=kt[:, 512 * k2:512 * (k2 + 1)], start=True, stop=True)),
                    reads=("qt", "kt", f"psS2e{pp}"), writes=(f"psS2{pp}",))
            if w == 512:
                dsl, dr, kbias = dmask2[:, 512:1024], "dmask2_r", kb0 - 1
            elif kb0 + 2 == nk:
                dsl, dr, kbias = dmask2[:, 0:1024], "dmask2_r", kb0
            else:
                dsl, dr, kbias = dplain2[:, 0:1024], "dplain2_r", kb0
            K.op("dve", (lambda k2=kbias, p2=pp, sl=slope: NCV.tensor_scalar_add(
                out=biast[p2][:], in0=basehe[:], scalar1=float(sl * 512 * k2))),
                reads=("basehe",), writes=(f"biast{pp}",))
            K.op("dve", (lambda d2=dsl, p2=pp, sp=sprime, w2=w: NCV.scalar_tensor_tensor(
                out=tsc[p2][:, 0:w2], in0=d2, scalar=sp, in1=psS2[p2][:, 0:w2],
                op0=ALU.mult, op1=ALU.add)),
                reads=(dr, f"psS2{pp}", f"tsce{pp}"),
                writes=(f"tsc{pp}", f"psS2e{pp}"))
            K.op("act", (lambda p2=pp, p3=p, i2=ip, w2=w: NCS.activation(
                out=pstore[i2][:, 1024 * p3:1024 * p3 + w2], in_=tsc[p2][:, 0:w2], func=AF.Exp,
                bias=biast[p2][:, 0:1], scale=SCALE,
                accum_out=rows[i2][:, p3:p3 + 1])),
                reads=(f"tsc{pp}", f"biast{pp}", f"rows{ip}_c"),
                writes=(f"pstore{ip}", f"rows{ip}", f"tsce{pp}"))
        K.op("dve", (lambda i2=ip, n2=npairs: NCV.reduce_sum(
            out=rs_[:], in_=rows[i2][:, 0:n2], axis=AX.X)),
            reads=(f"rows{ip}",), writes=("rs", f"rows{ip}_c"))
        K.op("dve", (lambda i2=ip: NCV.reciprocal(out=rrec[i2][:], in_=rs_[:])),
             reads=("rs",), writes=(f"rrec{ip}",))

    def c_phase2(i):
        h, e = idxs[i]
        ip = i % 2
        oe = i % 2
        qlo = 1024 * h + 128 * e
        nk = e + 1
        npairs = (nk + 1) // 2
        nchunks = 4 * nk
        for p in range(npairs):
            w = pair_width(nk, p)
            K.op("dve", (lambda i2=ip, p3=p, w2=w: NCV.tensor_scalar_mul(
                out=pstore[i2][:, 1024 * p3:1024 * p3 + w2],
                in0=pstore[i2][:, 1024 * p3:1024 * p3 + w2], scalar1=rrec[i2][:, 0:1])),
                reads=(f"pstore{ip}", f"rrec{ip}"), writes=(f"pstore{ip}",))
            pt = ctr["p2"] % 2
            ctr["p2"] += 1
            for u in range(w // 128):
                K.op("pe", (lambda i2=ip, p3=p, u2=u, t2=pt: NCT.transpose(
                    out=psPT[t2][:, 128 * u2:128 * (u2 + 1)],
                    in_=pstore[i2][:, 1024 * p3 + 128 * u2:1024 * p3 + 128 * (u2 + 1)],
                    identity=i128h[:])),
                    reads=(f"pstore{ip}", "i128h_r", f"psPTe{pt}"), writes=(f"psPT{pt}",))
            K.op("act", (lambda t2=pt, w2=w: NCS.copy(out=pts[t2][:, 0:w2], in_=psPT[t2][:, 0:w2])),
                 reads=(f"psPT{pt}",), writes=(f"pts{pt}", f"psPTe{pt}"))
            for u in range(w // 128):
                kc = 8 * p + u
                st = (p == 0 and u == 0)
                sp2 = (kc == nchunks - 1)
                K.op("pe", (lambda t2=pt, u2=u, k3=kc, o2=oe, s3=st, s4=sp2: NCT.matmul(
                    out=psO[o2][:], lhsT=vt[:, 128 * k3:128 * (k3 + 1)],
                    rhs=pts[t2][:, 128 * u2:128 * (u2 + 1)], start=s3, stop=s4)),
                    reads=(f"pts{pt}", "vt", f"psOe{oe}"), writes=(f"psO{oe}",))
        K.op("act", (lambda q2=qlo, o2=oe: NCS.copy(out=qt[:, q2:q2 + 128], in_=psO[o2][:])),
             reads=(f"psO{oe}",), writes=("qt", f"psOe{oe}"))

    for i in range(len(idxs)):
        c_phase1(i)
        if i > 0:
            c_phase2(i - 1)
    c_phase2(len(idxs) - 1)

    if upto == "C":
        return _dump([("d_qt", qt)])
    K.barrier()
    psC.close()
    esC.close()
    esKV.close()

    esS = ExitStack()
    sbS = lambda name, shape, dt: esS.enter_context(nc.sbuf_tensor(name, shape, dt))
    lg = sbS("lg", [128, 192], F32)
    bb24 = sbS("bb24", [128, 24], F32)
    zres = sbS("zres", [128, 128], F32)
    asb = sbS("asb", [128, 128], F32)
    bsb = sbS("bsb", [128, 128], F32)
    esb = sbS("esb", [128, 128], F32)
    hsb = sbS("hsb", [128, 128], F32)
    msb = sbS("msb", [128, 32], F32)
    lsb = sbS("lsb", [128, 32], F32)
    ssb = sbS("ssb", [128, 32], F32)
    ubuf = sbS("ubuf", [128, 32], F32)
    vbuf = sbS("vbuf", [128, 32], F32)
    mp = sbS("mp", [128, 8], F32)
    ep = sbS("ep", [128, 32], F32)
    sp_ = sbS("sp", [128, 8], F32)
    hpre = sbS("hpre", [128, 32], F32)
    hpost = sbS("hpost", [128, 32], F32)
    gcol = sbS("gcol", [128, 1], F32)
    omg = sbS("omg", [128, 1], F32)
    hresT = sbS("hresT", [16, 1024], BF16)
    hpreT = sbS("hpreT", [4, 1024], BF16)
    hpostT = sbS("hpostT", [4, 1024], BF16)
    psSt = ExitStack()
    psLT = psSt.enter_context(nc.psum_tensor("psLT", [128, 24], F32))
    psHT = psSt.enter_context(nc.psum_tensor("psHT", [16, 128], F32))
    psG = psSt.enter_context(nc.psum_tensor("psG", [128, 1], F32))

    K.op("pe", (lambda: NCT.matmul(out=psLT[:], lhsT=ones1f[:], rhs=bases_row[:],
                                   start=True, stop=True)),
         reads=("bases24_r", "ones1f_r", "psLTe"), writes=("psLT",))
    K.op("act", (lambda: NCS.copy(out=bb24[:], in_=psLT[:])),
         reads=("psLT",), writes=("bb24", "psLTe"))
    for e in range(8):
        K.op("pe", (lambda e2=e: NCT.transpose(out=psLT[:], in_=lgraw[:, 128 * e2:128 * (e2 + 1)],
                                               identity=i128f[0:24, 0:24])),
             reads=("lgraw", "i128f_r", "psLTe"), writes=("psLT",))
        K.op("dve", (lambda e2=e: NCV.scalar_tensor_tensor(
            out=lg[:, 24 * e2:24 * (e2 + 1)], in0=psLT[:], scalar=rstdF[:, e2:e2 + 1],
            in1=bb24[:], op0=ALU.mult, op1=ALU.add)),
            reads=("psLT", "rstdF", "bb24"), writes=("lg", "psLTe"))
    lgv = lg.rearrange("p (a x) -> p a x", a=8, x=24)
    K.op("dve", (lambda: NCV.tensor_scalar_mul(
        out=zres.rearrange("p (a x) -> p a x", a=8, x=16),
        in0=lgv[:, :, 0:16], scalar1=TAU_INV)),
        reads=("lg",), writes=("zres",))
    K.op("dve", (lambda: NCV.memset(ubuf[:], 0.0)), reads=(), writes=("ubuf",))
    K.op("dve", (lambda: NCV.memset(vbuf[:], 0.0)), reads=(), writes=("vbuf",))

    z4 = zres.rearrange("p (a i j) -> p a i j", a=8, i=4, j=4)
    a4 = asb.rearrange("p (a i j) -> p a i j", a=8, i=4, j=4)
    b4 = bsb.rearrange("p (a i j) -> p a i j", a=8, i=4, j=4)
    e4 = esb.rearrange("p (a i j) -> p a i j", a=8, i=4, j=4)
    uv = ubuf.rearrange("p (a i) -> p a i", a=8, i=4)
    vv = vbuf.rearrange("p (a j) -> p a j", a=8, j=4)
    mv = msb.rearrange("p (a i) -> p a i", a=8, i=4)
    sv = ssb.rearrange("p (a i) -> p a i", a=8, i=4)
    vb_bc = vv.unsqueeze(2).broadcast_to([128, 8, 4, 4])
    ub_bc = uv.unsqueeze(3).broadcast_to([128, 8, 4, 4])
    mb_bc = mv.unsqueeze(3).broadcast_to([128, 8, 4, 4])
    m2_bc = mv.unsqueeze(2).broadcast_to([128, 8, 4, 4])
    a4_ji = asb.rearrange("p (a i j) -> p a j i", a=8, i=4, j=4)
    e4_ji = esb.rearrange("p (a i j) -> p a j i", a=8, i=4, j=4)

    for it in range(SINK_ITERS):
        K.op("dve", (lambda: NCV.tensor_tensor(out=a4, in0=z4, in1=vb_bc, op=ALU.subtract)),
             reads=("zres", "vbuf"), writes=("asb",))
        K.op("dve", (lambda: NCV.tensor_reduce(out=mv, in_=a4, axis=AX.X, op=ALU.max)),
             reads=("asb",), writes=("msb",))
        K.op("dve", (lambda: NCV.tensor_tensor(out=b4, in0=a4, in1=mb_bc, op=ALU.subtract)),
             reads=("asb", "msb"), writes=("bsb",))
        K.op("act", (lambda: NCS.activation(out=esb[:], in_=bsb[:], func=AF.Exp, bias=0.0, scale=1.0)),
             reads=("bsb",), writes=("esb",))
        K.op("dve", (lambda: NCV.tensor_reduce(out=sv, in_=e4, axis=AX.X, op=ALU.add)),
             reads=("esb",), writes=("ssb",))
        K.op("act", (lambda: NCS.activation(out=lsb[:], in_=ssb[:], func=AF.Ln, bias=0.0, scale=1.0)),
             reads=("ssb",), writes=("lsb",))
        K.op("dve", (lambda: NCV.tensor_add(out=ubuf[:], in0=msb[:], in1=lsb[:])),
             reads=("msb", "lsb"), writes=("ubuf",))
        K.op("dve", (lambda: NCV.tensor_tensor(out=a4, in0=z4, in1=ub_bc, op=ALU.subtract)),
             reads=("zres", "ubuf"), writes=("asb",))
        K.op("dve", (lambda: NCV.tensor_reduce(out=mv, in_=a4_ji, axis=AX.X, op=ALU.max)),
             reads=("asb",), writes=("msb",))
        K.op("dve", (lambda: NCV.tensor_tensor(out=b4, in0=a4, in1=m2_bc, op=ALU.subtract)),
             reads=("asb", "msb"), writes=("bsb",))
        K.op("act", (lambda: NCS.activation(out=esb[:], in_=bsb[:], func=AF.Exp, bias=0.0, scale=1.0)),
             reads=("bsb",), writes=("esb",))
        K.op("dve", (lambda: NCV.tensor_reduce(out=sv, in_=e4_ji, axis=AX.X, op=ALU.add)),
             reads=("esb",), writes=("ssb",))
        K.op("act", (lambda: NCS.activation(out=lsb[:], in_=ssb[:], func=AF.Ln, bias=0.0, scale=1.0)),
             reads=("ssb",), writes=("lsb",))
        K.op("dve", (lambda: NCV.tensor_add(out=vbuf[:], in0=msb[:], in1=lsb[:])),
             reads=("msb", "lsb"), writes=("vbuf",))
    K.op("dve", (lambda: NCV.tensor_tensor(out=a4, in0=z4, in1=ub_bc, op=ALU.subtract)),
         reads=("zres", "ubuf"), writes=("asb",))
    K.op("dve", (lambda: NCV.tensor_tensor(out=b4, in0=a4, in1=vb_bc, op=ALU.subtract)),
         reads=("asb", "vbuf"), writes=("bsb",))
    K.op("act", (lambda: NCS.activation(out=hsb[:], in_=bsb[:], func=AF.Exp, bias=0.0, scale=1.0)),
         reads=("bsb",), writes=("hsb",))
    K.op("act", (lambda: NCS.activation(out=gate_s[:], in_=gate_s[:], func=AF.Sigmoid, bias=0.0, scale=1.0)),
         reads=("gate_r",), writes=("gate_v",))
    K.op("pe", (lambda: NCT.matmul(out=psG[:], lhsT=ones1f[:], rhs=gate_s[:], start=True, stop=True)),
         reads=("gate_v", "ones1f_r"), writes=("psG",))
    K.op("act", (lambda: NCS.copy(out=gcol[:], in_=psG[:])),
         reads=("psG",), writes=("gcol",))
    K.op("dve", (lambda: NCV.tensor_scalar(out=omg[:], in0=gcol[:], scalar1=-1.0, scalar2=1.0,
                                           op0=ALU.mult, op1=ALU.add)),
         reads=("gcol",), writes=("omg",))
    K.op("dve", (lambda: NCV.tensor_scalar_mul(out=hsb[:], in0=hsb[:], scalar1=gcol[:, 0:1])),
         reads=("hsb", "gcol"), writes=("hsb",))
    hdg = hsb.rearrange("p (a x) -> p a x", a=8, x=16)
    for dgi in range(4):
        K.op("dve", (lambda d2=dgi: NCV.tensor_scalar_add(
            out=hdg[:, :, 5 * d2:5 * d2 + 1], in0=hdg[:, :, 5 * d2:5 * d2 + 1],
            scalar1=omg[:, 0:1])),
            reads=("hsb", "omg"), writes=("hsb",))
    for (name, off, out_sb) in (("hpre", 16, hpre), ("hpost", 20, hpost)):
        lgs = lgv[:, :, off:off + 4]
        mbc = mp[:].unsqueeze(2).broadcast_to([128, 8, 4])
        spc = sp_[:].unsqueeze(2).broadcast_to([128, 8, 4])
        epv = ep.rearrange("p (a n) -> p a n", a=8, n=4)
        K.op("dve", (lambda l2=lgs: NCV.tensor_reduce(out=mp[:], in_=l2, axis=AX.X, op=ALU.max)),
             reads=("lg",), writes=("mp",))
        K.op("dve", (lambda l2=lgs, m2=mbc: NCV.tensor_tensor(
            out=epv, in0=l2, in1=m2, op=ALU.subtract)),
            reads=("lg", "mp"), writes=("ep",))
        K.op("act", (lambda: NCS.activation(out=ep[:], in_=ep[:], func=AF.Exp, bias=0.0, scale=1.0)),
             reads=("ep",), writes=("ep",))
        K.op("dve", (lambda: NCV.tensor_reduce(out=sp_[:], in_=epv, axis=AX.X, op=ALU.add)),
             reads=("ep",), writes=("sp",))
        K.op("dve", (lambda: NCV.reciprocal(out=sp_[:], in_=sp_[:])),
             reads=("sp",), writes=("sp",))
        K.op("dve", (lambda o2=out_sb, s3=spc: NCV.tensor_tensor(
            out=o2.rearrange("p (a n) -> p a n", a=8, n=4),
            in0=epv, in1=s3, op=ALU.mult)),
            reads=("ep", "sp"), writes=(name,))
    for e in range(8):
        K.op("pe", (lambda e2=e: NCT.transpose(out=psHT[:], in_=hsb[:, 16 * e2:16 * (e2 + 1)],
                                               identity=i128f[:])),
             reads=("hsb", "i128f_r", "psHTe"), writes=("psHT",))
        K.op("dve", (lambda e2=e: NCV.tensor_copy(out=hresT[:, 128 * e2:128 * (e2 + 1)], in_=psHT[:])),
             reads=("psHT",), writes=("hresT", "psHTe"))
        for (src, dst, rn) in ((hpre, hpreT, "hpreT"), (hpost, hpostT, "hpostT")):
            K.op("pe", (lambda e2=e, s3=src: NCT.transpose(
                out=psHT[0:4, :], in_=s3[:, 4 * e2:4 * (e2 + 1)], identity=i128f[:])),
                reads=("hpre", "hpost", "i128f_r", "psHTe"), writes=("psHT",))
            K.op("dve", (lambda e2=e, d2=dst: NCV.tensor_copy(
                out=d2[:, 128 * e2:128 * (e2 + 1)], in_=psHT[0:4, :])),
                reads=("psHT",), writes=(rn, "psHTe"))
    K.dma("pool", (lambda: NCP.dma_start(out=hscr[0:16, :], in_=hresT[:])),
          reads=("hresT",), writes=("hscr_res",))
    K.dma("pool", (lambda: NCP.dma_start(out=hscr[16:20, :], in_=hpreT[:])),
          reads=("hpreT",), writes=("hscr_pre",))
    K.dma("pool", (lambda: NCP.dma_start(out=hscr[20:24, :], in_=hpostT[:])),
          reads=("hpostT",), writes=("hscr_post",))
    if upto == "S":
        return _dump([("d_hresT", hresT), ("d_hpreT", hpreT), ("d_hpostT", hpostT),
                      ("d_lg", lg)])
    K.barrier()
    psSt.close()
    esS.close()

    yt = sb("yt", [128, 8 * 1024], BF16)
    esD1 = ExitStack()
    sb1 = lambda name, shape, dt: esD1.enter_context(nc.sbuf_tensor(name, shape, dt))
    wbraw = sb1("wbraw", [128, 1024], F32)
    wbrb = sb1("wbrb", [128, 8192], BF16)
    rowsl = sb1("rowsl", [1, 1024], BF16)
    hpre_bc = sb1("hpre_bc", [128, 4096], BF16)
    fftmp = sb1("fftmp", [128, 512], F32)
    xpre = sb1("xpre", [128, 8192], BF16)
    psD1 = ExitStack()
    psY = [psD1.enter_context(nc.psum_tensor(f"psY{p}", [128, 512], F32)) for p in range(2)]
    psBC = [psD1.enter_context(nc.psum_tensor(f"psBC{p}", [128, 512], F32)) for p in range(2)]
    for hf in range(8):
        K.dma("sync", (lambda h2=hf: NCY.dma_start(out=wbraw[:], in_=d["wbr"][:, 1024 * h2:1024 * (h2 + 1)])),
              reads=("wbraw_c",), writes=("wbraw",))
        K.op("act", (lambda h2=hf: NCS.copy(out=wbrb[:, 1024 * h2:1024 * (h2 + 1)], in_=wbraw[:])),
             reads=("wbraw",), writes=("wbrb", "wbraw_c"))
    for n in range(4):
        K.dma("pool", (lambda n2=n: NCP.dma_start(out=rowsl[:], in_=hscr[16 + n2:17 + n2, :])),
              reads=("hscr_pre", "rowsl"), writes=("rowsl",))
        for th in range(2):
            bp = th % 2
            K.op("pe", (lambda t2=th, b2=bp: NCT.matmul(
                out=psBC[b2][:], lhsT=ones1h[:], rhs=rowsl[:, 512 * t2:512 * (t2 + 1)],
                start=True, stop=True)),
                reads=("rowsl", "ones1h_r", f"psBCe{bp}"), writes=(f"psBC{bp}",))
            K.op("act", (lambda n2=n, t2=th, b2=bp: NCS.copy(
                out=hpre_bc[:, 1024 * n2 + 512 * t2:1024 * n2 + 512 * (t2 + 1)], in_=psBC[b2][:])),
                reads=(f"psBC{bp}",), writes=("hpre_bc", f"psBCe{bp}"))
    for cc in range(8):
        for th in range(2):
            for n in range(4):
                jg = n * 8 + cc
                xsl = (1024 * jg + 512 * th, 1024 * jg + 512 * (th + 1))
                hsl = (1024 * n + 512 * th, 1024 * n + 512 * (th + 1))
                osl = (1024 * cc + 512 * th, 1024 * cc + 512 * (th + 1))
                if n == 0:
                    K.op("pool", (lambda x2=xsl, h2=hsl, o2=osl: NCP.tensor_tensor(
                        out=xpre[:, o2[0]:o2[1]], in0=xnt[:, x2[0]:x2[1]],
                        in1=hpre_bc[:, h2[0]:h2[1]], op=ALU.mult)),
                        reads=("xnt", "hpre_bc"), writes=("xpre",))
                else:
                    K.op("pool", (lambda x2=xsl, h2=hsl: NCP.tensor_tensor(
                        out=fftmp[:], in0=xnt[:, x2[0]:x2[1]],
                        in1=hpre_bc[:, h2[0]:h2[1]], op=ALU.mult)),
                        reads=("xnt", "hpre_bc"), writes=("fftmp",))
                    K.op("pool", (lambda o2=osl: NCP.tensor_tensor(
                        out=xpre[:, o2[0]:o2[1]], in0=xpre[:, o2[0]:o2[1]],
                        in1=fftmp[:], op=ALU.add)),
                        reads=("fftmp", "xpre"), writes=("xpre",))
    for co in range(8):
        for th in range(2):
            yp = (2 * co + th) % 2
            for cc in range(8):
                K.op("pe", (lambda c2=cc, o2=co, t2=th, y2=yp: NCT.matmul(
                    out=psY[y2][:], lhsT=wbrb[:, 1024 * o2 + 128 * c2:1024 * o2 + 128 * (c2 + 1)],
                    rhs=xpre[:, 1024 * c2 + 512 * t2:1024 * c2 + 512 * (t2 + 1)],
                    start=(c2 == 0), stop=(c2 == 7))),
                    reads=("wbrb", "xpre", f"psYe{yp}"), writes=(f"psY{yp}",))
            K.op("act", (lambda o2=co, t2=th, y2=yp: NCS.copy(
                out=yt[:, 1024 * o2 + 512 * t2:1024 * o2 + 512 * (t2 + 1)], in_=psY[y2][:])),
                reads=(f"psY{yp}",), writes=("yt", f"psYe{yp}"))
    if upto == "D1":
        return _dump([("d_yt", yt), ("d_xpre", xpre), ("d_hpre_bc", hpre_bc)])
    K.barrier()
    psD1.close()
    esD1.close()

    esD = ExitStack()
    sbD = lambda name, shape, dt: esD.enter_context(nc.sbuf_tensor(name, shape, dt))
    woraw = [sbD(f"woraw{p}", [128, 1024], F32) for p in range(2)]
    wob = [sbD(f"wob{p}", [128, 4096], BF16) for p in range(2)]
    bc5 = sbD("bc5", [128, 5 * 1024], BF16)
    rslot = [sbD(f"rslot{j}", [1, 1024], BF16) for j in range(1)]
    xtrow = [sbD("xtrow0", [128, 1024], F32)] * 2
    osb = [sbD(f"osb{p}", [128, 1024], F32) for p in range(2)]
    fft2 = [sbD(f"fft2{p}", [128, 1024], F32) for p in range(2)]
    psD = ExitStack()
    psW = [psD.enter_context(nc.psum_tensor(f"psW{p}", [128, 1024], F32)) for p in range(2)]
    psB2 = [psD.enter_context(nc.psum_tensor(f"psB2{p}", [128, 512], F32)) for p in range(2)]

    for n in range(4):
        for j in range(5):
            src_row = (n * 4 + j) if j < 4 else (20 + n)
            rdep = "hscr_res" if j < 4 else "hscr_post"
            K.dma("pool", (lambda s3=src_row: NCP.dma_start(
                out=rslot[0][:], in_=hscr[s3:s3 + 1, :])),
                reads=(rdep, "rslot0"), writes=("rslot0",))
            for th in range(2):
                bp = th % 2
                K.op("pe", (lambda t2=th, b2=bp: NCT.matmul(
                    out=psB2[b2][:], lhsT=ones1h[:], rhs=rslot[0][:, 512 * t2:512 * (t2 + 1)],
                    start=True, stop=True)),
                    reads=("rslot0", "ones1h_r", f"psB2e{bp}"), writes=(f"psB2{bp}",))
                K.op("act", (lambda j2=j, t2=th, b2=bp: NCS.copy(
                    out=bc5[:, 1024 * j2 + 512 * t2:1024 * j2 + 512 * (t2 + 1)], in_=psB2[b2][:])),
                    reads=(f"psB2{bp}",), writes=("bc5", f"psB2e{bp}"))
        for cc in range(8):
            of = n * 8 + cc
            wp = of % 2
            for hf in range(4):
                qp_ = hf % 2
                K.dma("sync", (lambda o2=of, h2=hf, q2=qp_: NCY.dma_start(
                    out=woraw[q2][:], in_=d["wo"][o2][:, 1024 * h2:1024 * (h2 + 1)])),
                    reads=(f"worawc{qp_}",), writes=(f"woraw{qp_}",))
                K.op("act", (lambda w2=wp, h2=hf, q2=qp_: NCS.copy(
                    out=wob[w2][:, 1024 * h2:1024 * (h2 + 1)], in_=woraw[q2][:])),
                    reads=(f"woraw{qp_}",), writes=(f"wob{wp}", f"worawc{qp_}"))
            K.dma("pool", (lambda o2=of: NCP.dma_start(
                out=xtrow[0][:], in_=d["xqt"][128 * o2:128 * (o2 + 1), :])),
                reads=(), writes=("xtrow0",))
            op_ = of % 2
            for th in range(2):
                for c in range(32):
                    K.op("pe", (lambda c2=c, t2=th, p2=op_, w2=wp: NCT.matmul(
                        out=psW[p2][:, 512 * t2:512 * (t2 + 1)],
                        lhsT=wob[w2][:, 128 * c2:128 * (c2 + 1)],
                        rhs=qt[:, 1024 * c2 + 512 * t2:1024 * c2 + 512 * (t2 + 1)],
                        start=(c2 == 0), stop=(c2 == 31))),
                        reads=(f"wob{wp}", "qt", f"psWe{op_}"), writes=(f"psW{op_}",))
            K.op("dve", (lambda p2=op_, w2=wp: NCV.tensor_tensor(
                out=osb[p2][:], in0=psW[p2][:], in1=xtrow[w2][:], op=ALU.add)),
                reads=(f"psW{op_}", "xtrow0", f"osbc{op_}"), writes=(f"osb{op_}", f"psWe{op_}"))
            for j in range(4):
                jg = j * 8 + cc
                K.op("dve", (lambda j2=jg, j3=j, p2=op_: NCV.tensor_tensor(
                    out=fft2[p2][:], in0=xnt[:, 1024 * j2:1024 * (j2 + 1)],
                    in1=bc5[:, 1024 * j3:1024 * (j3 + 1)], op=ALU.mult)),
                    reads=("xnt", "bc5"), writes=(f"fft2{op_}",))
                K.op("dve", (lambda p2=op_: NCV.tensor_tensor(
                    out=osb[p2][:], in0=osb[p2][:], in1=fft2[p2][:], op=ALU.add)),
                    reads=(f"fft2{op_}",), writes=(f"osb{op_}",))
            K.op("dve", (lambda c2=cc, p2=op_: NCV.tensor_tensor(
                out=fft2[p2][:], in0=yt[:, 1024 * c2:1024 * (c2 + 1)],
                in1=bc5[:, 4096:5120], op=ALU.mult)),
                reads=("yt", "bc5"), writes=(f"fft2{op_}",))
            K.op("dve", (lambda p2=op_: NCV.tensor_tensor(
                out=osb[p2][:], in0=osb[p2][:], in1=fft2[p2][:], op=ALU.add)),
                reads=(f"fft2{op_}",), writes=(f"osb{op_}",))
            K.dma("act", (lambda o3=of, p2=op_: NCS.dma_start(
                out=d["yout"][o3], in_=osb[p2][:])),
                reads=(f"osb{op_}",), writes=(f"osbc{op_}",))

    K.barrier()
    final_qact = K.cnt["q_act"]
    K.ops["pool"].append(([("q_act", final_qact)], None, None))
    psD.close()
    esD.close()

    return _emit()


def _host_inputs(L0_attn_hidden_states, L0_attn_Wqkv, L0_attn_Wo, L0_ffn_W_res,
                 L0_ffn_W_pre, L0_ffn_W_post, L0_ffn_H_res_base, L0_ffn_H_pre_base,
                 L0_ffn_H_post_base, L0_ffn_gate, L0_ffn_W_branch):
    x = np.asarray(L0_attn_hidden_states, dtype=np.float32)
    wqkv = np.asarray(L0_attn_Wqkv, dtype=np.float32)
    wo = np.asarray(L0_attn_Wo, dtype=np.float32)

    def lay_blocks(w, nblocks):
        return np.ascontiguousarray(
            w.reshape(32, 128, nblocks, 128).transpose(2, 1, 0, 3).reshape(nblocks, 128, 32 * 128))

    wqkv_l = lay_blocks(wqkv, 34)
    wo_l = lay_blocks(wo, 32)
    wbr = np.asarray(L0_ffn_W_branch, dtype=np.float32)
    wbr_l = np.ascontiguousarray(
        wbr.reshape(8, 128, 8, 128).transpose(1, 2, 0, 3).reshape(128, 8192))
    w24 = np.concatenate([np.asarray(L0_ffn_W_res, dtype=np.float32),
                          np.asarray(L0_ffn_W_pre, dtype=np.float32),
                          np.asarray(L0_ffn_W_post, dtype=np.float32)], axis=0)
    w24_l = np.ascontiguousarray(w24.T.reshape(32, 128, 24).transpose(1, 0, 2).reshape(128, 32 * 24))
    bases = np.concatenate([np.asarray(L0_ffn_H_res_base, dtype=np.float32).reshape(-1),
                            np.asarray(L0_ffn_H_pre_base, dtype=np.float32),
                            np.asarray(L0_ffn_H_post_base, dtype=np.float32)]).reshape(1, 24)
    gate = np.asarray(L0_ffn_gate, dtype=np.float32).reshape(1, 1)

    kf = np.arange(512, dtype=np.float32)[None, :]
    kf2 = np.arange(1024, dtype=np.float32)[None, :]
    qp = np.arange(128, dtype=np.float32)[:, None]
    dplain2 = np.ascontiguousarray(np.broadcast_to(kf2, (128, 1024)).astype(np.float32))
    i128h = np.eye(128).astype(BF)
    i128f = np.eye(128).astype(np.float32)
    ones1h = np.ones((1, 128), dtype=BF)
    ones1f = np.ones((1, 128), dtype=np.float32)
    onescol_h = np.ones((128, 1), dtype=BF)

    in_maps = []
    for c in range(8):
        b, r = c // 4, c % 4
        rows = np.concatenate([np.arange(512 * e + 128 * r, 512 * e + 128 * r + 128)
                               for e in range(8)])
        xq = np.ascontiguousarray(x[b][rows])
        xqt = np.ascontiguousarray(xq.T)
        dmask2 = np.where((kf2 < 512) | (kf2 - 512 <= 128 * r + qp), kf2,
                          np.float32(NEG_BIG)).astype(np.float32)
        qcol = (qp + 128 * r).astype(np.float32)
        in_maps.append(dict(
            xkv=np.ascontiguousarray(x[b]), xq=xq, xqt=xqt, wqkv=wqkv_l, wo=wo_l, wbr=wbr_l,
            w24=w24_l, bases24=bases, gate=gate, dplain2=dplain2,
            dmask2=np.ascontiguousarray(dmask2),
            qcol=np.ascontiguousarray(qcol),
            i128h=i128h, i128f=i128f, ones1h=ones1h, ones1f=ones1f,
            onescol_h=onescol_h))
    return in_maps


def kernel(**inputs):
    in_maps = _host_inputs(**inputs)
    nc = build_program()
    res = run_bass_kernel_spmd(nc, in_maps, core_ids=list(range(8)))
    out = np.empty((B, S, H), dtype=np.float32)
    for c in range(8):
        b, r = c // 4, c % 4
        y = np.asarray(res.results[c]["yout"])
        rows = y.transpose(2, 0, 1).reshape(1024, 4096)
        for e in range(8):
            out[b, 512 * e + 128 * r:512 * e + 128 * r + 128, :] = rows[128 * e:128 * (e + 1)]
    return out


# version 07
# speedup vs baseline: 1.3729x; 1.0163x over previous
import math
from contextlib import ExitStack

import numpy as np
import ml_dtypes

import concourse.bass as bass
import concourse.mybir as mybir
from concourse.bass_utils import run_bass_kernel_spmd

F32 = mybir.dt.float32
BF16 = mybir.dt.bfloat16
AF = mybir.ActivationFunctionType
ALU = mybir.AluOpType
AX = mybir.AxisListType
BF = ml_dtypes.bfloat16

B, S, H = 2, 4096, 4096
NH, HD = 32, 128
SCALE = float(HD) ** -0.5
LN_EPS = 1e-5
RMS_EPS = 1.1920929e-07
TAU_INV = float(1.0 / np.float32(0.1))
NEG_BIG = -1.0e30
SINK_ITERS = 50
NT_KV = S // 128
NT_Q = 8
NCH = H // 128


def _alibi_slopes(n):
    cp2 = 2 ** math.floor(math.log2(n))
    base = 2.0 ** (-(2.0 ** (-(math.log2(cp2) - 3))))
    slopes = base ** np.arange(1, cp2 + 1, dtype=np.float64)
    if cp2 != n:
        extra_base = 2.0 ** (-(2.0 ** (-(math.log2(2 * cp2) - 3))))
        rem = min(cp2, n - cp2)
        extra = extra_base ** np.arange(1, 1 + 2 * rem, 2, dtype=np.float64)
        slopes = np.concatenate([slopes, extra])
    return slopes.astype(np.float32)

SLOPES = _alibi_slopes(NH)


class Kb:
    ENGINES = ("pe", "act", "dve", "pool", "sync")
    ESEM = {"pe": "s_pe", "act": "s_act", "dve": "s_dve", "pool": "s_pool"}
    QSEM = {"sync": "q_sync", "pool": "q_pool", "act": "q_act", "dve": "q_dve"}

    def __init__(self):
        self.ops = {e: [] for e in self.ENGINES}
        self.cnt = {}
        for s in list(self.ESEM.values()) + list(self.QSEM.values()):
            self.cnt[s] = 0
        self.res = {}
        self.waited = {e: {} for e in self.ENGINES}

    def _need(self, eng, waits, token):
        sem, val = token
        if eng == "pe" and sem == self.ESEM.get("pe"):
            return
        if self.waited[eng].get(sem, 0) >= val:
            return
        self.waited[eng][sem] = val
        waits[sem] = max(waits.get(sem, 0), val)

    def _emit(self, eng, fn, reads, writes, inc_sem, inc_n):
        waits = {}
        for r in reads:
            ent = self.res.get(r)
            if ent and ent[0]:
                self._need(eng, waits, ent[0])
        for w in writes:
            ent = self.res.get(w)
            if ent:
                if ent[0]:
                    self._need(eng, waits, ent[0])
                for sem, val in ent[1].items():
                    self._need(eng, waits, (sem, val))
        self.cnt[inc_sem] += inc_n
        token = (inc_sem, self.cnt[inc_sem])
        for r in reads:
            ent = self.res.setdefault(r, [None, {}])
            ent[1][token[0]] = token[1]
        for w in writes:
            self.res[w] = [token, {}]
        self.ops[eng].append((sorted(waits.items()), fn, (inc_sem, inc_n)))
        return token

    def op(self, eng, fn, reads=(), writes=()):
        return self._emit(eng, fn, reads, writes, self.ESEM[eng], 1)

    def dma(self, eng, fn, reads=(), writes=()):
        qs = self.QSEM[eng]
        prev = self.cnt[qs]
        tok = self._emit(eng, fn, reads, writes, qs, 16)
        if prev > 0:
            waits, fn_, inc = self.ops[eng][-1]
            waits = dict(waits)
            if self.waited[eng].get(qs, 0) < prev:
                self.waited[eng][qs] = prev
                waits[qs] = max(waits.get(qs, 0), prev)
            self.ops[eng][-1] = (sorted(waits.items()), fn_, inc)
        return tok

    def barrier(self):
        snap = dict(self.cnt)
        for eng in self.ENGINES:
            waits = {}
            for sem, val in snap.items():
                if val == 0:
                    continue
                if eng in self.ESEM and self.ESEM[eng] == sem:
                    continue
                if self.waited[eng].get(sem, 0) >= val:
                    continue
                self.waited[eng][sem] = val
                waits[sem] = val
            if waits:
                self.ops[eng].append((sorted(waits.items()), None, None))


def build_program(upto="FULL"):
    nc = bass.Bass("TRN2", target_bir_lowering=False)
    K = Kb()
    sems = {}
    _emit_state = {}

    def _emit():
        def mk(eng_name, nce):
            def body(_):
                for waits, fn, inc in K.ops[eng_name]:
                    for s, v in waits:
                        nce.wait_ge(sems[s], v)
                    if fn is not None:
                        ins = fn()
                        if inc is not None:
                            ins.then_inc(sems[inc[0]], inc[1])
            return body
        with nc.Block() as block:
            block.tensor(mk("pe", _emit_state["NCT"]))
            block.scalar(mk("act", _emit_state["NCS"]))
            block.vector(mk("dve", _emit_state["NCV"]))
            block.gpsimd(mk("pool", _emit_state["NCP"]))
            block.sync(mk("sync", _emit_state["NCY"]))
        return nc

    def _dump(pairs):
        NCP = _emit_state["NCP"]
        K.barrier()
        for nm, handle in pairs:
            shp = list(handle[:].shape)
            dt_ = handle[:].dtype
            dout = nc.dram_tensor(nm, shp, dt_, kind="ExternalOutput").ap()
            K.dma("pool", (lambda d2=dout, h2=handle: NCP.dma_start(out=d2, in_=h2[:])),
                  reads=(), writes=(nm + "_out",))
        K.ops["pool"].append(([("q_pool", K.cnt["q_pool"])], None, None))
        return _emit()

    d = {}
    d["xkv"] = nc.dram_tensor("xkv", [S, H], F32, kind="ExternalInput").ap()
    d["xq"] = nc.dram_tensor("xq", [1024, H], F32, kind="ExternalInput").ap()
    d["xqt"] = nc.dram_tensor("xqt", [H, 1024], F32, kind="ExternalInput").ap()
    d["wqkv"] = nc.dram_tensor("wqkv", [34, 128, 32 * 128], F32, kind="ExternalInput").ap()
    d["wo"] = nc.dram_tensor("wo", [32, 128, 32 * 128], F32, kind="ExternalInput").ap()
    d["wbr"] = nc.dram_tensor("wbr", [128, 8192], F32, kind="ExternalInput").ap()
    d["w24"] = nc.dram_tensor("w24", [128, 32 * 24], F32, kind="ExternalInput").ap()
    d["bases24"] = nc.dram_tensor("bases24", [1, 24], F32, kind="ExternalInput").ap()
    d["gate"] = nc.dram_tensor("gate", [1, 1], F32, kind="ExternalInput").ap()
    d["dplain2"] = nc.dram_tensor("dplain2", [128, 1024], F32, kind="ExternalInput").ap()
    d["dmask2"] = nc.dram_tensor("dmask2", [128, 1024], F32, kind="ExternalInput").ap()
    d["qcol"] = nc.dram_tensor("qcol", [128, 1], F32, kind="ExternalInput").ap()
    d["brow0"] = nc.dram_tensor("brow0", [1, 640], F32, kind="ExternalInput").ap()
    d["brow1"] = nc.dram_tensor("brow1", [1, 640], F32, kind="ExternalInput").ap()
    d["i128h"] = nc.dram_tensor("i128h", [128, 128], BF16, kind="ExternalInput").ap()
    d["i128f"] = nc.dram_tensor("i128f", [128, 128], F32, kind="ExternalInput").ap()
    d["ones1h"] = nc.dram_tensor("ones1h", [1, 128], BF16, kind="ExternalInput").ap()
    d["ones1f"] = nc.dram_tensor("ones1f", [1, 128], F32, kind="ExternalInput").ap()
    d["onescol_h"] = nc.dram_tensor("onescol_h", [128, 1], BF16, kind="ExternalInput").ap()
    d["yout"] = nc.dram_tensor("yout", [32, 128, 1024], F32, kind="ExternalOutput").ap()
    hscr = nc.dram_tensor("hscr", [24, 1024], BF16, kind="Internal").ap()
    nscr = nc.dram_tensor("nscr", [1, 640], F32, kind="Internal").ap()

    es = ExitStack()
    sb = lambda name, shape, dt: es.enter_context(nc.sbuf_tensor(name, shape, dt))

    xnt = sb("xnt", [128, NCH * 1024], BF16)
    esKV = ExitStack()
    kt = esKV.enter_context(nc.sbuf_tensor("kt", [128, S], BF16, side="right"))
    vt = esKV.enter_context(nc.sbuf_tensor("vt", [128, S], BF16, side="right"))
    dplain2 = esKV.enter_context(nc.sbuf_tensor("dplain2_s", [128, 1024], F32, side="right"))
    dmask2 = esKV.enter_context(nc.sbuf_tensor("dmask2_s", [128, 1024], F32, side="right"))
    bias_all = esKV.enter_context(nc.sbuf_tensor("bias_all", [128, 640], F32, side="right"))

    qcol = sb("qcol_s", [128, 1], F32)
    i128h = sb("i128h_s", [128, 128], BF16)
    i128f = sb("i128f_s", [128, 128], F32)
    ones1h = sb("ones1h_s", [1, 128], BF16)
    ones1f = sb("ones1f_s", [1, 128], F32)
    onescol_h = sb("onescol_h_s", [128, 1], BF16)
    rstdF = sb("rstdF", [128, 8], F32)
    lgraw = sb("lgraw", [24, 1024], F32)
    negc = sb("negc", [128, 1], F32)
    bases_row = sb("bases_row", [1, 24], F32)
    gate_s = sb("gate_s", [1, 1], F32)

    for s in list(Kb.ESEM.values()) + list(Kb.QSEM.values()):
        sems[s] = es.enter_context(nc.semaphore(s))

    NCV = nc.vector
    NCS = nc.scalar
    NCT = nc.tensor
    NCP = nc.gpsimd
    NCY = nc.sync
    _emit_state.update(NCV=NCV, NCS=NCS, NCT=NCT, NCP=NCP, NCY=NCY)

    for name, dst in [("dplain2", dplain2),
                      ("dmask2", dmask2), ("qcol", qcol),
                      ("i128h", i128h), ("i128f", i128f), ("ones1h", ones1h),
                      ("ones1f", ones1f), ("onescol_h", onescol_h),
                      ("bases24", bases_row), ("gate", gate_s)]:
        src = d[name]
        K.dma("pool", (lambda dd=dst, ss=src: NCP.dma_start(out=dd[:], in_=ss)),
              reads=(), writes=(name + "_r",))

    esA = ExitStack()
    sbA = lambda name, shape, dt: esA.enter_context(nc.sbuf_tensor(name, shape, dt))
    xbuf = [sbA(f"xbuf{p}", [128, H], F32) for p in range(2)]
    xnbuf = [sbA(f"xnbuf{p}", [128, H], BF16) for p in range(2)]
    stage = [sbA(f"stage{p}", [128, H], BF16) for p in range(2)]
    junk = sbA("junk", [128, H], BF16)
    wkvraw = sbA("wkvraw", [128, 4096], F32)
    wkvb = sbA("wkvb", [128, 2 * 4096], BF16)
    w24raw = sbA("w24raw", [128, 32 * 24], F32)
    w24b = sbA("w24b", [128, 32 * 24], BF16)
    s1 = [sbA("s1" + str(p), [128, 1], F32) for p in range(2)]
    s2 = [sbA("s2" + str(p), [128, 1], F32) for p in range(2)]
    mn = [sbA("mn" + str(p), [128, 1], F32) for p in range(2)]
    mn2 = [sbA("mn2" + str(p), [128, 1], F32) for p in range(2)]
    var = [sbA("var" + str(p), [128, 1], F32) for p in range(2)]
    sd = [sbA("sd" + str(p), [128, 1], F32) for p in range(2)]
    rstd = [sbA("rstd" + str(p), [128, 1], F32) for p in range(2)]
    negmr = [sbA("negmr" + str(p), [128, 1], F32) for p in range(2)]
    s2f = sbA("s2f", [128, 1], F32)
    sdF = sbA("sdF", [128, 1], F32)
    psA = ExitStack()
    psT = [psA.enter_context(nc.psum_tensor(f"psT{p}", [128, 512], BF16)) for p in range(2)]
    psK = [psA.enter_context(nc.psum_tensor(f"psK{p}", [128, 128], F32)) for p in range(2)]
    psV = [psA.enter_context(nc.psum_tensor(f"psV{p}", [128, 128], F32)) for p in range(2)]
    psL = psA.enter_context(nc.psum_tensor("psL", [24, 128], F32))

    for bi, blk in enumerate((32, 33)):
        K.dma("sync", (lambda b2=blk: NCY.dma_start(out=wkvraw[:], in_=d["wqkv"][b2])),
              reads=("wkvb_c",), writes=("wkvraw",))
        K.op("act", (lambda b3=bi: NCS.copy(out=wkvb[:, b3 * 4096:(b3 + 1) * 4096], in_=wkvraw[:])),
             reads=("wkvraw",), writes=("wkvb", "wkvb_c"))
    K.dma("sync", (lambda: NCY.dma_start(out=w24raw[:], in_=d["w24"])),
          reads=(), writes=("w24raw",))
    K.op("act", (lambda: NCS.copy(out=w24b[:], in_=w24raw[:])),
         reads=("w24raw",), writes=("w24b",))

    def ln_tile(src_ap, p, tag):
        xb, xn = xbuf[p], xnbuf[p]
        s1_, s2_, mn_, mn2_ = s1[p], s2[p], mn[p], mn2[p]
        var_, sd_, rstd_, negmr_ = var[p], sd[p], rstd[p], negmr[p]
        rx, rn = f"xbuf{p}", f"xnbuf{p}"
        sfx = str(p)
        K.dma("sync", (lambda: NCY.dma_start(out=xb[:], in_=src_ap)),
              reads=(rn,), writes=(rx,))
        K.op("dve", (lambda: NCV.reduce_sum(out=s1_[:], in_=xb[:], axis=AX.X)),
             reads=(rx,), writes=("s1" + sfx,))
        K.op("act", (lambda: NCS.activation(out=junk[:], in_=xb[:], func=AF.Square,
                                            bias=0.0, scale=1.0, accum_out=s2_[:])),
             reads=(rx, "junk"), writes=("junk", "s2" + sfx))
        K.op("dve", (lambda: NCV.tensor_scalar_mul(out=mn_[:], in0=s1_[:], scalar1=1.0 / H)),
             reads=("s1" + sfx,), writes=("mn" + sfx,))
        K.op("dve", (lambda: NCV.tensor_mul(out=mn2_[:], in0=mn_[:], in1=mn_[:])),
             reads=("mn" + sfx,), writes=("mn2" + sfx,))
        K.op("dve", (lambda: NCV.scalar_tensor_tensor(out=var_[:], in0=s2_[:], scalar=1.0 / H,
                                                      in1=mn2_[:], op0=ALU.mult, op1=ALU.subtract)),
             reads=("s2" + sfx, "mn2" + sfx), writes=("var" + sfx,))
        K.op("dve", (lambda: NCV.tensor_scalar_add(out=var_[:], in0=var_[:], scalar1=LN_EPS)),
             reads=("var" + sfx,), writes=("var" + sfx,))
        K.op("act", (lambda: NCS.activation(out=sd_[:], in_=var_[:], func=AF.Sqrt, bias=0.0, scale=1.0)),
             reads=("var" + sfx,), writes=("sd" + sfx,))
        K.op("dve", (lambda: NCV.reciprocal(out=rstd_[:], in_=sd_[:])),
             reads=("sd" + sfx,), writes=("rstd" + sfx,))
        K.op("dve", (lambda: NCV.scalar_tensor_tensor(out=negmr_[:], in0=mn_[:], scalar=-1.0,
                                                      in1=rstd_[:], op0=ALU.mult, op1=ALU.mult)),
             reads=("mn" + sfx, "rstd" + sfx), writes=("negmr" + sfx,))
        K.op("act", (lambda: NCS.activation(out=xn[:], in_=xb[:], func=AF.Identity,
                                            bias=negmr_[:, 0:1], scale=rstd_[:, 0:1])),
             reads=(rx, "rstd" + sfx, "negmr" + sfx), writes=(rn,))
        return xn, rn

    for i in range(NT_KV):
        p = i % 2
        xn, rn = ln_tile(d["xkv"][128 * i:128 * (i + 1), :], p, f"a1_{i}")
        st, rs = stage[p], f"stage{p}"
        for g in range(8):
            tp = g % 2
            for u in range(4):
                j = 4 * g + u
                K.op("pe", (lambda xn2=xn, j2=j, t2=tp: NCT.transpose(
                    out=psT[t2][:, 128 * (j2 % 4):128 * (j2 % 4 + 1)],
                    in_=xn2[:, 128 * j2:128 * (j2 + 1)], identity=i128h[:])),
                    reads=(rn, "i128h_r", f"psTe{tp}"), writes=(f"psT{tp}",))
            K.op("dve", (lambda s3=st, g2=g, t2=tp: NCV.tensor_copy(
                out=s3[:, 512 * g2:512 * (g2 + 1)], in_=psT[t2][:])),
                reads=(f"psT{tp}",), writes=(rs, f"psTe{tp}"))
        kp = i % 2
        for c in range(32):
            K.op("pe", (lambda c2=c, s3=st, k2=kp: NCT.matmul(
                out=psK[k2][:], lhsT=wkvb[:, 128 * c2:128 * (c2 + 1)],
                rhs=s3[:, 128 * c2:128 * (c2 + 1)],
                start=(c2 == 0), stop=(c2 == 31))),
                reads=(rs, "wkvb", f"psKe{kp}"), writes=(f"psK{kp}",))
        K.op("act", (lambda i2=i, k2=kp: NCS.copy(out=kt[:, 128 * i2:128 * (i2 + 1)], in_=psK[k2][:])),
             reads=(f"psK{kp}",), writes=("kt", f"psKe{kp}"))
        for c in range(32):
            K.op("pe", (lambda c2=c, s3=st, k2=kp: NCT.matmul(
                out=psV[k2][:], lhsT=s3[:, 128 * c2:128 * (c2 + 1)],
                rhs=wkvb[:, 4096 + 128 * c2:4096 + 128 * (c2 + 1)],
                start=(c2 == 0), stop=(c2 == 31))),
                reads=(rs, "wkvb", f"psVe{kp}"), writes=(f"psV{kp}",))
        K.op("act", (lambda i2=i, k2=kp: NCS.copy(out=vt[:, 128 * i2:128 * (i2 + 1)], in_=psV[k2][:])),
             reads=(f"psV{kp}",), writes=("vt", f"psVe{kp}"))

    for e in range(NT_Q):
        p = e % 2
        xn, rn = ln_tile(d["xq"][128 * e:128 * (e + 1), :], p, f"a2_{e}")
        for g in range(8):
            tp = g % 2
            for u in range(4):
                j = 4 * g + u
                K.op("pe", (lambda xn2=xn, j2=j, t2=tp: NCT.transpose(
                    out=psT[t2][:, 128 * (j2 % 4):128 * (j2 % 4 + 1)],
                    in_=xn2[:, 128 * j2:128 * (j2 + 1)], identity=i128h[:])),
                    reads=(rn, "i128h_r", f"psTe{tp}"), writes=(f"psT{tp}",))
            for u in range(4):
                j = 4 * g + u
                K.op("dve", (lambda j2=j, e2=e, t2=tp, u2=u: NCV.tensor_copy(
                    out=xnt[:, 1024 * j2 + 128 * e2:1024 * j2 + 128 * (e2 + 1)],
                    in_=psT[t2][:, 128 * u2:128 * (u2 + 1)])),
                    reads=(f"psT{tp}",),
                    writes=("xnt", f"psTe{tp}") if u == 3 else ("xnt",))
        K.op("act", (lambda xn2=xn: NCS.activation(out=junk[:], in_=xn2[:], func=AF.Square,
                                                   bias=0.0, scale=1.0, accum_out=s2f[:])),
             reads=(rn,), writes=("junk", "s2f"))
        K.op("dve", (lambda: NCV.tensor_scalar(out=s2f[:], in0=s2f[:], scalar1=1.0 / H,
                                               scalar2=RMS_EPS, op0=ALU.mult, op1=ALU.add)),
             reads=("s2f",), writes=("s2f",))
        K.op("act", (lambda: NCS.activation(out=sdF[:], in_=s2f[:], func=AF.Sqrt, bias=0.0, scale=1.0)),
             reads=("s2f",), writes=("sdF",))
        K.op("dve", (lambda e2=e: NCV.reciprocal(out=rstdF[:, e2:e2 + 1], in_=sdF[:])),
             reads=("sdF",), writes=("rstdF",))
        for c in range(32):
            K.op("pe", (lambda c2=c, e2=e: NCT.matmul(
                out=psL[:], lhsT=w24b[:, 24 * c2:24 * (c2 + 1)],
                rhs=xnt[:, 1024 * c2 + 128 * e2:1024 * c2 + 128 * (e2 + 1)],
                start=(c2 == 0), stop=(c2 == 31))),
                reads=("xnt", "w24b", "psLe"), writes=("psL",))
        K.op("act", (lambda e2=e: NCS.copy(out=lgraw[:, 128 * e2:128 * (e2 + 1)], in_=psL[:])),
             reads=("psL",), writes=("lgraw", "psLe"))

    if upto == "A":
        return _dump([("d_xnt", xnt), ("d_kt", kt), ("d_vt", vt),
                      ("d_lgraw", lgraw), ("d_rstdF", rstdF)])
    K.barrier()
    psA.close()
    esA.close()

    qt = sb("qt", [128, NH * 1024], BF16)
    esB = ExitStack()
    sbB = lambda name, shape, dt: esB.enter_context(nc.sbuf_tensor(name, shape, dt))
    esBR = ExitStack()
    wqraw = [esBR.enter_context(nc.sbuf_tensor(f"wqraw{p}", [128, 2048], F32, side="right"))
             for p in range(2)]
    wqb = [esBR.enter_context(nc.sbuf_tensor(f"wqb{p}", [128, 4096], BF16, side="right"))
           for p in range(2)]
    qsq = esBR.enter_context(nc.sbuf_tensor("qsq", [128, 512], BF16, side="right"))
    q2row = esBR.enter_context(nc.sbuf_tensor("q2row", [1, 512], F32, side="right"))
    q2cols = esBR.enter_context(nc.sbuf_tensor("q2cols", [1, 80], F32, side="right"))
    tiny = esBR.enter_context(nc.sbuf_tensor("tiny", [1, 4], F32, side="right"))
    b1tmp = esBR.enter_context(nc.sbuf_tensor("b1tmp", [1, 640], F32, side="right"))
    rhs2 = esBR.enter_context(nc.sbuf_tensor("rhs2", [2, 640], F32, side="right"))
    lhsT2 = esBR.enter_context(nc.sbuf_tensor("lhsT2", [2, 128], F32, side="right"))
    psB = ExitStack()
    psQ = [psB.enter_context(nc.psum_tensor(f"psQ{p}", [128, 512], F32)) for p in range(2)]
    ps12 = psB.enter_context(nc.psum_tensor("ps12", [1, 512], F32))
    psC1 = psB.enter_context(nc.psum_tensor("psC1", [128, 1], F32))

    for h in range(NH):
        wp = h % 2
        for hf in range(2):
            qp2 = hf % 2
            eng_ = "sync" if qp2 == 0 else "act"
            nce_ = NCY if qp2 == 0 else NCS
            K.dma(eng_, (lambda h2=h, f2=hf, q2=qp2, n3=nce_: n3.dma_start(
                out=wqraw[q2][:], in_=d["wqkv"][h2][:, 2048 * f2:2048 * (f2 + 1)])),
                reads=(f"wqrawc{qp2}",), writes=(f"wqraw{qp2}",))
            K.op("act", (lambda w2=wp, f2=hf, q2=qp2: NCS.copy(
                out=wqb[w2][:, 2048 * f2:2048 * (f2 + 1)], in_=wqraw[q2][:])),
                reads=(f"wqraw{qp2}",), writes=(f"wqb{wp}", f"wqrawc{qp2}"))
        for th in range(2):
            qp_ = (2 * h + th) % 2
            for c in range(32):
                K.op("pe", (lambda c2=c, t2=th, q2=qp_, w2=wp: NCT.matmul(
                    out=psQ[q2][:], lhsT=wqb[w2][:, 128 * c2:128 * (c2 + 1)],
                    rhs=xnt[:, 1024 * c2 + 512 * t2:1024 * c2 + 512 * (t2 + 1)],
                    start=(c2 == 0), stop=(c2 == 31))),
                    reads=(f"wqb{wp}", "xnt", f"psQe{qp_}"), writes=(f"psQ{qp_}",))
            K.op("act", (lambda h2=h, t2=th, q2=qp_: NCS.copy(
                out=qt[:, 1024 * h2 + 512 * t2:1024 * h2 + 512 * (t2 + 1)], in_=psQ[q2][:])),
                reads=(f"psQ{qp_}",), writes=("qt", f"psQe{qp_}"))
        for th in range(2):
            K.op("act", (lambda h2=h, t2=th: NCS.activation(
                out=qsq[:], in_=qt[:, 1024 * h2 + 512 * t2:1024 * h2 + 512 * (t2 + 1)],
                func=AF.Square, bias=0.0, scale=1.0)),
                reads=("qt", "qsq_c"), writes=("qsq",))
            K.op("pe", (lambda t2=th: NCT.matmul(
                out=ps12[:], lhsT=onescol_h[:], rhs=qsq[:],
                start=True, stop=True)),
                reads=("qsq", "onescol_h_r", "ps12e"), writes=("ps12", "qsq_c"))
            K.op("act", (lambda: NCS.copy(out=q2row[:], in_=ps12[:])),
                 reads=("ps12", "q2row_c"), writes=("q2row", "ps12e"))
            K.op("dve", (lambda h2=h, t2=th: NCV.reduce_max(
                out=q2cols[:, 2 * h2 + t2:2 * h2 + t2 + 1], in_=q2row[:], axis=AX.X)),
                reads=("q2row",), writes=("q2cols", "q2row_c"))
    for g in range(8):
        K.op("act", (lambda g2=g: NCS.activation(out=qsq[:], in_=kt[:, 512 * g2:512 * (g2 + 1)],
                                                 func=AF.Square, bias=0.0, scale=1.0)),
             reads=("kt", "qsq_c"), writes=("qsq",))
        K.op("pe", (lambda: NCT.matmul(
            out=ps12[:], lhsT=onescol_h[:], rhs=qsq[:],
            start=True, stop=True)),
            reads=("qsq", "onescol_h_r", "ps12e"), writes=("ps12", "qsq_c"))
        K.op("act", (lambda: NCS.copy(out=q2row[:], in_=ps12[:])),
             reads=("ps12", "q2row_c"), writes=("q2row", "ps12e"))
        K.op("dve", (lambda g2=g: NCV.reduce_max(
            out=q2cols[:, 64 + g2:65 + g2], in_=q2row[:], axis=AX.X)),
            reads=("q2row",), writes=("q2cols", "q2row_c"))
    K.op("dve", (lambda: NCV.reduce_max(out=tiny[:, 0:1], in_=q2cols[:, 0:64], axis=AX.X)),
         reads=("q2cols",), writes=("t_q2m",))
    K.op("dve", (lambda: NCV.reduce_max(out=tiny[:, 1:2], in_=q2cols[:, 64:72], axis=AX.X)),
         reads=("q2cols",), writes=("t_k2m",))
    K.op("dve", (lambda: NCV.tensor_mul(out=tiny[:, 2:3], in0=tiny[:, 0:1], in1=tiny[:, 1:2])),
         reads=("t_q2m", "t_k2m"), writes=("t_prod",))
    K.op("act", (lambda: NCS.activation(out=tiny[:, 3:4], in_=tiny[:, 2:3], func=AF.Sqrt,
                                        bias=0.0, scale=1.0)),
         reads=("t_prod",), writes=("t_sq",))
    K.op("dve", (lambda: NCV.tensor_scalar(out=tiny[:, 2:3], in0=tiny[:, 3:4], scalar1=SCALE,
                                           scalar2=3.0, op0=ALU.mult, op1=ALU.add)),
         reads=("t_sq",), writes=("t_C",))
    K.op("pe", (lambda: NCT.matmul(out=psC1[:], lhsT=ones1f[:], rhs=tiny[:, 2:3],
                                   start=True, stop=True)),
         reads=("t_C", "ones1f_r"), writes=("psC1",))
    K.op("act", (lambda: NCS.activation(out=negc[:], in_=psC1[:], func=AF.Copy, bias=0.0, scale=-1.0)),
         reads=("psC1",), writes=("negc",))
    K.dma("pool", (lambda: NCP.dma_start(out=b1tmp[:], in_=d["brow0"])),
          reads=(), writes=("b0",))
    K.dma("pool", (lambda: NCP.dma_start(out=rhs2[0:1, :], in_=b1tmp[:])),
          reads=("b0",), writes=("rhs2",))
    K.dma("pool", (lambda: NCP.dma_start(out=b1tmp[:], in_=d["brow1"])),
          reads=("rhs2",), writes=("b1",))
    K.op("dve", (lambda: NCV.tensor_single_scalar(out=b1tmp[:], in_=b1tmp[:],
                                                  scalar=negc[0:1, 0:1], op=ALU.add)),
         reads=("b1", "negc"), writes=("b1s",))
    K.dma("pool", (lambda: NCP.dma_start(out=nscr, in_=b1tmp[:])),
          reads=("b1s",), writes=("nscr_w",))
    K.dma("pool", (lambda: NCP.dma_start(out=rhs2[1:2, :], in_=nscr)),
          reads=("nscr_w",), writes=("rhs2b",))
    K.op("pe", (lambda: NCT.transpose(out=ps12[0:1, 0:128], in_=qcol[:], identity=i128f[:])),
         reads=("qcol_r", "i128f_r", "ps12e"), writes=("ps12",))
    K.op("act", (lambda: NCS.copy(out=lhsT2[0:1, :], in_=ps12[0:1, 0:128])),
         reads=("ps12",), writes=("lhsT2", "ps12e"))
    K.dma("pool", (lambda: NCP.dma_start(out=lhsT2[1:2, :], in_=d["ones1f"])),
          reads=("lhsT2",), writes=("lhsT2b",))
    for bi2 in range(2):
        wseg = 512 if bi2 == 0 else 128
        K.op("pe", (lambda b3=bi2, w2=wseg: NCT.matmul(
            out=psQ[0][:, 0:w2], lhsT=lhsT2[:], rhs=rhs2[:, 512 * b3:512 * b3 + w2],
            start=True, stop=True)),
            reads=("rhs2", "rhs2b", "lhsT2", "lhsT2b", "psQe0"), writes=("psQ0",))
        K.op("act", (lambda b3=bi2, w2=wseg: NCS.copy(
            out=bias_all[:, 512 * b3:512 * b3 + w2], in_=psQ[0][:, 0:w2])),
            reads=("psQ0",), writes=("bias_all", "psQe0"))

    if upto == "B":
        return _dump([("d_qt", qt), ("d_negc", negc), ("d_lgraw", lgraw)])
    K.barrier()
    psB.close()
    esB.close()
    esBR.close()

    esC = ExitStack()
    sbC = lambda name, shape, dt: esC.enter_context(nc.sbuf_tensor(name, shape, dt))
    DEPTH = 3
    tsc = [sbC(f"tsc{p}", [128, 1024], F32) for p in range(2)]
    pstore = [sbC(f"pstore{p}", [128, 4096], BF16) for p in range(DEPTH)]
    pts = [sbC(f"pts{p}", [128, 1024], BF16) for p in range(2)]
    rows = [sbC(f"rows{p}", [128, 4], F32) for p in range(DEPTH)]
    rs_ = [sbC(f"rs{p}", [128, 1], F32) for p in range(DEPTH)]
    rrec = [sbC(f"rrec{p}", [128, 1], F32) for p in range(DEPTH)]
    psC = ExitStack()
    psS2 = [psC.enter_context(nc.psum_tensor(f"psS2{p}", [128, 1024], F32)) for p in range(2)]
    psPT = [psC.enter_context(nc.psum_tensor(f"psPT{p}", [128, 1024], BF16)) for p in range(2)]
    psO = [psC.enter_context(nc.psum_tensor(f"psO{p}", [128, 128], F32)) for p in range(2)]

    idxs = [(h, e) for h in range(NH) for e in range(NT_Q)]
    ctr = {"p1": 0, "p2": 0, "pj": 0}

    def pair_width(nk, p):
        return 512 if 2 * p + 1 >= nk else 1024

    def c_phase1a(i):
        h, e = idxs[i]
        ip = i % DEPTH
        sprime = float(SLOPES[h]) / SCALE
        qlo = 1024 * h + 128 * e
        nk = e + 1
        npairs = (nk + 1) // 2
        for p in range(npairs):
            kb0 = 2 * p
            w = pair_width(nk, p)
            pp = ctr["p1"] % 2
            ctr["p1"] += 1
            pj = ctr["pj"]
            ctr["pj"] += 1
            for s_ in range(w // 512):
                kb = kb0 + s_
                K.op("pe", (lambda q2=qlo, k2=kb, p2=pp, s2=s_: NCT.matmul(
                    out=psS2[p2][:, 512 * s2:512 * (s2 + 1)], lhsT=qt[:, q2:q2 + 128],
                    rhs=kt[:, 512 * k2:512 * (k2 + 1)], start=True, stop=True)),
                    reads=("qt", "kt", f"psS2e{pp}"), writes=(f"psS2{pp}",))
            if w == 512:
                dsl, dr = dmask2[:, 512:1024], "dmask2_r"
            elif kb0 + 2 == nk:
                dsl, dr = dmask2[:, 0:1024], "dmask2_r"
            else:
                dsl, dr = dplain2[:, 0:1024], "dplain2_r"
            K.op("dve", (lambda d2=dsl, p2=pp, sp=sprime, w2=w: NCV.scalar_tensor_tensor(
                out=tsc[p2][:, 0:w2], in0=d2, scalar=sp, in1=psS2[p2][:, 0:w2],
                op0=ALU.mult, op1=ALU.add)),
                reads=(dr, f"psS2{pp}", f"tsce{pp}"),
                writes=(f"tsc{pp}", f"psS2e{pp}"))
            K.op("act", (lambda p2=pp, p3=p, i2=ip, w2=w, j2=pj: NCS.activation(
                out=pstore[i2][:, 1024 * p3:1024 * p3 + w2], in_=tsc[p2][:, 0:w2], func=AF.Exp,
                bias=bias_all[:, j2:j2 + 1], scale=SCALE,
                accum_out=rows[i2][:, p3:p3 + 1])),
                reads=(f"tsc{pp}", "bias_all", f"rows{ip}_c"),
                writes=(f"pstore{ip}", f"rows{ip}", f"tsce{pp}"))

    def c_phase1b(i):
        h, e = idxs[i]
        ip = i % DEPTH
        npairs = (e + 2) // 2
        K.op("dve", (lambda i2=ip, n2=npairs: NCV.reduce_sum(
            out=rs_[i2][:], in_=rows[i2][:, 0:n2], axis=AX.X)),
            reads=(f"rows{ip}",), writes=(f"rs{ip}", f"rows{ip}_c"))
        K.op("dve", (lambda i2=ip: NCV.reciprocal(out=rrec[i2][:], in_=rs_[i2][:])),
             reads=(f"rs{ip}",), writes=(f"rrec{ip}",))

    def c_phase2(i):
        h, e = idxs[i]
        ip = i % DEPTH
        oe = i % 2
        qlo = 1024 * h + 128 * e
        nk = e + 1
        npairs = (nk + 1) // 2
        nchunks = 4 * nk
        for p in range(npairs):
            w = pair_width(nk, p)
            K.op("dve", (lambda i2=ip, p3=p, w2=w: NCV.tensor_scalar_mul(
                out=pstore[i2][:, 1024 * p3:1024 * p3 + w2],
                in0=pstore[i2][:, 1024 * p3:1024 * p3 + w2], scalar1=rrec[i2][:, 0:1])),
                reads=(f"pstore{ip}", f"rrec{ip}"), writes=(f"pstore{ip}",))
            pt = ctr["p2"] % 2
            ctr["p2"] += 1
            for u in range(w // 128):
                K.op("pe", (lambda i2=ip, p3=p, u2=u, t2=pt: NCT.transpose(
                    out=psPT[t2][:, 128 * u2:128 * (u2 + 1)],
                    in_=pstore[i2][:, 1024 * p3 + 128 * u2:1024 * p3 + 128 * (u2 + 1)],
                    identity=i128h[:])),
                    reads=(f"pstore{ip}", "i128h_r", f"psPTe{pt}"), writes=(f"psPT{pt}",))
            K.op("act", (lambda t2=pt, w2=w: NCS.copy(out=pts[t2][:, 0:w2], in_=psPT[t2][:, 0:w2])),
                 reads=(f"psPT{pt}",), writes=(f"pts{pt}", f"psPTe{pt}"))
            for u in range(w // 128):
                kc = 8 * p + u
                st = (p == 0 and u == 0)
                sp2 = (kc == nchunks - 1)
                K.op("pe", (lambda t2=pt, u2=u, k3=kc, o2=oe, s3=st, s4=sp2: NCT.matmul(
                    out=psO[o2][:], lhsT=vt[:, 128 * k3:128 * (k3 + 1)],
                    rhs=pts[t2][:, 128 * u2:128 * (u2 + 1)], start=s3, stop=s4)),
                    reads=(f"pts{pt}", "vt", f"psOe{oe}"), writes=(f"psO{oe}",))
        K.op("act", (lambda q2=qlo, o2=oe: NCS.copy(out=qt[:, q2:q2 + 128], in_=psO[o2][:])),
             reads=(f"psO{oe}",), writes=("qt", f"psOe{oe}"))

    LAG = DEPTH - 1
    for i in range(len(idxs)):
        c_phase1a(i)
        if i >= LAG:
            c_phase2(i - LAG)
        c_phase1b(i)
    for i in range(len(idxs) - LAG, len(idxs)):
        c_phase2(i)

    if upto == "C":
        return _dump([("d_qt", qt)])
    K.barrier()
    psC.close()
    esC.close()
    esKV.close()

    esS = ExitStack()
    sbS = lambda name, shape, dt: esS.enter_context(nc.sbuf_tensor(name, shape, dt))
    lg = sbS("lg", [128, 192], F32)
    bb24 = sbS("bb24", [128, 24], F32)
    zres = sbS("zres", [128, 128], F32)
    asb = sbS("asb", [128, 128], F32)
    bsb = sbS("bsb", [128, 128], F32)
    esb = sbS("esb", [128, 128], F32)
    hsb = sbS("hsb", [128, 128], F32)
    msb = sbS("msb", [128, 32], F32)
    lsb = sbS("lsb", [128, 32], F32)
    ssb = sbS("ssb", [128, 32], F32)
    ubuf = sbS("ubuf", [128, 32], F32)
    vbuf = sbS("vbuf", [128, 32], F32)
    mp = sbS("mp", [128, 8], F32)
    ep = sbS("ep", [128, 32], F32)
    sp_ = sbS("sp", [128, 8], F32)
    hpre = sbS("hpre", [128, 32], F32)
    hpost = sbS("hpost", [128, 32], F32)
    gcol = sbS("gcol", [128, 1], F32)
    omg = sbS("omg", [128, 1], F32)
    hresT = sbS("hresT", [16, 1024], BF16)
    hpreT = sbS("hpreT", [4, 1024], BF16)
    hpostT = sbS("hpostT", [4, 1024], BF16)
    psSt = ExitStack()
    psLT = psSt.enter_context(nc.psum_tensor("psLT", [128, 24], F32))
    psHT = psSt.enter_context(nc.psum_tensor("psHT", [16, 128], F32))
    psG = psSt.enter_context(nc.psum_tensor("psG", [128, 1], F32))

    K.op("pe", (lambda: NCT.matmul(out=psLT[:], lhsT=ones1f[:], rhs=bases_row[:],
                                   start=True, stop=True)),
         reads=("bases24_r", "ones1f_r", "psLTe"), writes=("psLT",))
    K.op("act", (lambda: NCS.copy(out=bb24[:], in_=psLT[:])),
         reads=("psLT",), writes=("bb24", "psLTe"))
    for e in range(8):
        K.op("pe", (lambda e2=e: NCT.transpose(out=psLT[:], in_=lgraw[:, 128 * e2:128 * (e2 + 1)],
                                               identity=i128f[0:24, 0:24])),
             reads=("lgraw", "i128f_r", "psLTe"), writes=("psLT",))
        K.op("dve", (lambda e2=e: NCV.scalar_tensor_tensor(
            out=lg[:, 24 * e2:24 * (e2 + 1)], in0=psLT[:], scalar=rstdF[:, e2:e2 + 1],
            in1=bb24[:], op0=ALU.mult, op1=ALU.add)),
            reads=("psLT", "rstdF", "bb24"), writes=("lg", "psLTe"))
    lgv = lg.rearrange("p (a x) -> p a x", a=8, x=24)
    K.op("dve", (lambda: NCV.tensor_scalar_mul(
        out=zres.rearrange("p (a x) -> p a x", a=8, x=16),
        in0=lgv[:, :, 0:16], scalar1=TAU_INV)),
        reads=("lg",), writes=("zres",))
    K.op("dve", (lambda: NCV.memset(ubuf[:], 0.0)), reads=(), writes=("ubuf",))
    K.op("dve", (lambda: NCV.memset(vbuf[:], 0.0)), reads=(), writes=("vbuf",))

    z4 = zres.rearrange("p (a i j) -> p a i j", a=8, i=4, j=4)
    a4 = asb.rearrange("p (a i j) -> p a i j", a=8, i=4, j=4)
    b4 = bsb.rearrange("p (a i j) -> p a i j", a=8, i=4, j=4)
    e4 = esb.rearrange("p (a i j) -> p a i j", a=8, i=4, j=4)
    uv = ubuf.rearrange("p (a i) -> p a i", a=8, i=4)
    vv = vbuf.rearrange("p (a j) -> p a j", a=8, j=4)
    mv = msb.rearrange("p (a i) -> p a i", a=8, i=4)
    sv = ssb.rearrange("p (a i) -> p a i", a=8, i=4)
    vb_bc = vv.unsqueeze(2).broadcast_to([128, 8, 4, 4])
    ub_bc = uv.unsqueeze(3).broadcast_to([128, 8, 4, 4])
    mb_bc = mv.unsqueeze(3).broadcast_to([128, 8, 4, 4])
    m2_bc = mv.unsqueeze(2).broadcast_to([128, 8, 4, 4])
    a4_ji = asb.rearrange("p (a i j) -> p a j i", a=8, i=4, j=4)
    e4_ji = esb.rearrange("p (a i j) -> p a j i", a=8, i=4, j=4)

    for it in range(SINK_ITERS):
        K.op("dve", (lambda: NCV.tensor_tensor(out=a4, in0=z4, in1=vb_bc, op=ALU.subtract)),
             reads=("zres", "vbuf"), writes=("asb",))
        K.op("dve", (lambda: NCV.tensor_reduce(out=mv, in_=a4, axis=AX.X, op=ALU.max)),
             reads=("asb",), writes=("msb",))
        K.op("dve", (lambda: NCV.tensor_tensor(out=b4, in0=a4, in1=mb_bc, op=ALU.subtract)),
             reads=("asb", "msb"), writes=("bsb",))
        K.op("act", (lambda: NCS.activation(out=esb[:], in_=bsb[:], func=AF.Exp, bias=0.0, scale=1.0)),
             reads=("bsb",), writes=("esb",))
        K.op("dve", (lambda: NCV.tensor_reduce(out=sv, in_=e4, axis=AX.X, op=ALU.add)),
             reads=("esb",), writes=("ssb",))
        K.op("act", (lambda: NCS.activation(out=lsb[:], in_=ssb[:], func=AF.Ln, bias=0.0, scale=1.0)),
             reads=("ssb",), writes=("lsb",))
        K.op("dve", (lambda: NCV.tensor_add(out=ubuf[:], in0=msb[:], in1=lsb[:])),
             reads=("msb", "lsb"), writes=("ubuf",))
        K.op("dve", (lambda: NCV.tensor_tensor(out=a4, in0=z4, in1=ub_bc, op=ALU.subtract)),
             reads=("zres", "ubuf"), writes=("asb",))
        K.op("dve", (lambda: NCV.tensor_reduce(out=mv, in_=a4_ji, axis=AX.X, op=ALU.max)),
             reads=("asb",), writes=("msb",))
        K.op("dve", (lambda: NCV.tensor_tensor(out=b4, in0=a4, in1=m2_bc, op=ALU.subtract)),
             reads=("asb", "msb"), writes=("bsb",))
        K.op("act", (lambda: NCS.activation(out=esb[:], in_=bsb[:], func=AF.Exp, bias=0.0, scale=1.0)),
             reads=("bsb",), writes=("esb",))
        K.op("dve", (lambda: NCV.tensor_reduce(out=sv, in_=e4_ji, axis=AX.X, op=ALU.add)),
             reads=("esb",), writes=("ssb",))
        K.op("act", (lambda: NCS.activation(out=lsb[:], in_=ssb[:], func=AF.Ln, bias=0.0, scale=1.0)),
             reads=("ssb",), writes=("lsb",))
        K.op("dve", (lambda: NCV.tensor_add(out=vbuf[:], in0=msb[:], in1=lsb[:])),
             reads=("msb", "lsb"), writes=("vbuf",))
    K.op("dve", (lambda: NCV.tensor_tensor(out=a4, in0=z4, in1=ub_bc, op=ALU.subtract)),
         reads=("zres", "ubuf"), writes=("asb",))
    K.op("dve", (lambda: NCV.tensor_tensor(out=b4, in0=a4, in1=vb_bc, op=ALU.subtract)),
         reads=("asb", "vbuf"), writes=("bsb",))
    K.op("act", (lambda: NCS.activation(out=hsb[:], in_=bsb[:], func=AF.Exp, bias=0.0, scale=1.0)),
         reads=("bsb",), writes=("hsb",))
    K.op("act", (lambda: NCS.activation(out=gate_s[:], in_=gate_s[:], func=AF.Sigmoid, bias=0.0, scale=1.0)),
         reads=("gate_r",), writes=("gate_v",))
    K.op("pe", (lambda: NCT.matmul(out=psG[:], lhsT=ones1f[:], rhs=gate_s[:], start=True, stop=True)),
         reads=("gate_v", "ones1f_r"), writes=("psG",))
    K.op("act", (lambda: NCS.copy(out=gcol[:], in_=psG[:])),
         reads=("psG",), writes=("gcol",))
    K.op("dve", (lambda: NCV.tensor_scalar(out=omg[:], in0=gcol[:], scalar1=-1.0, scalar2=1.0,
                                           op0=ALU.mult, op1=ALU.add)),
         reads=("gcol",), writes=("omg",))
    K.op("dve", (lambda: NCV.tensor_scalar_mul(out=hsb[:], in0=hsb[:], scalar1=gcol[:, 0:1])),
         reads=("hsb", "gcol"), writes=("hsb",))
    hdg = hsb.rearrange("p (a x) -> p a x", a=8, x=16)
    for dgi in range(4):
        K.op("dve", (lambda d2=dgi: NCV.tensor_scalar_add(
            out=hdg[:, :, 5 * d2:5 * d2 + 1], in0=hdg[:, :, 5 * d2:5 * d2 + 1],
            scalar1=omg[:, 0:1])),
            reads=("hsb", "omg"), writes=("hsb",))
    for (name, off, out_sb) in (("hpre", 16, hpre), ("hpost", 20, hpost)):
        lgs = lgv[:, :, off:off + 4]
        mbc = mp[:].unsqueeze(2).broadcast_to([128, 8, 4])
        spc = sp_[:].unsqueeze(2).broadcast_to([128, 8, 4])
        epv = ep.rearrange("p (a n) -> p a n", a=8, n=4)
        K.op("dve", (lambda l2=lgs: NCV.tensor_reduce(out=mp[:], in_=l2, axis=AX.X, op=ALU.max)),
             reads=("lg",), writes=("mp",))
        K.op("dve", (lambda l2=lgs, m2=mbc: NCV.tensor_tensor(
            out=epv, in0=l2, in1=m2, op=ALU.subtract)),
            reads=("lg", "mp"), writes=("ep",))
        K.op("act", (lambda: NCS.activation(out=ep[:], in_=ep[:], func=AF.Exp, bias=0.0, scale=1.0)),
             reads=("ep",), writes=("ep",))
        K.op("dve", (lambda: NCV.tensor_reduce(out=sp_[:], in_=epv, axis=AX.X, op=ALU.add)),
             reads=("ep",), writes=("sp",))
        K.op("dve", (lambda: NCV.reciprocal(out=sp_[:], in_=sp_[:])),
             reads=("sp",), writes=("sp",))
        K.op("dve", (lambda o2=out_sb, s3=spc: NCV.tensor_tensor(
            out=o2.rearrange("p (a n) -> p a n", a=8, n=4),
            in0=epv, in1=s3, op=ALU.mult)),
            reads=("ep", "sp"), writes=(name,))
    for e in range(8):
        K.op("pe", (lambda e2=e: NCT.transpose(out=psHT[:], in_=hsb[:, 16 * e2:16 * (e2 + 1)],
                                               identity=i128f[:])),
             reads=("hsb", "i128f_r", "psHTe"), writes=("psHT",))
        K.op("dve", (lambda e2=e: NCV.tensor_copy(out=hresT[:, 128 * e2:128 * (e2 + 1)], in_=psHT[:])),
             reads=("psHT",), writes=("hresT", "psHTe"))
        for (src, dst, rn) in ((hpre, hpreT, "hpreT"), (hpost, hpostT, "hpostT")):
            K.op("pe", (lambda e2=e, s3=src: NCT.transpose(
                out=psHT[0:4, :], in_=s3[:, 4 * e2:4 * (e2 + 1)], identity=i128f[:])),
                reads=("hpre", "hpost", "i128f_r", "psHTe"), writes=("psHT",))
            K.op("dve", (lambda e2=e, d2=dst: NCV.tensor_copy(
                out=d2[:, 128 * e2:128 * (e2 + 1)], in_=psHT[0:4, :])),
                reads=("psHT",), writes=(rn, "psHTe"))
    K.dma("pool", (lambda: NCP.dma_start(out=hscr[0:16, :], in_=hresT[:])),
          reads=("hresT",), writes=("hscr_res",))
    K.dma("pool", (lambda: NCP.dma_start(out=hscr[16:20, :], in_=hpreT[:])),
          reads=("hpreT",), writes=("hscr_pre",))
    K.dma("pool", (lambda: NCP.dma_start(out=hscr[20:24, :], in_=hpostT[:])),
          reads=("hpostT",), writes=("hscr_post",))
    if upto == "S":
        return _dump([("d_hresT", hresT), ("d_hpreT", hpreT), ("d_hpostT", hpostT),
                      ("d_lg", lg)])
    K.barrier()
    psSt.close()
    esS.close()

    yt = sb("yt", [128, 8 * 1024], BF16)
    esD1 = ExitStack()
    sb1 = lambda name, shape, dt: esD1.enter_context(nc.sbuf_tensor(name, shape, dt))
    wbraw = sb1("wbraw", [128, 1024], F32)
    wbrb = sb1("wbrb", [128, 8192], BF16)
    rowsl = sb1("rowsl", [1, 1024], BF16)
    hpre_bc = sb1("hpre_bc", [128, 4096], BF16)
    fftmp = sb1("fftmp", [128, 512], F32)
    xpre = sb1("xpre", [128, 8192], BF16)
    psD1 = ExitStack()
    psY = [psD1.enter_context(nc.psum_tensor(f"psY{p}", [128, 512], F32)) for p in range(2)]
    psBC = [psD1.enter_context(nc.psum_tensor(f"psBC{p}", [128, 512], F32)) for p in range(2)]
    for hf in range(8):
        K.dma("sync", (lambda h2=hf: NCY.dma_start(out=wbraw[:], in_=d["wbr"][:, 1024 * h2:1024 * (h2 + 1)])),
              reads=("wbraw_c",), writes=("wbraw",))
        K.op("act", (lambda h2=hf: NCS.copy(out=wbrb[:, 1024 * h2:1024 * (h2 + 1)], in_=wbraw[:])),
             reads=("wbraw",), writes=("wbrb", "wbraw_c"))
    for n in range(4):
        K.dma("pool", (lambda n2=n: NCP.dma_start(out=rowsl[:], in_=hscr[16 + n2:17 + n2, :])),
              reads=("hscr_pre", "rowsl"), writes=("rowsl",))
        for th in range(2):
            bp = th % 2
            K.op("pe", (lambda t2=th, b2=bp: NCT.matmul(
                out=psBC[b2][:], lhsT=ones1h[:], rhs=rowsl[:, 512 * t2:512 * (t2 + 1)],
                start=True, stop=True)),
                reads=("rowsl", "ones1h_r", f"psBCe{bp}"), writes=(f"psBC{bp}",))
            K.op("act", (lambda n2=n, t2=th, b2=bp: NCS.copy(
                out=hpre_bc[:, 1024 * n2 + 512 * t2:1024 * n2 + 512 * (t2 + 1)], in_=psBC[b2][:])),
                reads=(f"psBC{bp}",), writes=("hpre_bc", f"psBCe{bp}"))
    for cc in range(8):
        for th in range(2):
            for n in range(4):
                jg = n * 8 + cc
                xsl = (1024 * jg + 512 * th, 1024 * jg + 512 * (th + 1))
                hsl = (1024 * n + 512 * th, 1024 * n + 512 * (th + 1))
                osl = (1024 * cc + 512 * th, 1024 * cc + 512 * (th + 1))
                if n == 0:
                    K.op("pool", (lambda x2=xsl, h2=hsl, o2=osl: NCP.tensor_tensor(
                        out=xpre[:, o2[0]:o2[1]], in0=xnt[:, x2[0]:x2[1]],
                        in1=hpre_bc[:, h2[0]:h2[1]], op=ALU.mult)),
                        reads=("xnt", "hpre_bc"), writes=("xpre",))
                else:
                    K.op("pool", (lambda x2=xsl, h2=hsl: NCP.tensor_tensor(
                        out=fftmp[:], in0=xnt[:, x2[0]:x2[1]],
                        in1=hpre_bc[:, h2[0]:h2[1]], op=ALU.mult)),
                        reads=("xnt", "hpre_bc"), writes=("fftmp",))
                    K.op("pool", (lambda o2=osl: NCP.tensor_tensor(
                        out=xpre[:, o2[0]:o2[1]], in0=xpre[:, o2[0]:o2[1]],
                        in1=fftmp[:], op=ALU.add)),
                        reads=("fftmp", "xpre"), writes=("xpre",))
    for co in range(8):
        for th in range(2):
            yp = (2 * co + th) % 2
            for cc in range(8):
                K.op("pe", (lambda c2=cc, o2=co, t2=th, y2=yp: NCT.matmul(
                    out=psY[y2][:], lhsT=wbrb[:, 1024 * o2 + 128 * c2:1024 * o2 + 128 * (c2 + 1)],
                    rhs=xpre[:, 1024 * c2 + 512 * t2:1024 * c2 + 512 * (t2 + 1)],
                    start=(c2 == 0), stop=(c2 == 7))),
                    reads=("wbrb", "xpre", f"psYe{yp}"), writes=(f"psY{yp}",))
            K.op("act", (lambda o2=co, t2=th, y2=yp: NCS.copy(
                out=yt[:, 1024 * o2 + 512 * t2:1024 * o2 + 512 * (t2 + 1)], in_=psY[y2][:])),
                reads=(f"psY{yp}",), writes=("yt", f"psYe{yp}"))
    if upto == "D1":
        return _dump([("d_yt", yt), ("d_xpre", xpre), ("d_hpre_bc", hpre_bc)])
    K.barrier()
    psD1.close()
    esD1.close()

    esD = ExitStack()
    sbD = lambda name, shape, dt: esD.enter_context(nc.sbuf_tensor(name, shape, dt))
    woraw = [sbD(f"woraw{p}", [128, 1024], F32) for p in range(2)]
    wob = [sbD(f"wob{p}", [128, 4096], BF16) for p in range(2)]
    bc5 = sbD("bc5", [128, 5 * 1024], BF16)
    rslot = [sbD(f"rslot{j}", [1, 1024], BF16) for j in range(1)]
    xtrow = [sbD("xtrow0", [128, 1024], F32)] * 2
    osb = [sbD(f"osb{p}", [128, 1024], F32) for p in range(2)]
    fft2 = [sbD(f"fft2{p}", [128, 1024], F32) for p in range(2)]
    psD = ExitStack()
    psW = [psD.enter_context(nc.psum_tensor(f"psW{p}", [128, 1024], F32)) for p in range(2)]
    psB2 = [psD.enter_context(nc.psum_tensor(f"psB2{p}", [128, 512], F32)) for p in range(2)]

    for n in range(4):
        for j in range(5):
            src_row = (n * 4 + j) if j < 4 else (20 + n)
            rdep = "hscr_res" if j < 4 else "hscr_post"
            K.dma("pool", (lambda s3=src_row: NCP.dma_start(
                out=rslot[0][:], in_=hscr[s3:s3 + 1, :])),
                reads=(rdep, "rslot0"), writes=("rslot0",))
            for th in range(2):
                bp = th % 2
                K.op("pe", (lambda t2=th, b2=bp: NCT.matmul(
                    out=psB2[b2][:], lhsT=ones1h[:], rhs=rslot[0][:, 512 * t2:512 * (t2 + 1)],
                    start=True, stop=True)),
                    reads=("rslot0", "ones1h_r", f"psB2e{bp}"), writes=(f"psB2{bp}",))
                K.op("act", (lambda j2=j, t2=th, b2=bp: NCS.copy(
                    out=bc5[:, 1024 * j2 + 512 * t2:1024 * j2 + 512 * (t2 + 1)], in_=psB2[b2][:])),
                    reads=(f"psB2{bp}",), writes=("bc5", f"psB2e{bp}"))
        for cc in range(8):
            of = n * 8 + cc
            wp = of % 2
            for hf in range(4):
                qp_ = hf % 2
                eng_ = ("sync", "act", "sync", "pool")[hf]
                nce_ = {"sync": NCY, "act": NCS, "pool": NCP}[eng_]
                K.dma(eng_, (lambda o2=of, h2=hf, q2=qp_, n3=nce_: n3.dma_start(
                    out=woraw[q2][:], in_=d["wo"][o2][:, 1024 * h2:1024 * (h2 + 1)])),
                    reads=(f"worawc{qp_}",), writes=(f"woraw{qp_}",))
                K.op("act", (lambda w2=wp, h2=hf, q2=qp_: NCS.copy(
                    out=wob[w2][:, 1024 * h2:1024 * (h2 + 1)], in_=woraw[q2][:])),
                    reads=(f"woraw{qp_}",), writes=(f"wob{wp}", f"worawc{qp_}"))
            K.dma("pool", (lambda o2=of: NCP.dma_start(
                out=xtrow[0][:], in_=d["xqt"][128 * o2:128 * (o2 + 1), :])),
                reads=(), writes=("xtrow0",))
            op_ = of % 2
            for th in range(2):
                for c in range(32):
                    K.op("pe", (lambda c2=c, t2=th, p2=op_, w2=wp: NCT.matmul(
                        out=psW[p2][:, 512 * t2:512 * (t2 + 1)],
                        lhsT=wob[w2][:, 128 * c2:128 * (c2 + 1)],
                        rhs=qt[:, 1024 * c2 + 512 * t2:1024 * c2 + 512 * (t2 + 1)],
                        start=(c2 == 0), stop=(c2 == 31))),
                        reads=(f"wob{wp}", "qt", f"psWe{op_}"), writes=(f"psW{op_}",))
            K.op("dve", (lambda p2=op_, w2=wp: NCV.tensor_tensor(
                out=osb[p2][:], in0=psW[p2][:], in1=xtrow[w2][:], op=ALU.add)),
                reads=(f"psW{op_}", "xtrow0", f"osbc{op_}"), writes=(f"osb{op_}", f"psWe{op_}"))
            for j in range(4):
                jg = j * 8 + cc
                K.op("dve", (lambda j2=jg, j3=j, p2=op_: NCV.tensor_tensor(
                    out=fft2[p2][:], in0=xnt[:, 1024 * j2:1024 * (j2 + 1)],
                    in1=bc5[:, 1024 * j3:1024 * (j3 + 1)], op=ALU.mult)),
                    reads=("xnt", "bc5"), writes=(f"fft2{op_}",))
                K.op("dve", (lambda p2=op_: NCV.tensor_tensor(
                    out=osb[p2][:], in0=osb[p2][:], in1=fft2[p2][:], op=ALU.add)),
                    reads=(f"fft2{op_}",), writes=(f"osb{op_}",))
            K.op("dve", (lambda c2=cc, p2=op_: NCV.tensor_tensor(
                out=fft2[p2][:], in0=yt[:, 1024 * c2:1024 * (c2 + 1)],
                in1=bc5[:, 4096:5120], op=ALU.mult)),
                reads=("yt", "bc5"), writes=(f"fft2{op_}",))
            K.op("dve", (lambda p2=op_: NCV.tensor_tensor(
                out=osb[p2][:], in0=osb[p2][:], in1=fft2[p2][:], op=ALU.add)),
                reads=(f"fft2{op_}",), writes=(f"osb{op_}",))
            K.dma("act", (lambda o3=of, p2=op_: NCS.dma_start(
                out=d["yout"][o3], in_=osb[p2][:])),
                reads=(f"osb{op_}",), writes=(f"osbc{op_}",))

    K.barrier()
    final_qact = K.cnt["q_act"]
    K.ops["pool"].append(([("q_act", final_qact)], None, None))
    psD.close()
    esD.close()

    return _emit()


def _host_inputs(L0_attn_hidden_states, L0_attn_Wqkv, L0_attn_Wo, L0_ffn_W_res,
                 L0_ffn_W_pre, L0_ffn_W_post, L0_ffn_H_res_base, L0_ffn_H_pre_base,
                 L0_ffn_H_post_base, L0_ffn_gate, L0_ffn_W_branch):
    x = np.asarray(L0_attn_hidden_states, dtype=np.float32)
    wqkv = np.asarray(L0_attn_Wqkv, dtype=np.float32)
    wo = np.asarray(L0_attn_Wo, dtype=np.float32)

    def lay_blocks(w, nblocks):
        return np.ascontiguousarray(
            w.reshape(32, 128, nblocks, 128).transpose(2, 1, 0, 3).reshape(nblocks, 128, 32 * 128))

    wqkv_l = lay_blocks(wqkv, 34)
    wo_l = lay_blocks(wo, 32)
    wbr = np.asarray(L0_ffn_W_branch, dtype=np.float32)
    wbr_l = np.ascontiguousarray(
        wbr.reshape(8, 128, 8, 128).transpose(1, 2, 0, 3).reshape(128, 8192))
    w24 = np.concatenate([np.asarray(L0_ffn_W_res, dtype=np.float32),
                          np.asarray(L0_ffn_W_pre, dtype=np.float32),
                          np.asarray(L0_ffn_W_post, dtype=np.float32)], axis=0)
    w24_l = np.ascontiguousarray(w24.T.reshape(32, 128, 24).transpose(1, 0, 2).reshape(128, 32 * 24))
    bases = np.concatenate([np.asarray(L0_ffn_H_res_base, dtype=np.float32).reshape(-1),
                            np.asarray(L0_ffn_H_pre_base, dtype=np.float32),
                            np.asarray(L0_ffn_H_post_base, dtype=np.float32)]).reshape(1, 24)
    gate = np.asarray(L0_ffn_gate, dtype=np.float32).reshape(1, 1)
    r0, r1 = [], []
    for h in range(NH):
        s = float(SLOPES[h])
        for e in range(8):
            nk = e + 1
            for p in range((nk + 1) // 2):
                kb0 = 2 * p
                kbias = kb0 - 1 if (kb0 + 1 >= nk and nk % 2 == 1) else kb0
                r0.append(-s)
                r1.append(512.0 * s * (kbias - e))
    brow0 = np.asarray(r0, np.float32).reshape(1, 640)
    brow1 = np.asarray(r1, np.float32).reshape(1, 640)

    kf = np.arange(512, dtype=np.float32)[None, :]
    kf2 = np.arange(1024, dtype=np.float32)[None, :]
    qp = np.arange(128, dtype=np.float32)[:, None]
    dplain2 = np.ascontiguousarray(np.broadcast_to(kf2, (128, 1024)).astype(np.float32))
    i128h = np.eye(128).astype(BF)
    i128f = np.eye(128).astype(np.float32)
    ones1h = np.ones((1, 128), dtype=BF)
    ones1f = np.ones((1, 128), dtype=np.float32)
    onescol_h = np.ones((128, 1), dtype=BF)

    in_maps = []
    for c in range(8):
        b, r = c // 4, c % 4
        rows = np.concatenate([np.arange(512 * e + 128 * r, 512 * e + 128 * r + 128)
                               for e in range(8)])
        xq = np.ascontiguousarray(x[b][rows])
        xqt = np.ascontiguousarray(xq.T)
        dmask2 = np.where((kf2 < 512) | (kf2 - 512 <= 128 * r + qp), kf2,
                          np.float32(NEG_BIG)).astype(np.float32)
        qcol = (qp + 128 * r).astype(np.float32)
        in_maps.append(dict(
            xkv=np.ascontiguousarray(x[b]), xq=xq, xqt=xqt, wqkv=wqkv_l, wo=wo_l, wbr=wbr_l,
            w24=w24_l, bases24=bases, gate=gate, dplain2=dplain2,
            dmask2=np.ascontiguousarray(dmask2),
            qcol=np.ascontiguousarray(qcol), brow0=brow0, brow1=brow1,
            i128h=i128h, i128f=i128f, ones1h=ones1h, ones1f=ones1f,
            onescol_h=onescol_h))
    return in_maps


def kernel(**inputs):
    in_maps = _host_inputs(**inputs)
    nc = build_program()
    res = run_bass_kernel_spmd(nc, in_maps, core_ids=list(range(8)))
    out = np.empty((B, S, H), dtype=np.float32)
    for c in range(8):
        b, r = c // 4, c % 4
        y = np.asarray(res.results[c]["yout"])
        rows = y.transpose(2, 0, 1).reshape(1024, 4096)
        for e in range(8):
            out[b, 512 * e + 128 * r:512 * e + 128 * r + 128, :] = rows[128 * e:128 * (e + 1)]
    return out
